# version 03 (final)
# baseline (speedup 1.0000x reference)
import numpy as np
import ml_dtypes
import concourse.bass as bass
import concourse.mybir as mybir
from contextlib import ExitStack, nullcontext

F32 = mybir.dt.float32
BF16 = mybir.dt.bfloat16
AF = mybir.ActivationFunctionType
ALU = mybir.AluOpType
AX = mybir.AxisListType

ENGINES = ("sync", "scalar", "vector", "gpsimd", "tensor")


def ceil_div(a, b):
    return (a + b - 1) // b

bfc = lambda x: np.asarray(x, dtype=np.float32).astype(ml_dtypes.bfloat16)


class Op:
    __slots__ = ("eng", "fn", "reads", "writes", "inc", "idx", "dma_sem", "deps",
                 "_dbg_waits", "_dbg_incs", "dsnap")

    def __init__(self, eng, fn, reads, writes, inc, dma_sem=None):
        self.eng, self.fn = eng, fn
        self.reads, self.writes = list(reads), list(writes)
        self.inc, self.dma_sem = inc, dma_sem
        self.idx = None
        self.deps = []


class K:
    def __init__(self, nc, ctx):
        self.nc, self.ctx = nc, ctx
        self.ops = []
        self.last_w = {}
        self.readers = {}
        self.dma_cnt = {}
        self.full_streams = set()
        self.pfx = ""
        self.pool = None
        self.pool_cols = 0
        self.pool_off = 0

    def arena_reset(self, pfx):
        self.pfx = pfx
        self.pool_off = 0

    def barrier(self):
        snap = {}
        for op in reversed(self.ops):
            if op.eng not in snap:
                snap[op.eng] = op
            if len(snap) == len(ENGINES):
                break
        self._fence = {"snap": snap, "todo": set(ENGINES)}

    def emit(self, eng, fn, reads=(), writes=(), inc=True, dma_sem=None):
        op = Op(eng, fn, reads, writes, inc, dma_sem)
        op.dsnap = dict(self.dma_cnt)
        fence = getattr(self, "_fence", None)
        if fence and eng in fence["todo"]:
            fence["todo"].discard(eng)
            for e2, d in fence["snap"].items():
                if e2 != eng:
                    op.deps.append(d)
        for r in op.reads:
            w = self.last_w.get(r)
            if w is not None:
                op.deps.append(w)
            if isinstance(r, tuple) and len(r) == 2 and r[0] == "PS":
                for rd in self.readers.get(r, ()):
                    if rd is not op and rd.eng != eng and rd.dma_sem is None                             and op.dma_sem is None:
                        op.deps.append(rd)
            self.readers.setdefault(r, []).append(op)
        for wr in op.writes:
            w = self.last_w.get(wr)
            if w is not None and w is not op:
                op.deps.append(w)
            for rd in self.readers.get(wr, ()):
                if rd is not op:
                    op.deps.append(rd)
            self.readers[wr] = []
            self.last_w[wr] = op
        self.ops.append(op)
        return op

    def dma(self, queue, out, in_, stream, reads=(), writes=(), full=False):
        stream = self.pfx + stream
        self.dma_cnt[stream] = self.dma_cnt.get(stream, 0) + 1
        if full:
            self.full_streams.add(stream)
        return self.emit(queue, lambda e, o=out, i=in_: e.dma_start(out=o, in_=i),
                         reads=reads, writes=writes, inc=True, dma_sem=stream)

    def sbuf(self, name, shape, dtype):
        if self.pool is None:
            return self.ctx.enter_context(
                self.nc.sbuf_tensor(self.pfx + name, shape, dtype))
        itm = 4 if dtype == F32 else 2
        need = ceil_div(shape[1] * itm, 4)
        off = self.pool_off
        assert off + need <= self.pool_cols, \
            f"SBUF arena overflow at {self.pfx}{name} (+{need} @ {off})"
        self.pool_off = off + need
        v = self.pool[0:shape[0], off:off + need]
        if dtype == F32:
            return v
        return v.bitcast(dtype)[:, 0:shape[1]]

    def psum(self, name, shape, dtype=F32):
        return self.nc.psum_tensor(name, shape, dtype)

    def finalize(self, same_engine_sync=False):
        self.same_engine_sync = same_engine_sync
        return self._finalize()

    def _finalize(self):
        per_eng = {e: [] for e in ENGINES}
        for op in self.ops:
            op.idx = len(per_eng[op.eng])
            per_eng[op.eng].append(op)
        ses = getattr(self, "same_engine_sync", False)
        if ses is True:
            ses = {"sync", "scalar", "vector", "gpsimd"}
        elif ses is False:
            ses = set()
        self._ses_set = ses
        import os as _oss
        vlo = int(_oss.environ.get("VSES_LO", "0"))
        vhi = int(_oss.environ.get("VSES_HI", "1000000000"))
        self._vses = lambda op: (op.eng != "vector" or vlo <= op.idx < vhi)
        needed = set()
        for op in self.ops:
            for d in op.deps:
                if d.eng != op.eng and d.dma_sem is None:
                    needed.add(d)
                elif (d.eng in ses and d.eng == op.eng and d.dma_sem is None
                      and op.dma_sem is None and d.eng != "tensor"
                      and self._vses(op)):
                    needed.add(d)
                elif d.eng == op.eng and d.dma_sem is None and op.dma_sem is not None:
                    needed.add(d)
        for op in self.ops:
            if op.dma_sem is None:
                op.inc = op in needed
        sems = {}
        for e in ENGINES:
            sems[e] = self.ctx.enter_context(self.nc.semaphore(f"ms_{e}"))
        dsems = {s: self.ctx.enter_context(self.nc.semaphore(s))
                 for s in self.dma_cnt}
        inc_no = {}
        cover = {}
        for e in ENGINES:
            n = 0
            lst = per_eng[e]
            for op in lst:
                if op.dma_sem is not None:
                    continue
                if op.inc:
                    n += 1
                    inc_no[op] = n
            nxt = None
            for op in reversed(lst):
                if op.dma_sem is not None:
                    continue
                if op.inc:
                    nxt = inc_no[op]
                cover[op] = nxt
        dma_no = {s: 0 for s in self.dma_cnt}
        for op in self.ops:
            if op.dma_sem is not None:
                dma_no[op.dma_sem] += 16
                cover[op] = dma_no[op.dma_sem]
        hwm = {e: {} for e in ENGINES}
        plans = {e: [] for e in ENGINES}
        for op in self.ops:
            waits = {}
            for d in op.deps:
                if d.eng == op.eng and (d.dma_sem is None) == (op.dma_sem is None):
                    if d.dma_sem is None and not (d.eng in ses
                                                  and d.eng != "tensor"
                                                  and self._vses(op)):
                        continue
                if d.dma_sem is not None:
                    if (d.dma_sem == op.dma_sem
                            and d.dma_sem in self.full_streams):
                        continue
                    key = ("d", d.dma_sem)
                    val = (16 * op.dsnap.get(d.dma_sem, 0)
                           if d.dma_sem in self.full_streams else cover[d])
                    if val == 0:
                        continue
                else:
                    key = ("e", d.eng)
                    val = cover[d]
                    if val is None:
                        raise AssertionError(
                            "dependency on never-inc'd op: op eng=%s idx=%s "
                            "line=%s <- dep eng=%s idx=%s line=%s needed=%s "
                            "inc=%s" % (op.eng, op.idx,
                                        op.fn.__code__.co_firstlineno,
                                        d.eng, d.idx,
                                        d.fn.__code__.co_firstlineno,
                                        d in needed, d.inc))
                if waits.get(key, 0) < val:
                    waits[key] = val
            wl = []
            for key, val in waits.items():
                if hwm[op.eng].get(key, 0) >= val:
                    continue
                hwm[op.eng][key] = val
                sem = dsems[key[1]] if key[0] == "d" else sems[key[1]]
                wl.append((sem, val))
            if op.dma_sem is not None:
                incs = [(dsems[op.dma_sem], 16)]
            elif op.inc:
                incs = [(sems[op.eng], 1)]
            else:
                incs = []
            plans[op.eng].append((wl, op.fn, incs))
            op._dbg_waits = [(s.name, v) for s, v in wl]
            op._dbg_incs = [(s.name, n) for s, n in incs]
        blk = self.ctx.enter_context(self.nc.Block())
        nops = {e: len(plans[e]) for e in ENGINES}

        import os as _osn
        nop_eng = set(_osn.environ.get(
            "NOPW", "vector,scalar,gpsimd,sync").split(",")) - {""}

        wsep = "WSEP" in _osn.environ

        def section(e, plan, pad=False):
            for wl, fn, incs in plan:
                for sem, val in wl:
                    e.wait_ge(sem, val)
                    if pad and wsep:
                        e.nop()
                if pad and wl and not wsep:
                    e.nop()
                ins = fn(e)
                for sem, n in incs:
                    assert ins is not None, "inc on op returning no instruction"
                    ins = ins.then_inc(sem, n)

        for ename in ENGINES:
            getattr(blk, ename)(lambda e, p=plans[ename],
                                pd=(ename in nop_eng): section(e, p, pd))
        global LAST_STATS, LAST_OPS
        LAST_OPS = self.ops
        LAST_STATS = {e: {"ops": len(plans[e]),
                          "waits": sum(len(wl) for wl, _, _ in plans[e]),
                          "incs": sum(len(ic) for _, _, ic in plans[e])}
                      for e in ENGINES}
        return nops


def load_convert_bf16(k, src, nchunks, cols, dst, dst_col0, scr, scr_cols,
                      conv="vector", stream="dw", gamma_col=None, tag="w",
                      scr_key="scrA"):
    for c in range(nchunks):
        par = c % 2
        sc = scr[:, par * scr_cols: par * scr_cols + cols]
        k.dma("sync", sc, src[c * 128:(c + 1) * 128, :], f"{stream}{par}",
              writes=[(scr_key, par)])
        dv = dst[:, dst_col0 + c * cols: dst_col0 + (c + 1) * cols]
        if gamma_col is None:
            k.emit(conv, lambda e, o=dv, i=sc: e.tensor_copy(out=o, in_=i),
                   reads=[(scr_key, par)], writes=[(tag, "dst")])
        else:
            g = gamma_col[:, c:c + 1]
            k.emit(conv, lambda e, o=dv, i=sc, g=g: e.tensor_scalar(
                out=o, in0=i, scalar1=g, scalar2=None, op0=ALU.mult),
                reads=[(scr_key, par), (tag, "gamma")], writes=[(tag, "dst")])


def psb_f32(k, b0, nb, rows=128):
    return k.PSB[0:rows, b0 * 512:(b0 + nb) * 512]


def psb_bf16(k, b0, nb, rows=128):
    return k.PSB[0:rows, b0 * 512:(b0 + nb) * 512].bitcast(BF16)


def psk(b0, nb):
    return [("PS", i) for i in range(b0, b0 + nb)]


def ln_tokenmajor(k, x_src_fn, ntiles, scr, sums, sumsq, negmean, rstd, sqscr,
                  eps_col, tag, dstream="dx", post_sq_hook=None):
    k.emit("vector", lambda v: v.memset(sumsq[:, 0:ntiles], 0.0),
           writes=[(tag, "sumsq")])
    for t in range(ntiles):
        par = t % 2
        xt = scr[:, par * 1024:(par + 1) * 1024]
        k.dma("sync", xt, x_src_fn(t), f"{dstream}{par}",
              writes=[(tag, "scr", par)])
        k.emit("vector", lambda v, o=sums[:, t:t + 1], i=xt: v.tensor_reduce(
            out=o, in_=i, axis=AX.X, op=ALU.add),
            reads=[(tag, "scr", par)], writes=[(tag, "sums")])
        k.emit("scalar", lambda s, o=sqscr[:], i=xt, a=sumsq[:, t:t + 1]:
               s.activation(out=o, in_=i, func=AF.Square, accum_out=a),
               reads=[(tag, "scr", par)], writes=[(tag, "sumsq"), ("hsq",)])
    if post_sq_hook is not None:
        post_sq_hook()
    k.emit("vector", lambda v: v.tensor_scalar(
        out=negmean[:, 0:ntiles], in0=sums[:, 0:ntiles], scalar1=-1.0 / 1024,
        scalar2=None, op0=ALU.mult),
        reads=[(tag, "sums")], writes=[(tag, "negmean")])
    k.emit("vector", lambda v: v.tensor_scalar(
        out=rstd[:, 0:ntiles], in0=sumsq[:, 0:ntiles], scalar1=1.0 / 1024,
        scalar2=None, op0=ALU.mult),
        reads=[(tag, "sumsq")], writes=[(tag, "rstd")])
    k.emit("vector", lambda v: v.scalar_tensor_tensor(
        out=sums[:, 0:ntiles], in0=negmean[:, 0:ntiles], scalar=1.0,
        in1=negmean[:, 0:ntiles], op0=ALU.mult, op1=ALU.mult),
        reads=[(tag, "negmean")], writes=[(tag, "sums")])
    k.emit("vector", lambda v: v.tensor_tensor(
        out=rstd[:, 0:ntiles], in0=rstd[:, 0:ntiles], in1=sums[:, 0:ntiles],
        op=ALU.subtract), reads=[(tag, "sums"), (tag, "rstd")],
        writes=[(tag, "rstd")])
    k.emit("scalar", lambda s: s.activation(
        out=sumsq[:, 0:ntiles], in_=rstd[:, 0:ntiles], func=AF.Sqrt,
        bias=eps_col[:, 0:1]),
        reads=[(tag, "rstd"), ("epsc",)], writes=[(tag, "sumsq")])
    k.emit("vector", lambda v: v.reciprocal(
        out=rstd[:, 0:ntiles], in_=sumsq[:, 0:ntiles]),
        reads=[(tag, "sumsq")], writes=[(tag, "rstd")])


def build_l0_attn(debug=False, races=True, ses=True, fuse=None):
    if fuse is None:
        nc = bass.Bass("TRN2", target_bir_lowering=False,
                       detect_race_conditions=races)
        ctx = ExitStack()
        k = K(nc, ctx)
        outer = ctx
    else:
        nc, ctx, k = fuse["nc"], fuse["ctx"], fuse["k"]
        outer = nullcontext()

    def din(name, shape, dtype=F32):
        return nc.dram_tensor(k.pfx + name, shape, dtype,
                              kind="ExternalInput").ap()

    x_h = fuse["x"] if fuse else din("x", [4096, 1024])
    Wq_h = din("Wq", [1024, 1024]); bq_h = din("bq", [1, 1024])
    Wl_h = din("Wl", [1024, 256]);  bl_h = din("bl", [1, 256])
    Wk_h = din("Wk", [256, 1024]);  bk_h = din("bk", [1, 1024])
    Wv_h = din("Wv", [256, 1024]);  bv_h = din("bv", [1, 1024])
    Wo_h = din("Wo", [1024, 1024]); bo_h = din("bo", [1, 1024])
    idn_h = din("IDN", [128, 128], BF16)
    y_h = fuse["y"] if fuse else nc.dram_tensor(
        "y", [4096, 1024], F32, kind="ExternalOutput").ap()

    SCALE = 64 ** (-0.5)

    dbg = {}
    def dout(name, shape, dtype=F32):
        dbg[name] = nc.dram_tensor("dbg_" + name, shape, dtype,
                                   kind="ExternalOutput").ap()
        return dbg[name]
    if debug:
        dout("xnfm", [128, 16384], BF16)
        dout("stats", [128, 64], F32)
        dout("sumsq_raw", [128, 16], F32)
        dout("lat", [128, 4096], BF16)
        dout("vsb", [128, 16640], BF16)
        dout("q0", [64, 2048], BF16)
        dout("k0", [64, 2048], BF16)
        dout("nm0", [1, 16], F32)
        dout("negc0", [128, 1], F32)
        dout("pts00", [128, 1024], BF16)
        dout("linv00", [1, 512], F32)
        dout("ao0", [128, 16384], BF16)
    with outer:
        wq = k.sbuf("wq", [128, 8 * 1024], BF16)
        wl = k.sbuf("wl", [128, 8 * 256], BF16)
        wk = k.sbuf("wk", [128, 2 * 1024], BF16)
        wv = k.sbuf("wv", [128, 2 * 1024], BF16)
        wo = k.sbuf("wo", [128, 8 * 1024], BF16)
        bq = k.sbuf("bq_r", [1, 1024], F32); bl = k.sbuf("bl_r", [1, 256], F32)
        bk = k.sbuf("bk_r", [1, 1024], F32); bv = k.sbuf("bv_r", [1, 1024], F32)
        bo = k.sbuf("bo_r", [1, 1024], F32)
        onesf = k.sbuf("onesf", [1, 512], F32)
        ones64 = k.sbuf("ones64", [64, 1], BF16)
        idn = k.sbuf("idn", [128, 128], BF16)
        scrA = k.sbuf("scrA", [128, 2 * 1024], F32)
        scrB = k.sbuf("scrB", [128, 2 * 1024], F32)
        sums = k.sbuf("sums", [128, 16], F32)
        sumsq = k.sbuf("sumsq", [128, 16], F32)
        negmean = k.sbuf("negmean", [128, 16], F32)
        rstd = k.sbuf("rstd", [128, 16], F32)
        sqscr = k.sbuf("sqscr", [128, 1024], F32)
        xnscr = k.sbuf("xnscr", [128, 2 * 1024], BF16)
        xn_fm = k.sbuf("xn_fm", [128, 8 * 2048], BF16)
        lat = k.sbuf("lat", [128, 2 * 2048], BF16)
        v_sb = k.sbuf("v_sb", [128, 16 * 1040], BF16)
        q_sb = k.sbuf("q_sb", [64, 2048], BF16)
        k_sb = k.sbuf("k_sb", [64, 2048], BF16)
        hsq = sqscr[0:64, :].bitcast(BF16)
        pts = k.sbuf("pts", [128, 2 * 1024], BF16)
        attnout = k.sbuf("attnout", [128, 8 * 2048], BF16)
        nm = k.sbuf("nm", [1, 16], F32)
        epsc = k.sbuf("epsc", [128, 1], F32)
        negc = k.sbuf("negc", [128, 1], F32)
        linv = k.sbuf("linv", [1, 512], F32)
        if not hasattr(k, "PSB"):
            k.PSB = ctx.enter_context(
                nc.psum_tensor("PSB", [128, 4096], F32))

        load_convert_bf16(k, Wq_h, 8, 1024, wq, 0, scrA, 1024, tag="wq")
        load_convert_bf16(k, Wl_h, 8, 256, wl, 0, scrA, 1024, tag="wl")
        load_convert_bf16(k, Wk_h, 2, 1024, wk, 0, scrA, 1024, tag="wk")
        load_convert_bf16(k, Wv_h, 2, 1024, wv, 0, scrA, 1024, tag="wv")
        load_convert_bf16(k, Wo_h, 8, 1024, wo, 0, scrA, 1024, tag="wo")
        for dst, src in [(bq, bq_h), (bl, bl_h), (bk, bk_h), (bv, bv_h), (bo, bo_h)]:
            k.dma("sync", dst[:], src, "dwi", writes=[("bias",)], full=True)
        k.dma("sync", idn[:], idn_h, "dwi", writes=[("idn",)], full=True)
        k.emit("vector", lambda v: v.memset(onesf[:], 1.0), writes=[("ones",)])
        k.emit("vector", lambda v: v.memset(epsc[:], 1e-5), writes=[("epsc",)])
        k.emit("vector", lambda v: v.memset(nm[:], 0.0), writes=[("nm",)])
        k.emit("vector", lambda v: v.memset(attnout[:], 0.0), writes=[("attnout",)])
        k.emit("vector", lambda v: v.memset(ones64[:], 1.0), writes=[("ones64",)])

        WK = {"wq": [("wq", "dst")], "wl": [("wl", "dst")], "wk": [("wk", "dst")],
              "wv": [("wv", "dst")], "wo": [("wo", "dst")]}

        for s in range(2):
            tb = 16 * s
            tag = "ln"
            ln_tokenmajor(k, lambda t: x_h[(tb + t) * 128:(tb + t + 1) * 128, :],
                          16, scrB, sums, sumsq, negmean, rstd, sqscr, epsc, tag,
                          post_sq_hook=(lambda: k.dma("gpsimd", dbg["sumsq_raw"],
                                                      sumsq[:], "dbg",
                                                      reads=[(tag, "sumsq")]))
                          if debug and s == 0 else None)
            for t in range(16):
                par = t % 2
                xt = scrB[:, par * 1024:(par + 1) * 1024]
                k.dma("sync", xt, x_h[(tb + t) * 128:(tb + t + 1) * 128, :],
                      f"dx{par}", writes=[(tag, "scr", par)])
                xn = xnscr[:, par * 1024:(par + 1) * 1024]
                k.emit("vector", lambda v, o=xn, i=xt, m=negmean[:, t:t + 1],
                       r=rstd[:, t:t + 1]: v.tensor_scalar(
                           out=o, in0=i, scalar1=m, scalar2=r, op0=ALU.add,
                           op1=ALU.mult),
                       reads=[(tag, "scr", par), (tag, "negmean"), (tag, "rstd")],
                       writes=[("xn", par)])
                pbf = psb_bf16(k, par, 1)
                for c in range(8):
                    k.emit("tensor", lambda te, o=pbf[:, c * 128:(c + 1) * 128],
                           i=xn[:, c * 128:(c + 1) * 128]:
                           te.transpose(out=o, in_=i, identity=idn[:]),
                           reads=[("xn", par), ("idn",)], writes=psk(par, 1),
                           inc=False)
                o_ap = xn_fm[:, :].rearrange("p (c n) -> p c n", c=8)[
                    :, :, t * 128:(t + 1) * 128]
                i_ap = pbf.rearrange("p (c r) -> p c r", c=8)
                k.emit("vector", lambda v, o=o_ap, i=i_ap: v.tensor_copy(out=o, in_=i),
                       reads=psk(par, 1), writes=[("xnfm",)])
            if debug and s == 0:
                k.dma("gpsimd", dbg["xnfm"], xn_fm[:], "dbg", reads=[("xnfm",)])
                k.dma("gpsimd", dbg["stats"][:, 0:16], negmean[:], "dbg",
                      reads=[("ln", "negmean")])
                k.dma("gpsimd", dbg["stats"][:, 16:32], rstd[:], "dbg",
                      reads=[("ln", "rstd")])
                k.dma("gpsimd", dbg["stats"][:, 32:48], sums[:], "dbg",
                      reads=[("ln", "sums")])
                k.dma("gpsimd", dbg["stats"][:, 48:64], sumsq[:], "dbg",
                      reads=[("ln", "sumsq")])
            for c in range(2):
                for qb in range(4):
                    it = c * 4 + qb
                    par = it % 2
                    pl = psb_f32(k, par, 1)
                    for kc in range(8):
                        k.emit("tensor", lambda te, o=pl,
                               l=wl[:, kc * 256 + c * 128: kc * 256 + (c + 1) * 128],
                               r=xn_fm[:, kc * 2048 + qb * 512: kc * 2048 + (qb + 1) * 512],
                               st=(kc == 0):
                               te.matmul(out=o, lhsT=l, rhs=r, start=st,
                                         stop=False),
                               reads=[("wl", "dst"), ("xnfm",)], writes=psk(par, 1),
                               inc=False)
                    k.emit("tensor", lambda te, o=pl, l=bl[0:1, c * 128:(c + 1) * 128],
                           r=onesf[0:1, 0:512]:
                           te.matmul(out=o, lhsT=l, rhs=r, start=False, stop=True),
                           reads=[("bias",), ("ones",)], writes=psk(par, 1), inc=False)
                    k.emit("scalar", lambda sc, o=lat[:, c * 2048 + qb * 512:
                                                      c * 2048 + (qb + 1) * 512],
                           i=pl: sc.copy(out=o, in_=i),
                           reads=psk(par, 1), writes=[("lat",)])
            k.emit("vector", lambda v: v.memset(v_sb[:], 1.0), writes=[("vsb",)])
            for t in range(16):
                par = t % 2
                pv = psb_f32(k, 2 * par, 2)
                for kc in range(2):
                    for nc2 in range(2):
                        k.emit("tensor", lambda te,
                               o=pv[:, nc2 * 512:(nc2 + 1) * 512],
                               l=lat[:, kc * 2048 + t * 128: kc * 2048 + (t + 1) * 128],
                               r=wv[:, kc * 1024 + nc2 * 512: kc * 1024 + (nc2 + 1) * 512],
                               st=(kc == 0):
                               te.matmul(out=o, lhsT=l, rhs=r, start=st,
                                         stop=False),
                               reads=[("lat",), ("wv", "dst")], writes=psk(2 * par, 2),
                               inc=False)
                for nc2 in range(2):
                    k.emit("tensor", lambda te, o=pv[:, nc2 * 512:(nc2 + 1) * 512],
                           l=onesf[0:1, 0:128], r=bv[0:1, nc2 * 512:(nc2 + 1) * 512]:
                           te.matmul(out=o, lhsT=l, rhs=r, start=False, stop=True),
                           reads=[("bias",), ("ones",)], writes=psk(2 * par, 2),
                           inc=False)
                o_ap = v_sb[:, t * 1040:(t + 1) * 1040].rearrange(
                    "p (h c) -> p h c", h=16)[:, :, 0:64]
                i_ap = pv.rearrange("p (h c) -> p h c", h=16)
                k.emit("vector", lambda v, o=o_ap, i=i_ap: v.tensor_copy(out=o, in_=i),
                       reads=psk(2 * par, 2), writes=[("vsb",)])
            if debug and s == 0:
                k.dma("gpsimd", dbg["lat"], lat[:], "dbg", reads=[("lat",)])
                k.dma("gpsimd", dbg["vsb"], v_sb[:], "dbg", reads=[("vsb",)])
            for h in range(16):
                pq = psb_f32(k, 0, 4, rows=64)
                for qb in range(4):
                    pqs = pq[:, qb * 512:(qb + 1) * 512]
                    for kc in range(8):
                        k.emit("tensor", lambda te, o=pqs,
                               l=wq[:, kc * 1024 + h * 64: kc * 1024 + h * 64 + 64],
                               r=xn_fm[:, kc * 2048 + qb * 512: kc * 2048 + (qb + 1) * 512],
                               st=(kc == 0):
                               te.matmul(out=o, lhsT=l, rhs=r, start=st,
                                         stop=False),
                               reads=[("wq", "dst"), ("xnfm",)], writes=psk(0, 4),
                               inc=False)
                    k.emit("tensor", lambda te, o=pqs, l=bq[0:1, h * 64: h * 64 + 64],
                           r=onesf[0:1, 0:512]:
                           te.matmul(out=o, lhsT=l, rhs=r, start=False, stop=True),
                           reads=[("bias",), ("ones",)], writes=psk(0, 4), inc=False)
                k.emit("scalar", lambda sc: sc.copy(out=q_sb[:], in_=pq[:]),
                       reads=psk(0, 4), writes=[("q",)])
                k.emit("scalar", lambda sc: sc.activation(
                    out=hsq, in_=pq[:], func=AF.Square),
                    reads=psk(0, 4), writes=[("hsq",)])
                pn = psb_f32(k, 4, 1, rows=1)
                for qb in range(4):
                    k.emit("tensor", lambda te, r=hsq[:, qb * 512:(qb + 1) * 512]:
                           te.matmul(out=pn, lhsT=ones64[:], rhs=r, start=True,
                                     stop=True),
                           reads=[("ones64",), ("hsq",)], writes=psk(4, 1), inc=False)
                    k.emit("vector", lambda v, o=nm[0:1, qb:qb + 1]:
                           v.reduce_max(out=o, in_=pn, axis=AX.X),
                           reads=psk(4, 1), writes=[("nm",)])
                for qb in range(4):
                    pqs = pq[:, qb * 512:(qb + 1) * 512]
                    for kc in range(2):
                        k.emit("tensor", lambda te, o=pqs,
                               l=wk[:, kc * 1024 + h * 64: kc * 1024 + h * 64 + 64],
                               r=lat[:, kc * 2048 + qb * 512: kc * 2048 + (qb + 1) * 512],
                               st=(kc == 0):
                               te.matmul(out=o, lhsT=l, rhs=r, start=st,
                                         stop=False),
                               reads=[("wk", "dst"), ("lat",)], writes=psk(0, 4),
                               inc=False)
                    k.emit("tensor", lambda te, o=pqs, l=bk[0:1, h * 64: h * 64 + 64],
                           r=onesf[0:1, 0:512]:
                           te.matmul(out=o, lhsT=l, rhs=r, start=False, stop=True),
                           reads=[("bias",), ("ones",)], writes=psk(0, 4), inc=False)
                k.emit("scalar", lambda sc: sc.copy(out=k_sb[:], in_=pq[:]),
                       reads=psk(0, 4), writes=[("k",)])
                k.emit("scalar", lambda sc: sc.activation(
                    out=hsq, in_=pq[:], func=AF.Square),
                    reads=psk(0, 4), writes=[("hsq",)])
                for qb in range(4):
                    k.emit("tensor", lambda te, r=hsq[:, qb * 512:(qb + 1) * 512]:
                           te.matmul(out=pn, lhsT=ones64[:], rhs=r, start=True,
                                     stop=True),
                           reads=[("ones64",), ("hsq",)], writes=psk(4, 1), inc=False)
                    k.emit("vector", lambda v, o=nm[0:1, 4 + qb:5 + qb]:
                           v.reduce_max(out=o, in_=pn, axis=AX.X),
                           reads=psk(4, 1), writes=[("nm",)])
                k.emit("vector", lambda v: v.reduce_max(
                    out=nm[0:1, 8:9], in_=nm[0:1, 0:4], axis=AX.X),
                    reads=[("nm",)], writes=[("nm",)])
                k.emit("vector", lambda v: v.reduce_max(
                    out=nm[0:1, 9:10], in_=nm[0:1, 4:8], axis=AX.X),
                    reads=[("nm",)], writes=[("nm",)])
                k.emit("vector", lambda v: v.tensor_tensor(
                    out=nm[0:1, 10:11], in0=nm[0:1, 8:9], in1=nm[0:1, 9:10],
                    op=ALU.mult), reads=[("nm",)], writes=[("nm",)])
                k.emit("scalar", lambda sc: sc.activation(
                    out=nm[0:1, 11:12], in_=nm[0:1, 10:11], func=AF.Sqrt),
                    reads=[("nm",)], writes=[("nm",)])
                k.emit("vector", lambda v: v.tensor_scalar(
                    out=nm[0:1, 12:13], in0=nm[0:1, 11:12], scalar1=-SCALE,
                    scalar2=None, op0=ALU.mult), reads=[("nm",)], writes=[("nm",)])
                if debug and s == 0 and h == 0:
                    k.dma("gpsimd", dbg["q0"], q_sb[:], "dbg", reads=[("q",)])
                    k.dma("gpsimd", dbg["k0"], k_sb[:], "dbg", reads=[("k",)])
                    k.dma("gpsimd", dbg["nm0"], nm[:], "dbg", reads=[("nm",)])
                pnb = psb_f32(k, 5, 1)[:, 0:1]
                k.emit("tensor", lambda te: te.matmul(
                    out=pnb, lhsT=onesf[0:1, 0:128], rhs=nm[0:1, 12:13],
                    start=True, stop=True),
                    reads=[("nm",), ("ones",)], writes=psk(5, 1), inc=False)
                k.emit("vector", lambda v: v.tensor_copy(out=negc[:], in_=pnb),
                       reads=psk(5, 1), writes=[("negc",)])
                if debug and s == 0 and h == 0:
                    k.dma("gpsimd", dbg["negc0"], negc[:], "dbg", reads=[("negc",)])
                pO = psb_f32(k, 0, 4, rows=65)
                for kc in range(16):
                    for qh in range(2):
                        it = kc * 2 + qh
                        par = it % 2
                        pstv = psb_f32(k, 4 + 2 * par, 2)
                        for j in range(2):
                            qb = 2 * qh + j
                            k.emit("tensor", lambda te,
                                   o=pstv[:, j * 512:(j + 1) * 512],
                                   l=k_sb[0:64, kc * 128:(kc + 1) * 128],
                                   r=q_sb[0:64, qb * 512:(qb + 1) * 512]:
                                   te.matmul(out=o, lhsT=l, rhs=r, start=True,
                                             stop=True),
                                   reads=[("q",), ("k",)], writes=psk(4 + 2 * par, 2),
                                   inc=False)
                        ptv = pts[:, par * 1024:(par + 1) * 1024]
                        k.emit("scalar", lambda sc, o=ptv, i=pstv:
                               sc.activation(out=o, in_=i, func=AF.Exp, scale=SCALE,
                                             bias=negc[:, 0:1]),
                               reads=psk(4 + 2 * par, 2) + [("negc",)],
                               writes=[("pts", par)])
                        if debug and s == 0 and h == 0 and kc == 0 and qh == 0:
                            k.dma("gpsimd", dbg["pts00"], pts[:, 0:1024], "dbg",
                                  reads=[("pts", 0)])
                        for j in range(2):
                            qb = 2 * qh + j
                            k.emit("tensor", lambda te,
                                   o=pO[:, qb * 512:(qb + 1) * 512],
                                   l=v_sb[:, kc * 1040 + h * 65: kc * 1040 + h * 65 + 65],
                                   r=ptv[:, j * 512:(j + 1) * 512],
                                   st=(kc == 0), sp=(kc == 15):
                                   te.matmul(out=o, lhsT=l, rhs=r, start=st,
                                             stop=sp),
                                   reads=[("vsb",), ("pts", par)], writes=psk(0, 4),
                                   inc=False)
                for qb in range(4):
                    par = qb % 2
                    k.emit("vector", lambda v, o=linv[0:1, 0:512],
                           i=pO[64:65, qb * 512:(qb + 1) * 512]:
                           v.reciprocal(out=o, in_=i),
                           reads=[("PS", qb)], writes=[("linv",)])
                    pbv = psb_f32(k, 4 + par, 1, rows=64)
                    k.emit("tensor", lambda te, o=pbv,
                           r=linv[0:1, 0:512]:
                           te.matmul(out=o, lhsT=onesf[0:1, 0:64], rhs=r,
                                     start=True, stop=True),
                           reads=[("linv",), ("ones",)], writes=psk(4 + par, 1),
                           inc=False)
                    ao = attnout[64 * (h % 2):64 * (h % 2) + 64,
                                 (h // 2) * 2048 + qb * 512:
                                 (h // 2) * 2048 + (qb + 1) * 512]
                    k.emit("scalar", lambda sc, o=ao, i=pO[0:64, qb * 512:(qb + 1) * 512]:
                           sc.copy(out=o, in_=i),
                           reads=[("PS", qb)], writes=[("attnout",)])
                    k.emit("vector", lambda v, o=ao, b=pbv: v.tensor_tensor(
                        out=o, in0=o, in1=b, op=ALU.mult),
                        reads=[("attnout",), ("PS", 4 + par)], writes=[("attnout",)])
                    if debug and s == 0 and h == 0 and qb == 0:
                        k.dma("gpsimd", dbg["linv00"], linv[:], "dbg",
                              reads=[("linv",)])
                if debug and s == 0 and h == 0:
                    k.dma("gpsimd", dbg["ao0"], attnout[:], "dbg",
                          reads=[("attnout",)])
            for t in range(16):
                par = t % 2
                xres = scrA[:, par * 1024:(par + 1) * 1024]
                k.dma("gpsimd", xres, x_h[(tb + t) * 128:(tb + t + 1) * 128, :],
                      f"dr{par}", writes=[("scrA", par)])
                pw = psb_f32(k, 2 * par, 2)
                for nc2 in range(2):
                    for kc in range(8):
                        k.emit("tensor", lambda te, o=pw[:, nc2 * 512:(nc2 + 1) * 512],
                               l=attnout[:, kc * 2048 + t * 128: kc * 2048 + (t + 1) * 128],
                               r=wo[:, kc * 1024 + nc2 * 512: kc * 1024 + (nc2 + 1) * 512],
                               st=(kc == 0):
                               te.matmul(out=o, lhsT=l, rhs=r, start=st,
                                         stop=False),
                               reads=[("attnout",), ("wo", "dst")],
                               writes=psk(2 * par, 2), inc=False)
                    k.emit("tensor", lambda te, o=pw[:, nc2 * 512:(nc2 + 1) * 512],
                           l=onesf[0:1, 0:128], r=bo[0:1, nc2 * 512:(nc2 + 1) * 512]:
                           te.matmul(out=o, lhsT=l, rhs=r, start=False, stop=True),
                           reads=[("bias",), ("ones",)], writes=psk(2 * par, 2),
                           inc=False)
                yt = scrB[:, par * 1024:(par + 1) * 1024]
                k.emit("vector", lambda v, o=yt, i0=xres, i1=pw: v.tensor_tensor(
                    out=o, in0=i0, in1=i1, op=ALU.add),
                    reads=[("scrA", par)] + psk(2 * par, 2),
                    writes=[("ln", "scr", par)])
                k.dma("gpsimd", y_h[(tb + t) * 128:(tb + t + 1) * 128, :], yt,
                      f"dy{par}", reads=[("ln", "scr", par)], writes=[("y", par)])
        k.emit("sync", lambda e: e.nop(), reads=[("y", 0)])
        nops = k.finalize(same_engine_sync=ses) if fuse is None else {}
    return nc, nops


CFG_L0FFN = dict(name="l0ffn", depth=6, H=16, W=256, INNER=1024, HID=2730,
                 hgroups=2, tdenom=128)
CFG_L1FFN = dict(name="l1ffn", depth=2, H=8, W=512, INNER=512, HID=2730,
                 hgroups=1, tdenom=256)


def host_consts_inner(W):
    import numpy as np
    t = np.arange(2048, dtype=np.float64)
    di = np.arange(32, dtype=np.float64)
    inv_freq = 1.0 / 10000.0 ** (2 * di / 64)
    fr = t[:, None] * inv_freq[None, :]
    base = (2 * di + 0.4 * 64) / (1.4 * 64)
    power = (t - 1024.0) / (W // 2)
    scale = base[None, :] ** power[:, None]
    cs = np.cos(fr) * scale
    sn = np.sin(fr) * scale
    csk = np.cos(fr) / scale
    snk = np.sin(fr) / scale
    def dup(a):
        return np.concatenate([a, a], axis=1)
    CS = dup(cs); CSK = dup(csk)
    SSM = np.concatenate([-sn, sn], axis=1)
    SSMK = np.concatenate([-snk, snk], axis=1)
    def tm(a):
        return np.ascontiguousarray(
            a.reshape(16, 128, 64).transpose(1, 0, 2).reshape(128, 1024))
    TBL = np.concatenate([tm(CS), tm(SSM), tm(CSK), tm(SSMK)], axis=1)
    r = np.arange(128)[:, None]
    c = np.arange(W + 128)[None, :]
    MASK = np.where((c - r >= 0) & (c - r <= W), 0.0, -1e30)
    return bfc(TBL), bfc(MASK)


def build_inner_txf(cfg, debug=False, races=True, ses=True, depth_run=None,
                    seqs=(0, 1), fuse=None):
    name = cfg["name"]
    depth = cfg["depth"] if depth_run is None else depth_run
    full_depth = cfg["depth"]
    H, W, INNER, HID = cfg["H"], cfg["W"], cfg["INNER"], cfg["HID"]
    HG = cfg["hgroups"]
    HPG = H // HG
    NC3 = 3 * INNER
    NCH = INNER // 128
    LOOK = W // 128
    NHC = (HID + 127) // 128
    STB = LOOK + 2
    RK = 4 + LOOK

    if fuse is None:
        nc = bass.Bass("TRN2", target_bir_lowering=False,
                       detect_race_conditions=races)
        ctx = ExitStack()
        k = K(nc, ctx)
        outer = ctx
    else:
        nc, ctx, k = fuse["nc"], fuse["ctx"], fuse["k"]
        outer = nullcontext()

    def din(nm2, shape, dtype=F32):
        return nc.dram_tensor(k.pfx + nm2, shape, dtype,
                              kind="ExternalInput").ap()

    x_h = fuse["x"] if fuse else din("x", [4096, 1024])
    wqkv_h = din("wqkv", [full_depth, 1024, NC3])
    qs_h = din("qs", [full_depth, 64]); ks_h = din("ks", [full_depth, 64])
    wo_h = din("wo", [full_depth, INNER, 1024])
    lng_h = din("lng", [full_depth, 1024]); lnb_h = din("lnb", [full_depth, 1024])
    w1_h = din("w1", [full_depth, 1024, 5460]); b1_h = din("b1", [full_depth, 5460])
    w2_h = din("w2", [full_depth, 2730, 1024]); b2_h = din("b2", [full_depth, 1024])
    idn_h = din("IDN", [128, 128], BF16)
    tbl_h = din("TBL", [128, 4096], BF16)
    mask_h = din("MASK", [128, W + 128], BF16)
    y_h = fuse["y"] if fuse else nc.dram_tensor(
        "y", [4096, 1024], F32, kind="ExternalOutput").ap()
    cwqkv = nc.dram_tensor(k.pfx + "cwqkv", [full_depth, 1024, NC3], BF16,
                           kind="Internal").ap()
    cwo = nc.dram_tensor(k.pfx + "cwo", [full_depth, INNER, 1024], BF16,
                         kind="Internal").ap()
    cw1 = nc.dram_tensor(k.pfx + "cw1", [full_depth, 1024, 5460], BF16,
                         kind="Internal").ap()
    cw2 = nc.dram_tensor(k.pfx + "cw2", [full_depth, 2730, 1024], BF16,
                         kind="Internal").ap()

    dbg = {}
    def dout(nm2, shape, dtype=F32):
        dbg[nm2] = nc.dram_tensor("dbg_" + nm2, shape, dtype,
                                  kind="ExternalOutput").ap()
        return dbg[nm2]
    if debug:
        dout("xin", [128, 16 * 1024])
        dout("qkvt", [128, 1536], BF16)
        dout("qtm", [128, 512], BF16)
        dout("ktm", [128, 512], BF16)
        dout("ao", [128, NCH * 2048], BF16)
        dout("negc", [128, 1])
        dout("xn1", [128, 8192], BF16)
        dout("hh", [128, NHC * 1024], BF16)

    with outer:
        xin = k.sbuf("xin", [128, 16 * 1024], F32)
        wbuf = k.sbuf("wbuf", [128, 12288], BF16)
        bigbuf = k.sbuf("bigbuf", [128, 22528], BF16)
        kfm = k.sbuf("kfm", [128, RK * 512], BF16)
        vau = k.sbuf("vau", [128, RK * (HPG * 65)], BF16)
        qfm = k.sbuf("qfm", [128, 4 * 512], BF16)
        qktm = k.sbuf("qktm", [128, 1024], BF16)
        dtab = k.sbuf("dtab", [128, 4096], BF16)
        maskb = k.sbuf("maskb", [128, W + 128], BF16)
        pts = k.sbuf("pts", [128, max((LOOK + 1) * 512, 2048)], BF16)
        rtmp = pts[:, 0:1024]
        gelsb = pts[:, 1024:2048]
        pan = k.sbuf("pan", [128, 2 * 8 * 129], BF16)
        xs2 = k.sbuf("xs2", [128, 1024], F32)
        xnscr = k.sbuf("xnscr", [128, 1024], BF16)
        sqscr = k.sbuf("sqscr", [128, 1024], F32)
        scr = k.sbuf("scr", [128, 2 * 1024], F32)
        sums = k.sbuf("sums", [128, 16], F32)
        sumsq = k.sbuf("sumsq", [128, 16], F32)
        negmean = k.sbuf("negmean", [128, 16], F32)
        rstd = k.sbuf("rstd", [128, 16], F32)
        nrm = k.sbuf("nrm", [128, 32], F32)
        epsc = k.sbuf("epsc", [128, 1], F32)
        nm = k.sbuf("nm", [1, 32], F32)
        negc = k.sbuf("negc", [128, 1], F32)
        linv = k.sbuf("linv", [1, 512], F32)
        onesf = k.sbuf("onesf", [1, 512], F32)
        onesb = k.sbuf("onesb", [1, 512], BF16)
        zrowb = k.sbuf("zrowb", [1, 128], BF16)
        idn = k.sbuf("idn", [128, 128], BF16)
        qsr = k.sbuf("qsr", [1, 64], F32); ksr = k.sbuf("ksr", [1, 64], F32)
        qsb = k.sbuf("qsb", [128, 128], BF16)
        gbc = k.sbuf("gbc", [128, 1024], BF16)
        bbc = k.sbuf("bbc", [128, 1024], BF16)
        b2r = k.sbuf("b2r", [1, 1024], F32)
        b1sl = k.sbuf("b1sl", [1, 4 * 128], F32)
        if not hasattr(k, "PSB"):
            k.PSB = ctx.enter_context(
                nc.psum_tensor("PSB", [128, 4096], F32))

        def wpage(c0, c1, buf="wb"):
            return [(buf, p) for p in range(c0 // 1024, (c1 + 1023) // 1024)]

        k.dma("sync", maskb[:], mask_h, "dc", writes=[("mask",)], full=True)
        k.dma("sync", idn[:], idn_h, "dc", writes=[("idn",)], full=True)
        k.emit("vector", lambda v: v.memset(onesf[:], 1.0), writes=[("ones",)])
        k.emit("vector", lambda v: v.memset(onesb[:], 1.0), writes=[("onesb",)])
        k.emit("vector", lambda v: v.memset(zrowb[:], 0.0), writes=[("zrow",)])
        k.emit("vector", lambda v: v.memset(epsc[:], 1e-5), writes=[("epsc",)])
        k.emit("vector", lambda v: v.memset(nm[:], 0.0), writes=[("nm",)])
        k.emit("vector", lambda v: v.memset(vau[:], 1.0),
               writes=[("vau", sl) for sl in range(RK)])
        k.emit("vector", lambda v: v.memset(bigbuf[:], 0.0),
               writes=[("bb", p) for p in range(22)])

        def loadw(src_f32, cache_bf, dst, dcol, rows, cols, first, buf="wb",
                  conv="gpsimd"):
            nch2 = ceil_div(rows, 128)
            assert cols <= 1024
            for c in range(nch2):
                r0, r1 = c * 128, min((c + 1) * 128, rows)
                par = c % 2
                keys = wpage(dcol + c * cols, dcol + (c + 1) * cols, buf)
                dv = dst[0:r1 - r0, dcol + c * cols: dcol + (c + 1) * cols]
                if first:
                    sc = scr[0:r1 - r0, par * 1024: par * 1024 + cols]
                    k.dma("sync", sc, src_f32[r0:r1, :], f"dw{par}",
                          writes=[("scr", par)])
                    k.emit(conv, lambda e, o=dv, i=sc: e.tensor_copy(out=o, in_=i),
                           reads=[("scr", par)], writes=keys)
                    k.dma("gpsimd", cache_bf[r0:r1, :], dv, f"cw{par}", reads=keys)
                else:
                    k.dma("sync", dv, cache_bf[r0:r1, :], f"dw{par}", writes=keys)

        def qkv_tile(s, t, hg):
            par = t % 2
            xbf = xnscr
            k.emit("vector", lambda v, o=xbf[:], i=xin[:, t * 1024:(t + 1) * 1024]:
                   v.tensor_copy(out=o, in_=i),
                   reads=[("xin", t)], writes=[("xn",)])
            ptr = psb_bf16(k, 6 + par, 1)
            for c in range(8):
                k.emit("tensor", lambda te, o=ptr[:, c * 128:(c + 1) * 128],
                       i=xbf[:, c * 128:(c + 1) * 128]:
                       te.transpose(out=o, in_=i, identity=idn[:]),
                       reads=[("xn",), ("idn",)], writes=psk(6 + par, 1), inc=False)
            pview = pan[:, par * 1032:(par + 1) * 1032].rearrange(
                "p (c n) -> p c n", c=8)
            if t % 8 == 0 and hg is not None:
                pass
            if t == 0:
                k.emit("vector", lambda v, o=pview[:, :, 0:1]: v.memset(o, 0.0),
                       writes=[("pan", par)])
            k.emit("vector", lambda v, o=pview[:, :, 1:129],
                   i=ptr.rearrange("p (c n) -> p c n", c=8):
                   v.tensor_copy(out=o, in_=i),
                   reads=psk(6 + par, 1), writes=[("pan", par)])
            if t < 15:
                oview = pan[:, (1 - par) * 1032:(2 - par) * 1032].rearrange(
                    "p (c n) -> p c n", c=8)
                k.emit("vector", lambda v, o=oview[:, :, 0:1],
                       i=ptr.rearrange("p (c n) -> p c n", c=8)[:, :, 127:128]:
                       v.tensor_copy(out=o, in_=i),
                       reads=psk(6 + par, 1), writes=[("pan", 1 - par)])
            pp = psb_f32(k, 3 * par, 3)
            for kc in range(8):
                coff = kc * 129 + (1 if kc < 4 else 0)
                lv = pan[:, par * 1032 + coff: par * 1032 + coff + 128]
                for rj in range(3):
                    k.emit("tensor", lambda te, o=pp[:, rj * 512:(rj + 1) * 512],
                           l=lv, r=wbuf[:, rj * 4096 + kc * 512:
                                        rj * 4096 + (kc + 1) * 512],
                           st=(kc == 0), sp=(kc == 7):
                           te.matmul(out=o, lhsT=l, rhs=r, start=st, stop=sp),
                           reads=[("pan", par)] + wpage(rj * 4096, (rj + 1) * 4096),
                           writes=psk(3 * par, 3), inc=False)
            k.emit("scalar", lambda sc2: sc2.copy(out=qktm[:], in_=pp[:, 0:1024]),
                   reads=psk(3 * par, 3), writes=[("qktm",)])
            slot = t % RK
            vdst = vau[:, slot * (HPG * 65):(slot + 1) * (HPG * 65)].rearrange(
                "p (h c) -> p h c", h=HPG)[:, :, 0:64]
            k.emit("scalar", lambda sc2, o=vdst,
                   i=pp[:, 1024:1536].rearrange("p (h c) -> p h c", h=HPG):
                   sc2.copy(out=o, in_=i),
                   reads=psk(3 * par, 3), writes=[("vau", slot)])
            k.emit("scalar", lambda sc2: sc2.activation(
                out=sqscr[:], in_=qktm[:], func=AF.Square),
                reads=[("qktm",)], writes=[("sq",)])
            k.emit("vector", lambda v: v.tensor_reduce(
                out=nrm[:, 0:16], in_=sqscr.rearrange("p (s d) -> p s d", d=64),
                axis=AX.X, op=ALU.add),
                reads=[("sq",)], writes=[("nrm",)])
            k.emit("scalar", lambda sc2: sc2.activation(
                out=nrm[:, 16:32], in_=nrm[:, 0:16], func=AF.Sqrt),
                reads=[("nrm",)], writes=[("nrm",)])
            k.emit("vector", lambda v: v.tensor_scalar_max(
                out=nrm[:, 16:32], in0=nrm[:, 16:32], scalar1=1e-12),
                reads=[("nrm",)], writes=[("nrm",)])
            k.emit("vector", lambda v: v.reciprocal(
                out=nrm[:, 0:16], in_=nrm[:, 16:32]),
                reads=[("nrm",)], writes=[("nrm",)])
            for which, c0, tb0, rinv0 in [("q", 0, 0, 0), ("k", 512, 2048, 8)]:
                src = qktm[:, c0:c0 + 512]
                sview = src.rearrange("p (h d) -> p h d", d=64)
                rb = nrm[:, rinv0:rinv0 + 8].unsqueeze(2).broadcast_to([128, 8, 64])
                k.emit("vector", lambda v, o=sview, i=sview, f=rb:
                       v.tensor_tensor(out=o, in0=i, in1=f, op=ALU.mult),
                       reads=[("qktm",), ("nrm",)], writes=[("qktm",)])
                cs = dtab[:, tb0 + t % 16 * 64: tb0 + (t % 16) * 64 + 64]
                csb = cs.unsqueeze(1).broadcast_to([128, 8, 64])
                t1 = rtmp[:, 0:512].rearrange("p (h d) -> p h d", d=64)
                k.emit("vector", lambda v, o=t1, i=sview, f=csb:
                       v.tensor_tensor(out=o, in0=i, in1=f, op=ALU.mult),
                       reads=[("qktm",), ("dtab",)], writes=[("rt", 0), ("pts",)])
                ss = dtab[:, tb0 + 1024 + (t % 16) * 64: tb0 + 1024 + (t % 16) * 64 + 64]
                t2 = rtmp[:, 512:1024].rearrange("p (h d) -> p h d", d=64)
                ssl = ss[:, 0:32].unsqueeze(1).broadcast_to([128, 8, 32])
                ssh = ss[:, 32:64].unsqueeze(1).broadcast_to([128, 8, 32])
                k.emit("vector", lambda v, o=t2[:, :, 0:32], i=sview[:, :, 32:64],
                       f=ssl: v.tensor_tensor(out=o, in0=i, in1=f, op=ALU.mult),
                       reads=[("qktm",), ("dtab",)], writes=[("rt", 1), ("pts",)])
                k.emit("vector", lambda v, o=t2[:, :, 32:64], i=sview[:, :, 0:32],
                       f=ssh: v.tensor_tensor(out=o, in0=i, in1=f, op=ALU.mult),
                       reads=[("qktm",), ("dtab",)], writes=[("rt", 1), ("pts",)])
                k.emit("vector", lambda v, o=src, i0=rtmp[:, 0:512],
                       i1=rtmp[:, 512:1024]:
                       v.tensor_tensor(out=o, in0=i0, in1=i1, op=ALU.add),
                       reads=[("rt", 0), ("rt", 1)], writes=[("qktm",)])
            if debug and s == 0 and t == 0 and hg == 0:
                k.dma("gpsimd", dbg["qtm"], qktm[:, 0:512], "dbg", reads=[("qktm",)])
                k.dma("gpsimd", dbg["ktm"], qktm[:, 512:1024], "dbg",
                      reads=[("qktm",)])
            ptr2 = psb_bf16(k, 6 + par, 1)
            for c in range(8):
                k.emit("tensor", lambda te, o=ptr2[:, c * 128:(c + 1) * 128],
                       i=qktm[:, c * 128:(c + 1) * 128]:
                       te.transpose(out=o, in_=i, identity=idn[:]),
                       reads=[("qktm",), ("idn",)], writes=psk(6 + par, 1),
                       inc=False)
            qslot = t % 4
            k.emit("vector", lambda v, o=qfm[:, qslot * 512:(qslot + 1) * 512],
                   i=ptr2[:, 0:512]: v.tensor_copy(out=o, in_=i),
                   reads=psk(6 + par, 1), writes=[("qfm", qslot)])
            kslot = t % RK
            k.emit("vector", lambda v, o=kfm[:, kslot * 512:(kslot + 1) * 512],
                   i=ptr2[:, 512:1024]: v.tensor_copy(out=o, in_=i),
                   reads=psk(6 + par, 1), writes=[("kfm", kslot)])

        def attn_qbh(s, li, hg, qb, h):
            segs = []
            ptcol = 0
            for kc in range(max(0, 4 * qb - LOOK), min(16, 4 * qb + 4)):
                a = max(512 * qb, 128 * kc)
                b = min(512 * qb + 512, 128 * kc + W + 128, 2048)
                if b <= a:
                    continue
                segs.append((kc, a, b, ptcol))
                ptcol += b - a
            pst = psb_f32(k, 0, STB)
            stk = psk(0, STB)
            first = True
            for kc, a, b, pc in segs:
                kslot = kc % RK
                lts = kfm[64 * (h % 2):64 * (h % 2) + 64,
                          kslot * 512 + (h // 2) * 128: kslot * 512 + (h // 2) * 128 + 128]
                t0 = a // 128
                qv = qfm[:, :].rearrange("p (u n) -> p u n", n=512)
                off = 0
                while off < b - a:
                    ln2 = min(b - a - off, 512 - (pc + off) % 512)
                    us = (t0 + off // 128) % 4
                    nslots = ln2 // 128
                    us128 = 128 // 128
                    q3p = qv[64 * (h % 2):64 * (h % 2) + 64,
                             us:us + nslots,
                             (h // 2) * 128:(h // 2) * 128 + 128]
                    k.emit("tensor", lambda te, o=pst[:, pc + off: pc + off + ln2],
                           l=lts, r=q3p:
                           te.matmul(out=o, lhsT=l, rhs=r, start=True, stop=False),
                           reads=[("kfm", kslot)] +
                                 [("qfm", us + j3) for j3 in range(nslots)],
                           writes=stk, inc=False)
                    mo = a - 128 * kc + off
                    k.emit("tensor", lambda te, o=pst[:, pc + off: pc + off + ln2],
                           l=idn[:], r=maskb[:, mo: mo + ln2]:
                           te.matmul(out=o, lhsT=l, rhs=r, start=False, stop=True),
                           reads=[("idn",), ("mask",)], writes=stk, inc=False)
                    off += ln2
            k.emit("scalar", lambda sc2, o=pts[:, 0:ptcol], i=pst[:, 0:ptcol]:
                   sc2.activation(out=o, in_=i, func=AF.Exp, scale=8.0,
                                  bias=negc[:, 0:1]),
                   reads=stk + [("negc",)], writes=[("pts",)])
            pO = psb_f32(k, 6, 1, rows=65)
            k.emit("tensor", lambda te: te.matmul(
                out=pO, lhsT=zrowb[0:1, 0:65], rhs=onesb[0:1, 0:512],
                start=True, stop=False),
                reads=[("zrow",), ("onesb",)], writes=psk(6, 1), inc=False)
            for i2, (kc, a, b, pc) in enumerate(segs):
                kslot = kc % RK
                lv = vau[:, kslot * (HPG * 65) + h * 65: kslot * (HPG * 65) + h * 65 + 65]
                k.emit("tensor", lambda te, o=pO[:, a - 512 * qb: b - 512 * qb],
                       l=lv, r=pts[:, pc: pc + (b - a)],
                       sp=(i2 == len(segs) - 1):
                       te.matmul(out=o, lhsT=l, rhs=r, start=False, stop=sp),
                       reads=[("vau", kslot), ("pts",)], writes=psk(6, 1), inc=False)
            hh2 = hg * HPG + h
            k.emit("vector", lambda v: v.reciprocal(out=linv[:], in_=pO[64:65, :]),
                   reads=psk(6, 1), writes=[("linv",)])
            pB = psb_f32(k, 7, 1, rows=64)
            k.emit("tensor", lambda te: te.matmul(
                out=pB, lhsT=onesf[0:1, 0:64], rhs=linv[:], start=True, stop=True),
                reads=[("linv",), ("ones",)], writes=psk(7, 1), inc=False)
            ao = bigbuf[64 * (hh2 % 2):64 * (hh2 % 2) + 64,
                        (hh2 // 2) * 2048 + 512 * qb:(hh2 // 2) * 2048 + 512 * qb + 512]
            aokeys = wpage((hh2 // 2) * 2048 + 512 * qb,
                           (hh2 // 2) * 2048 + 512 * qb + 512, "bb")
            k.emit("scalar", lambda sc2, o=ao: sc2.copy(out=o, in_=pO[0:64, :]),
                   reads=psk(6, 1), writes=aokeys)
            k.emit("vector", lambda v, o=ao: v.tensor_tensor(
                out=o, in0=o, in1=pB, op=ALU.mult),
                reads=aokeys + psk(7, 1), writes=aokeys)

        def outproj_tile(t):
            par = t % 2
            pw = psb_f32(k, 2 * par, 2)
            for kc in range(NCH):
                for nc2 in range(2):
                    k.emit("tensor", lambda te, o=pw[:, nc2 * 512:(nc2 + 1) * 512],
                           l=bigbuf[:, kc * 2048 + t * 128: kc * 2048 + (t + 1) * 128],
                           r=wbuf[:, kc * 1024 + nc2 * 512: kc * 1024 + (nc2 + 1) * 512],
                           st=(kc == 0), sp=(kc == NCH - 1):
                           te.matmul(out=o, lhsT=l, rhs=r, start=st, stop=sp),
                           reads=wpage(kc * 2048 + t * 128, kc * 2048 + (t + 1) * 128,
                                       "bb") + wpage(kc * 1024, (kc + 1) * 1024),
                           writes=psk(2 * par, 2), inc=False)
            k.emit("vector", lambda v, o=xin[:, t * 1024:(t + 1) * 1024]:
                   v.tensor_tensor(out=o, in0=o, in1=pw, op=ALU.add),
                   reads=psk(2 * par, 2) + [("xin", t)], writes=[("xin", t)])

        def ffn_half(s, li, hs, frst, dump):
            bnd = linv
            if hs == 0:
                k.dma("gpsimd", bnd[0:1, :],
                      xin[127:128, 7 * 1024 + 512: 8 * 1024], "shb",
                      reads=[("xin", 7)], writes=[("linv",)])
            def assemble(tl):
                t = hs * 8 + tl
                par = 0
                xt = xs2[:, 0:1024]
                k.dma("gpsimd", xt[:, 0:512], xin[:, t * 1024: t * 1024 + 512],
                      f"sh{par}", reads=[("xin", t)], writes=[("xs2", par)])
                k.dma("gpsimd", xt[1:128, 512:1024],
                      xin[0:127, t * 1024 + 512: t * 1024 + 1024],
                      f"sh{par}", reads=[("xin", t)], writes=[("xs2", par)])
                if tl > 0:
                    k.dma("gpsimd", xt[0:1, 512:1024],
                          xin[127:128, (t - 1) * 1024 + 512: (t - 1) * 1024 + 1024],
                          f"sh{par}", reads=[("xin", t - 1)], writes=[("xs2", par)])
                elif hs == 1:
                    k.dma("gpsimd", xt[0:1, 512:1024], bnd[0:1, :],
                          f"sh{par}", reads=[("linv",)], writes=[("xs2", par)])
                else:
                    k.emit("vector", lambda v, o=xt[0:1, 512:1024]: v.memset(o, 0.0),
                           writes=[("xs2", par)])
                return xt, par
            k.emit("vector", lambda v: v.memset(sumsq[:, 0:8], 0.0),
                   writes=[("fsumsq",)])
            for tl in range(8):
                xt, par = assemble(tl)
                k.emit("vector", lambda v, o=sums[:, tl:tl + 1], i=xt:
                       v.tensor_reduce(out=o, in_=i, axis=AX.X, op=ALU.add),
                       reads=[("xs2", par)], writes=[("fsums",)])
                k.emit("scalar", lambda sc2, i=xt, a=sumsq[:, tl:tl + 1]:
                       sc2.activation(out=sqscr[:], in_=i, func=AF.Square,
                                      accum_out=a),
                       reads=[("xs2", par)], writes=[("fsumsq",), ("sq",)])
            k.emit("vector", lambda v: v.tensor_scalar(
                out=negmean[:, 0:8], in0=sums[:, 0:8], scalar1=-1.0 / 1024,
                scalar2=None, op0=ALU.mult), reads=[("fsums",)],
                writes=[("fnegmean",)])
            k.emit("vector", lambda v: v.tensor_scalar(
                out=rstd[:, 0:8], in0=sumsq[:, 0:8], scalar1=1.0 / 1024,
                scalar2=None, op0=ALU.mult), reads=[("fsumsq",)], writes=[("frstd",)])
            k.emit("vector", lambda v: v.scalar_tensor_tensor(
                out=sums[:, 0:8], in0=negmean[:, 0:8], scalar=1.0,
                in1=negmean[:, 0:8], op0=ALU.mult, op1=ALU.mult),
                reads=[("fnegmean",)], writes=[("fsums",)])
            k.emit("vector", lambda v: v.tensor_tensor(
                out=rstd[:, 0:8], in0=rstd[:, 0:8], in1=sums[:, 0:8],
                op=ALU.subtract), reads=[("fsums",), ("frstd",)],
                writes=[("frstd",)])
            k.emit("scalar", lambda sc2: sc2.activation(
                out=sumsq[:, 0:8], in_=rstd[:, 0:8], func=AF.Sqrt,
                bias=epsc[:, 0:1]),
                reads=[("frstd",), ("epsc",)], writes=[("fsumsq",)])
            k.emit("vector", lambda v: v.reciprocal(
                out=rstd[:, 0:8], in_=sumsq[:, 0:8]),
                reads=[("fsumsq",)], writes=[("frstd",)])
            for tl in range(8):
                xt, par = assemble(tl)
                k.emit("vector", lambda v, i=xt, m=negmean[:, tl:tl + 1],
                       r=rstd[:, tl:tl + 1]:
                       v.tensor_scalar(out=sqscr[:], in0=i, scalar1=m, scalar2=r,
                                       op0=ALU.add, op1=ALU.mult),
                       reads=[("xs2", par), ("fnegmean",), ("frstd",)],
                       writes=[("sq",)])
                k.emit("vector", lambda v: v.tensor_tensor(
                    out=sqscr[:], in0=sqscr[:], in1=gbc[:], op=ALU.mult),
                    reads=[("sq",), ("gbc",)], writes=[("sq",)])
                k.emit("vector", lambda v: v.tensor_tensor(
                    out=xnscr[:], in0=sqscr[:], in1=bbc[:], op=ALU.add),
                    reads=[("sq",), ("bbc",)], writes=[("xn",)])
                ptr = psb_bf16(k, 6 + par, 1)
                for c in range(8):
                    k.emit("tensor", lambda te, o=ptr[:, c * 128:(c + 1) * 128],
                           i=xnscr[:, c * 128:(c + 1) * 128]:
                           te.transpose(out=o, in_=i, identity=idn[:]),
                           reads=[("xn",), ("idn",)], writes=psk(6 + par, 1),
                           inc=False)
                dst = wbuf[:, 0:8192].rearrange("p (c n) -> p c n", c=8)[
                    :, :, tl * 128:(tl + 1) * 128]
                k.emit("vector", lambda v, o=dst,
                       i=ptr.rearrange("p (c n) -> p c n", c=8):
                       v.tensor_copy(out=o, in_=i),
                       reads=psk(6 + par, 1), writes=wpage(0, 8192))
            if dump:
                k.dma("gpsimd", dbg["xn1"], wbuf[:, 0:8192], "dbg",
                      reads=wpage(0, 8192))
            for hc in range(NHC):
                acols = min(2730, (hc + 1) * 128) - hc * 128
                par = hc % 2
                for gi, base in enumerate([0, 2730]):
                    c0 = base + hc * 128
                    woff = 8192 + (2 * par + gi) * 1024
                    wsl3 = wbuf[:, woff:].rearrange(
                        "p (c n) -> p c n", n=128)[:, 0:8, 0:acols]
                    wk2 = wpage(woff, woff + 1024)
                    if frst:
                        sc3 = scr[:, par * 1024:].rearrange(
                            "p (c n) -> p c n", n=128)[:, 0:8, 0:acols]
                        k.dma("sync", sc3,
                              w1_h[li, :, c0:c0 + acols].rearrange(
                                  "(c p) n -> p c n", p=128),
                              f"dw1{par}", writes=[("scr", par)])
                        k.emit("gpsimd", lambda e, o=wsl3, i=sc3:
                               e.tensor_copy(out=o, in_=i),
                               reads=[("scr", par)], writes=wk2)
                        k.dma("gpsimd", cw1[li, :, c0:c0 + acols].rearrange(
                            "(c p) n -> p c n", p=128), wsl3, f"cw{par}",
                            reads=wk2)
                    else:
                        k.dma("sync", wsl3,
                              cw1[li, :, c0:c0 + acols].rearrange(
                                  "(c p) n -> p c n", p=128),
                              f"dw1{par}", writes=wk2)
                    bsl = b1sl[0:1, (2 * par + gi) * 128:(2 * par + gi) * 128 + acols]
                    k.dma("sync", bsl, b1_h[li:li + 1, c0:c0 + acols],
                          f"db1{par}{gi}", writes=[("b1sl", par, gi)])
                pa = psb_f32(k, 4 * par, 2, rows=acols)
                pg = psb_f32(k, 4 * par + 2, 2, rows=acols)
                for gi, pt2 in [(0, pa), (1, pg)]:
                    woff = 8192 + (2 * par + gi) * 1024
                    wk2 = wpage(woff, woff + 1024)
                    for qb2 in range(2):
                        for kc in range(8):
                            lw = wbuf[:, woff:].rearrange(
                                "p (c n) -> p c n", n=128)[:, kc, 0:acols]
                            k.emit("tensor", lambda te,
                                   o=pt2[:, qb2 * 512:(qb2 + 1) * 512], l=lw,
                                   r=wbuf[:, kc * 1024 + qb2 * 512:
                                          kc * 1024 + (qb2 + 1) * 512],
                                   st=(kc == 0):
                                   te.matmul(out=o, lhsT=l, rhs=r, start=st,
                                             stop=False),
                                   reads=wk2 + wpage(0, 8192),
                                   writes=psk(4 * par + 2 * gi, 2), inc=False)
                        bsl = b1sl[0:1, (2 * par + gi) * 128:
                                   (2 * par + gi) * 128 + acols]
                        k.emit("tensor", lambda te,
                               o=pt2[:, qb2 * 512:(qb2 + 1) * 512], l=bsl:
                               te.matmul(out=o, lhsT=l, rhs=onesf[0:1, 0:512],
                                         start=False, stop=True),
                               reads=[("b1sl", par, gi), ("ones",)],
                               writes=psk(4 * par + 2 * gi, 2), inc=False)
                k.emit("scalar", lambda sc2, o=gelsb[0:acols, 0:1024], i=pg:
                       sc2.activation(out=o, in_=i, func=AF.Gelu),
                       reads=psk(4 * par + 2, 2), writes=[("gel",), ("pts",)])
                hdst = bigbuf[0:acols, hc * 1024:(hc + 1) * 1024]
                k.emit("vector", lambda v, o=hdst, i0=pa, i1=gelsb[0:acols, 0:1024]:
                       v.tensor_tensor(out=o, in0=i0, in1=i1, op=ALU.mult),
                       reads=psk(4 * par, 2) + [("gel",)],
                       writes=wpage(hc * 1024, (hc + 1) * 1024, "bb"))
            if dump:
                k.dma("gpsimd", dbg["hh"], bigbuf[:, 0:NHC * 1024], "dbg",
                      reads=[("bb", p) for p in range(NHC)])
            for nc2 in range(4):
                par = nc2 % 2
                wsl3 = wbuf[:, par * 5632:].rearrange(
                    "p (c n) -> p c n", n=256)[:, 0:NHC, :]
                if frst:
                    cv = w2_h[li, 0:2688, nc2 * 256:(nc2 + 1) * 256].rearrange(
                        "(c p) n -> p c n", p=128)
                    for g4 in range(6):
                        h0, h1 = g4 * 4, min((g4 + 1) * 4, 21)
                        if h1 > h0:
                            sc3 = scr[:, par * 1024:].rearrange(
                                "p (c n) -> p c n", n=256)[:, 0:h1 - h0, :]
                            k.dma("sync", sc3, cv[:, h0:h1, :], f"dw2{par}",
                                  writes=[("scr", par)])
                            k.emit("gpsimd", lambda e, o=wsl3[:, h0:h1, :], i=sc3:
                                   e.tensor_copy(out=o, in_=i),
                                   reads=[("scr", par)], writes=wpage(par * 5632, (par + 1) * 5632))
                    sc2r = scr[0:42, par * 1024: par * 1024 + 256]
                    k.dma("sync", sc2r,
                          w2_h[li, 2688:2730, nc2 * 256:(nc2 + 1) * 256],
                          f"dw2{par}", writes=[("scr", par)])
                    k.emit("gpsimd", lambda e, o=wsl3[0:42, 21, :], i=sc2r:
                           e.tensor_copy(out=o, in_=i),
                           reads=[("scr", par)], writes=wpage(par * 5632, (par + 1) * 5632))
                    k.dma("gpsimd", cw2[li, 0:2688, nc2 * 256:(nc2 + 1) * 256]
                          .rearrange("(c p) n -> p c n", p=128),
                          wsl3[:, 0:21, :],
                          f"cw{par}", reads=wpage(par * 5632, (par + 1) * 5632))
                    k.dma("gpsimd", cw2[li, 2688:2730, nc2 * 256:(nc2 + 1) * 256],
                          wsl3[0:42, 21, :], f"cw{par}", reads=wpage(par * 5632, (par + 1) * 5632))
                else:
                    srcv = cw2[li, 0:2688, nc2 * 256:(nc2 + 1) * 256].rearrange(
                        "(c p) n -> p c n", p=128)
                    k.dma("sync", wsl3[:, 0:21, :], srcv[:, 0:21, :], f"dw2{par}",
                          writes=wpage(par * 5632, (par + 1) * 5632))
                    k.dma("sync", wsl3[0:42, 21, :],
                          cw2[li, 2688:2730, nc2 * 256:(nc2 + 1) * 256],
                          f"dw2{par}", writes=wpage(par * 5632, (par + 1) * 5632))
                for tl in range(8):
                    t = hs * 8 + tl
                    pw = psb_f32(k, tl % 2, 1)[:, 0:256]
                    for hc in range(NHC):
                        acols = min(2730, (hc + 1) * 128) - hc * 128
                        k.emit("tensor", lambda te, o=pw,
                               l=bigbuf[0:acols, hc * 1024 + tl * 128:
                                        hc * 1024 + (tl + 1) * 128],
                               r=wsl3[0:acols, hc, :], st=(hc == 0):
                               te.matmul(out=o, lhsT=l, rhs=r, start=st,
                                         stop=False),
                               reads=wpage(par * 5632, (par + 1) * 5632) + [("bb", hc)],
                               writes=psk(tl % 2, 1), inc=False)
                    k.emit("tensor", lambda te, o=pw,
                           r=b2r[0:1, nc2 * 256:(nc2 + 1) * 256]:
                           te.matmul(out=o, lhsT=onesf[0:1, 0:128], rhs=r,
                                     start=False, stop=True),
                           reads=[("b2r",), ("ones",)], writes=psk(tl % 2, 1),
                           inc=False)
                    k.emit("vector", lambda v, p2=pw,
                           o=xin[:, t * 1024 + nc2 * 256: t * 1024 + (nc2 + 1) * 256]:
                           v.tensor_tensor(out=o, in0=o, in1=p2, op=ALU.add),
                           reads=psk(tl % 2, 1) + [("xin", t)],
                           writes=[("xin", t)])

        seen_layer = set()
        for si, s in enumerate(seqs):
            tb = 16 * s
            tag = "oln"
            ln_tokenmajor(k, lambda t: x_h[(tb + t) * 128:(tb + t + 1) * 128, :],
                          16, scr, sums, sumsq, negmean, rstd, sqscr, epsc, tag,
                          dstream="dx")
            for t in range(16):
                par = t % 2
                xt = scr[:, par * 1024:(par + 1) * 1024]
                k.dma("sync", xt, x_h[(tb + t) * 128:(tb + t + 1) * 128, :],
                      f"dx{par}", writes=[(tag, "scr", par)])
                k.emit("vector", lambda v, o=xin[:, t * 1024:(t + 1) * 1024], i=xt,
                       m=negmean[:, t:t + 1], r=rstd[:, t:t + 1]:
                       v.tensor_scalar(out=o, in0=i, scalar1=m, scalar2=r,
                                       op0=ALU.add, op1=ALU.mult),
                       reads=[(tag, "scr", par), (tag, "negmean"), (tag, "rstd")],
                       writes=[("xin", t)])

            for li in range(depth):
                frst = li not in seen_layer
                seen_layer.add(li)
                k.dma("sync", qsr[:], qs_h[li:li + 1, :], "dli", writes=[("qsr",)],
                      full=True)
                k.dma("sync", ksr[:], ks_h[li:li + 1, :], "dli", writes=[("ksr",)],
                      full=True)
                k.dma("sync", b2r[:], b2_h[li:li + 1, :], "dli", writes=[("b2r",)],
                      full=True)
                k.emit("vector", lambda v: v.reduce_max(
                    out=nm[0:1, 0:1], in_=qsr[:], axis=AX.X,
                    apply_absolute_value=True), reads=[("qsr",)], writes=[("nm",)])
                k.emit("vector", lambda v: v.reduce_max(
                    out=nm[0:1, 1:2], in_=ksr[:], axis=AX.X,
                    apply_absolute_value=True), reads=[("ksr",)], writes=[("nm",)])
                k.emit("vector", lambda v: v.tensor_tensor(
                    out=nm[0:1, 2:3], in0=nm[0:1, 0:1], in1=nm[0:1, 1:2],
                    op=ALU.mult), reads=[("nm",)], writes=[("nm",)])
                k.emit("vector", lambda v: v.tensor_scalar(
                    out=nm[0:1, 3:4], in0=nm[0:1, 2:3], scalar1=-8.0, scalar2=None,
                    op0=ALU.mult), reads=[("nm",)], writes=[("nm",)])
                pnb = psb_f32(k, 7, 1)[:, 0:1]
                k.emit("tensor", lambda te: te.matmul(
                    out=pnb, lhsT=onesf[0:1, 0:128], rhs=nm[0:1, 3:4],
                    start=True, stop=True),
                    reads=[("nm",), ("ones",)], writes=psk(7, 1), inc=False)
                k.emit("vector", lambda v: v.tensor_copy(out=negc[:], in_=pnb),
                       reads=psk(7, 1), writes=[("negc",)])
                pq1 = psb_f32(k, 7, 1)
                k.emit("tensor", lambda te: te.matmul(
                    out=pq1[:, 0:64], lhsT=onesf[0:1, 0:128], rhs=qsr[:],
                    start=True, stop=True), reads=[("qsr",), ("ones",)],
                    writes=psk(7, 1), inc=False)
                k.emit("tensor", lambda te: te.matmul(
                    out=pq1[:, 64:128], lhsT=onesf[0:1, 0:128], rhs=ksr[:],
                    start=True, stop=True), reads=[("ksr",), ("ones",)],
                    writes=psk(7, 1), inc=False)
                k.emit("vector", lambda v: v.tensor_copy(
                    out=qsb[:], in_=pq1[:, 0:128]),
                    reads=psk(7, 1), writes=[("qsb",)])
                k.dma("sync", dtab[:], tbl_h, "dtb", writes=[("dtab",)])
                for base_off, col, rot in [(0, 0, False), (1024, 0, True),
                                           (2048, 64, False), (3072, 64, True)]:
                    dst3 = dtab[:, base_off:base_off + 1024].rearrange(
                        "p (t d) -> p t d", d=64)
                    if not rot:
                        fac = qsb[:, col:col + 64].unsqueeze(1).broadcast_to(
                            [128, 16, 64])
                        k.emit("vector", lambda v, o=dst3, i=dst3, f=fac:
                               v.tensor_tensor(out=o, in0=i, in1=f, op=ALU.mult),
                               reads=[("dtab",), ("qsb",)], writes=[("dtab",)])
                    else:
                        f1 = qsb[:, col + 32:col + 64].unsqueeze(1).broadcast_to(
                            [128, 16, 32])
                        f2 = qsb[:, col:col + 32].unsqueeze(1).broadcast_to(
                            [128, 16, 32])
                        k.emit("vector", lambda v, o=dst3[:, :, 0:32],
                               i=dst3[:, :, 0:32], f=f1:
                               v.tensor_tensor(out=o, in0=i, in1=f, op=ALU.mult),
                               reads=[("dtab",), ("qsb",)], writes=[("dtab",)])
                        k.emit("vector", lambda v, o=dst3[:, :, 32:64],
                               i=dst3[:, :, 32:64], f=f2:
                               v.tensor_tensor(out=o, in0=i, in1=f, op=ALU.mult),
                               reads=[("dtab",), ("qsb",)], writes=[("dtab",)])
                for vech, dstb in [(lng_h, gbc), (lnb_h, bbc)]:
                    k.dma("sync", xs2[0:1, 0:1024], vech[li:li + 1, :], "dli",
                          writes=[("xs2", 0)], full=True)
                    pgb = psb_f32(k, 7, 1)
                    for half in range(2):
                        k.emit("tensor", lambda te, o=pgb,
                               r=xs2[0:1, half * 512:(half + 1) * 512],
                               st=True:
                               te.matmul(out=o, lhsT=onesf[0:1, 0:128], rhs=r,
                                         start=True, stop=True),
                               reads=[("xs2", 0), ("ones",)], writes=psk(7, 1),
                               inc=False)
                        k.emit("vector", lambda v,
                               o=dstb[:, half * 512:(half + 1) * 512], i=pgb:
                               v.tensor_copy(out=o, in_=i),
                               reads=psk(7, 1),
                               writes=[("gbc",) if dstb is gbc else ("bbc",)])
                for hg in range(HG):
                    for rj, roff in enumerate([0, INNER, 2 * INNER]):
                        c0 = roff + hg * HPG * 64
                        loadw(wqkv_h[li, :, c0:c0 + HPG * 64],
                              cwqkv[li, :, c0:c0 + HPG * 64],
                              wbuf, rj * 4096, 1024, HPG * 64, frst and si == 0)
                    for qb in range(4):
                        for tl in range(4):
                            qkv_tile(s, 4 * qb + tl, hg)
                        for h in range(HPG):
                            attn_qbh(s, li, hg, qb, h)
                if debug and si == 0 and li == depth - 1:
                    k.dma("gpsimd", dbg["ao"], bigbuf[:, 0:NCH * 2048], "dbg",
                          reads=[("bb", p) for p in range(ceil_div(NCH * 2048, 1024))])
                    k.dma("gpsimd", dbg["negc"], negc[:], "dbg", reads=[("negc",)])
                loadw(wo_h[li, :, :], cwo[li, :, :], wbuf, 0, INNER, 1024,
                      frst and si == 0)
                for t in range(16):
                    outproj_tile(t)
                import os as _os
                hsd = int(_os.environ.get("HSD", "0"))
                for hs in range(2):
                    ffn_half(s, li, hs, frst and si == 0,
                             debug and si == 0 and li == depth - 1 and hs == hsd)
            for t in range(16):
                par = t % 2
                xr = scr[:, par * 1024:(par + 1) * 1024]
                k.dma("sync", xr, x_h[(tb + t) * 128:(tb + t + 1) * 128, :],
                      f"dx{par}", writes=[("oln", "scr", par)])
                yt = xs2[:, 0:1024]
                k.emit("vector", lambda v, o=yt, i0=xr,
                       i1=xin[:, t * 1024:(t + 1) * 1024]:
                       v.tensor_tensor(out=o, in0=i0, in1=i1, op=ALU.add),
                       reads=[("oln", "scr", par), ("xin", t)],
                       writes=[("xs2", 0)])
                k.dma("gpsimd", y_h[(tb + t) * 128:(tb + t + 1) * 128, :], yt,
                      "dy0", reads=[("xs2", 0)], writes=[("y", 0)])
            if debug and si == 0:
                k.dma("gpsimd", dbg["xin"], xin[:], "dbg",
                      reads=[("xin", t) for t in range(16)])
        k.emit("sync", lambda e: e.nop(), reads=[("y", 0)])
        nops = k.finalize(same_engine_sync=ses) if fuse is None else {}
    return nc, nops, dbg


def host_consts_mamba():
    kk = np.arange(128)[:, None]
    ll = np.arange(128)[None, :]
    UTF = (ll >= kk).astype(np.float32)
    MLF = np.where(ll < kk, -1e30, 0.0).astype(np.float32)
    IDNF = np.eye(128, dtype=np.float32)
    return UTF, MLF, IDNF


def build_l1_attn(debug=False, races=True, ses=True, seqs=(0, 1), nsb=4,
                  fuse=None):
    NH, NC, CH = 32, 17, 128
    if fuse is None:
        nc = bass.Bass("TRN2", target_bir_lowering=False,
                       detect_race_conditions=races)
        ctx = ExitStack()
        k = K(nc, ctx)
        outer = ctx
    else:
        nc, ctx, k = fuse["nc"], fuse["ctx"], fuse["k"]
        outer = nullcontext()

    def din(nm2, shape, dtype=F32):
        return nc.dram_tensor(k.pfx + nm2, shape, dtype,
                              kind="ExternalInput").ap()

    x_h = fuse["x"] if fuse else din("x", [4096, 1024])
    wiT_h = din("wiT", [1024, 4256])
    woT_h = din("woT", [2048, 1024])
    cwv_h = din("cwv", [128, 68])
    cb_h = din("cb", [1, 2176])
    inis_h = din("inis", [64, 2048])
    dtb_h = din("dtb", [1, 32])
    alog_h = din("alog", [1, 32])
    dd_h = din("dd", [1, 32])
    nw_h = din("nw", [1, 2048])
    idn_h = din("IDN", [128, 128], BF16)
    idnf_h = din("IDNF", [128, 128], F32)
    utf_h = din("UTF", [128, 128], F32)
    mlf_h = din("MLF", [128, 128], F32)
    y_h = fuse["y"] if fuse else nc.dram_tensor(
        "y", [4096, 1024], F32, kind="ExternalOutput").ap()
    cwi = nc.dram_tensor(k.pfx + "cwi", [1024, 4256], BF16,
                         kind="Internal").ap()
    cwo2 = nc.dram_tensor(k.pfx + "cwo2", [2048, 1024], BF16,
                          kind="Internal").ap()

    dbg = {}
    def dout(nm2, shape, dtype=F32):
        dbg[nm2] = nc.dram_tensor("dbg_" + nm2, shape, dtype,
                                  kind="ExternalOutput").ap()
    if debug:
        dout("xconv", [128, 17 * 516], BF16)
        dout("dt", [128, 4 * 32])
        dout("eacs", [128, 4 * 32])
        dout("hstate", [64, 2048])
        dout("ytm", [128, 4 * 2048], BF16)
        dout("lmat0", [128, 128], BF16)
        dout("yfm", [128, 16 * 512], BF16)

    with outer:
        ufm = k.sbuf("ufm", [128, 8 * 512], BF16)
        wsl = k.sbuf("wsl", [128, 8448], BF16)
        zsb = k.sbuf("zsb", [128, 4 * 2048], BF16)
        xbc = k.sbuf("xbc", [128, 17 * 516], BF16)
        xcv = k.sbuf("xcv", [128, 17 * 516], BF16)
        dcw = k.sbuf("dcw", [128, 68 * 128], BF16)
        idd = k.sbuf("idd", [128, 32 * 128], BF16)
        cwvc = k.sbuf("cwvc", [128, 68], F32)
        cbr = k.sbuf("cbr", [1, 2176], F32)
        dtt = k.sbuf("dtt", [128, 4 * 32], F32)
        adt = k.sbuf("adt", [128, 4 * 32], F32)
        eacs = k.sbuf("eacs", [128, 4 * 32], F32)
        acst = k.sbuf("acst", [128, 4 * 32], F32)
        nacst = k.sbuf("nacst", [128, 4 * 32], F32)
        lam = k.sbuf("lam", [1, 4 * 32], F32)
        lbc = k.sbuf("lbc", [128, 4 * 32], F32)
        nab = k.sbuf("nab", [128, 32], F32)
        dbc = k.sbuf("dbc", [128, 32], F32)
        hsb = k.sbuf("hsb", [64, 2048], F32)
        hbf = k.sbuf("hbf", [64, 2048], BF16)
        lmat = k.sbuf("lmat", [128, 2 * 128], BF16)
        mt = k.sbuf("mt", [128, 2 * 128], BF16)
        btm = k.sbuf("btm", [128, 64], BF16)
        ccp = k.sbuf("ccp", [64, 512], BF16)
        xtm = k.sbuf("xtm", [128, 2 * 64], BF16)
        xsc = k.sbuf("xsc", [128, 2 * 64], BF16)
        xs3 = k.sbuf("xs3", [128, 2 * 64], BF16)
        yrt = k.sbuf("yrt", [128, 2 * 64], BF16)
        ytm = k.sbuf("ytm", [128, 4 * 2048], BF16)
        yg = k.sbuf("yg", [128, 2048], BF16)
        ysq = k.sbuf("ysq", [128, 4], F32)
        yrs = k.sbuf("yrs", [128, 4], F32)
        yfm = k.sbuf("yfm", [128, 16 * 512], BF16)
        ostg = ytm[:, 0:8192].bitcast(F32)
        xres = k.sbuf("xres", [128, 1024], F32)
        rowbuf = k.sbuf("rowbuf", [1, 128], F32)
        deccol = k.sbuf("deccol", [128, 2], F32)
        idn = k.sbuf("idn", [128, 128], BF16)
        idnf = k.sbuf("idnf", [128, 128], F32)
        utf = k.sbuf("utf", [128, 128], F32)
        mlf = k.sbuf("mlf", [128, 128], F32)
        nwc = k.sbuf("nwc", [128, 16], F32)
        rows = k.sbuf("rows", [1, 2048], F32)
        r32 = k.sbuf("r32", [1, 4 * 32], F32)
        onesf = k.sbuf("onesf", [1, 512], F32)
        onescc = k.sbuf("onescc", [128, 1], F32)
        epsc = k.sbuf("epsc", [128, 1], F32)
        scr = k.sbuf("scr", [128, 2 * 1024], F32)
        xs2 = k.sbuf("xs2", [128, 1024], F32)
        sq2 = xs2
        xnscr = k.sbuf("xnscr", [128, 1024], BF16)
        sums = k.sbuf("sums", [128, 4], F32)
        sumsq = k.sbuf("sumsq", [128, 4], F32)
        negmean = k.sbuf("negmean", [128, 4], F32)
        rstd = k.sbuf("rstd", [128, 4], F32)
        if not hasattr(k, "PSB"):
            k.PSB = ctx.enter_context(
                nc.psum_tensor("PSB", [128, 4096], F32))

        for dst, srch, key in [(idn, idn_h, "idn"), (idnf, idnf_h, "idnf"),
                               (utf, utf_h, "utf"), (mlf, mlf_h, "mlf"),
                               (cwvc, cwv_h, "cwvc"), (rows, nw_h, "nwrow"),
                               (cbr, cb_h, "cbr"), (hsb, inis_h, "hsb")]:
            k.dma("sync", dst[:], srch, "dc", writes=[(key,)], full=True)
        k.dma("sync", r32[0:1, 0:32], dtb_h, "dc", writes=[("dtbr",)], full=True)
        k.dma("sync", r32[0:1, 32:64], alog_h, "dc", writes=[("alogr",)], full=True)
        k.dma("sync", r32[0:1, 64:96], dd_h, "dc", writes=[("ddr",)], full=True)
        import os as _osp
        for _pi in range(int(_osp.environ.get("VPAD", "0"))):
            k.emit("vector", lambda v: v.memset(onesf[:], 1.0),
                   writes=[("ones",)])
        k.emit("vector", lambda v: v.memset(onesf[:], 1.0), writes=[("ones",)])
        k.emit("vector", lambda v: v.memset(onescc[:], 1.0), writes=[("onescc",)])
        k.emit("vector", lambda v: v.memset(xcv[:], 0.0), writes=[("xcv",)])
        k.emit("vector", lambda v: v.memset(epsc[:], 1e-5), writes=[("epsc",)])
        k.emit("vector", lambda v: v.tensor_copy(out=hbf[:], in_=hsb[:]),
               reads=[("hsb",)], writes=[("hbf",)])
        k.emit("scalar", lambda s2: s2.activation(
            out=r32[0:1, 96:128], in_=r32[0:1, 32:64], func=AF.Exp),
            reads=[("alogr",)], writes=[("negar",)])
        k.emit("vector", lambda v: v.tensor_scalar(
            out=r32[0:1, 96:128], in0=r32[0:1, 96:128], scalar1=-1.0, scalar2=None,
            op0=ALU.mult), reads=[("negar",)], writes=[("negar",)])
        pb1 = psb_f32(k, 7, 1)
        k.emit("tensor", lambda te: te.matmul(
            out=pb1[:, 0:32], lhsT=onesf[0:1, 0:128], rhs=r32[0:1, 96:128],
            start=True, stop=True), reads=[("negar",), ("ones",)],
            writes=psk(7, 1), inc=False)
        k.emit("tensor", lambda te: te.matmul(
            out=pb1[:, 32:64], lhsT=onesf[0:1, 0:128], rhs=r32[0:1, 64:96],
            start=True, stop=True), reads=[("ddr",), ("ones",)],
            writes=psk(7, 1), inc=False)
        k.emit("vector", lambda v: v.tensor_copy(out=nab[:], in_=pb1[:, 0:32]),
               reads=psk(7, 1), writes=[("nab",)])
        k.emit("vector", lambda v: v.tensor_copy(out=dbc[:], in_=pb1[:, 32:64]),
               reads=psk(7, 1), writes=[("dbc",)])
        for c in range(16):
            pt3 = psb_f32(k, 7, 1)
            k.emit("tensor", lambda te, o=pt3[:, 0:1],
                   i=rows[0:1, c * 128:(c + 1) * 128]:
                   te.transpose(out=o, in_=i, identity=idnf[0:1, 0:1]),
                   reads=[("nwrow",), ("idnf",)], writes=psk(7, 1), inc=False)
            k.emit("vector", lambda v, o=nwc[:, c:c + 1], i=pt3[:, 0:1]:
                   v.tensor_copy(out=o, in_=i), reads=psk(7, 1), writes=[("nwc",)])
        for c in range(NC):
            for tp in range(4):
                j = c * 4 + tp
                wb2 = cwvc[:, j:j + 1].broadcast_to([128, 128])
                k.emit("gpsimd", lambda g, o=dcw[:, j * 128:(j + 1) * 128], i=wb2:
                       g.affine_select(out=o, in_=i, pattern=[[-1, 128]],
                                       compare_op=ALU.is_equal, fill=0.0,
                                       base=0, channel_multiplier=1),
                       reads=[("cwvc",)], writes=[("dcw",)])
        for h in range(NH):
            k.emit("vector", lambda v, o=idd[:, h * 128:(h + 1) * 128],
                   s2=dbc[:, h:h + 1]:
                   v.tensor_scalar(out=o, in0=idn[:], scalar1=s2, scalar2=None,
                                   op0=ALU.mult),
                   reads=[("idn",), ("dbc",)], writes=[("idd",)])

        def ssd_chunk(s, sb, cl):
            c32 = cl * 32
            tok0 = 4 + cl * 128
            pacs = psb_f32(k, 7, 1)[:, 0:32]
            k.emit("tensor", lambda te: te.matmul(
                out=pacs, lhsT=utf[:], rhs=adt[:, c32:c32 + 32],
                start=True, stop=True),
                reads=[("utf",), ("adt",)], writes=psk(7, 1), inc=False)
            k.emit("scalar", lambda s2: s2.activation(
                out=eacs[:, c32:c32 + 32], in_=pacs, func=AF.Exp),
                reads=psk(7, 1), writes=[("eacs",)])
            k.emit("vector", lambda v: v.tensor_copy(
                out=acst[:, c32:c32 + 32], in_=pacs),
                reads=psk(7, 1), writes=[("acst",)])
            k.emit("vector", lambda v: v.tensor_scalar(
                out=nacst[:, c32:c32 + 32], in0=pacs, scalar1=-1.0, scalar2=None,
                op0=ALU.mult), reads=psk(7, 1), writes=[("nacst",)])
            plam = psb_f32(k, 7, 1)[0:1, 64:96]
            k.emit("tensor", lambda te: te.matmul(
                out=plam, lhsT=onescc[:], rhs=adt[:, c32:c32 + 32],
                start=True, stop=True),
                reads=[("onescc",), ("adt",)], writes=psk(7, 1), inc=False)
            k.emit("vector", lambda v: v.tensor_copy(
                out=lam[0:1, c32:c32 + 32], in_=plam),
                reads=psk(7, 1), writes=[("lam",)])
            plb = psb_f32(k, 7, 1)[:, 96:128]
            k.emit("tensor", lambda te: te.matmul(
                out=plb, lhsT=onesf[0:1, 0:128], rhs=lam[0:1, c32:c32 + 32],
                start=True, stop=True),
                reads=[("lam",), ("ones",)], writes=psk(7, 1), inc=False)
            k.emit("scalar", lambda s2: s2.activation(
                out=lbc[:, c32:c32 + 32], in_=plb, func=AF.Exp),
                reads=psk(7, 1), writes=[("lbc",)])
            pS = psb_f32(k, 0, 1)[:, 0:128]
            bfmv = xcv[0:64, 16 * 516 + tok0: 16 * 516 + tok0 + 128]
            if cl == 0:
                k.emit("vector", lambda v: v.tensor_copy(
                    out=ccp[:], in_=xcv[64:128, 16 * 516 + 4: 16 * 516 + 516]),
                    reads=[("xcv",)], writes=[("ccp",)])
            cfmv = ccp[:, cl * 128:(cl + 1) * 128]
            k.emit("tensor", lambda te: te.matmul(
                out=pS, lhsT=bfmv, rhs=cfmv, start=True, stop=True),
                reads=[("ccp",), ("xcv",)], writes=psk(0, 1), inc=False)
            pbt = psb_bf16(k, 7, 1)
            k.emit("tensor", lambda te: te.transpose(
                out=pbt[:, 0:64], in_=bfmv, identity=idn[0:64, 0:64]),
                reads=[("xcv",), ("idn",)], writes=psk(7, 1), inc=False)
            k.emit("vector", lambda v: v.tensor_copy(out=btm[:], in_=pbt[:, 0:64]),
                   reads=psk(7, 1), writes=[("btm",)])
            for h in range(NH):
                par = h % 2
                pr = psb_f32(k, 7, 1)[0:1, 0:128]
                k.emit("tensor", lambda te, o=pr, i=acst[:, c32 + h:c32 + h + 1]:
                       te.transpose(out=o, in_=i, identity=idnf[:]),
                       reads=[("acst",), ("idnf",)], writes=psk(7, 1), inc=False)
                k.emit("vector", lambda v, i=pr: v.tensor_copy(out=rowbuf[:], in_=i),
                       reads=psk(7, 1), writes=[("rowb",)])
                pD = psb_f32(k, 1 + par * 5, 1)[:, 0:128]
                k.emit("tensor", lambda te, o=pD: te.matmul(
                    out=o, lhsT=onesf[0:1, 0:128], rhs=rowbuf[:],
                    start=True, stop=True),
                    reads=[("rowb",), ("ones",)], writes=psk(1 + par * 5, 1),
                    inc=False)
                k.emit("vector", lambda v, o=pD: v.tensor_tensor(
                    out=o, in0=o, in1=mlf[:], op=ALU.add),
                    reads=psk(1 + par * 5, 1) + [("mlf",)],
                    writes=psk(1 + par * 5, 1))
                k.emit("scalar", lambda s2, o=lmat[:, par * 128:(par + 1) * 128],
                       i=pD, b=nacst[:, c32 + h:c32 + h + 1]:
                       s2.activation(out=o, in_=i, func=AF.Exp, bias=b),
                       reads=psk(1 + par * 5, 1) + [("nacst",)],
                       writes=[("lmat", par)])
                if debug and h == 0 and cl == 3 and dbg:
                    k.dma("gpsimd", dbg["lmat0"], lmat[:, par * 128:(par + 1) * 128],
                          "dbg", reads=[("lmat", par)])
                k.emit("vector", lambda v, o=mt[:, par * 128:(par + 1) * 128],
                       i=pS, l2=lmat[:, par * 128:(par + 1) * 128]:
                       v.tensor_tensor(out=o, in0=i, in1=l2, op=ALU.mult),
                       reads=psk(0, 1) + [("lmat", par)], writes=[("mt", par)])
                ch2, ro = h // 2, 64 * (h % 2)
                xfmv = xcv[ro:ro + 64, ch2 * 516 + tok0: ch2 * 516 + tok0 + 128]
                pxt = psb_bf16(k, 7, 1)
                idv = idn[:, 0:64] if ro == 0 else idn[64:128, 64:128]
                idv = idn[0:64, 0:64] if ro == 0 else idn[64:128, 64:128]
                k.emit("tensor", lambda te, o=pxt[:, 0:64], i=xfmv, iv=idv:
                       te.transpose(out=o, in_=i, identity=iv),
                    reads=[("xcv",), ("idn",)], writes=psk(7, 1), inc=False)
                k.emit("vector", lambda v, o=xtm[:, par * 64:(par + 1) * 64],
                       i=pxt[:, 0:64]:
                       v.tensor_copy(out=o, in_=i),
                       reads=psk(7, 1), writes=[("xtm", par)])
                k.emit("vector", lambda v, o=xsc[:, par * 64:(par + 1) * 64],
                       i=xtm[:, par * 64:(par + 1) * 64],
                       s3=dtt[:, c32 + h:c32 + h + 1]:
                       v.tensor_scalar(out=o, in0=i, scalar1=s3, scalar2=None,
                                       op0=ALU.mult),
                       reads=[("xtm", par), ("dtt",)], writes=[("xsc", par)])
                k.emit("vector", lambda v, o=deccol[:, par:par + 1],
                       i=lmat[:, par * 128 + 127: par * 128 + 128]:
                       v.tensor_copy(out=o, in_=i),
                       reads=[("lmat", par)], writes=[("dec", par)])
                k.emit("vector", lambda v, o=xs3[:, par * 64:(par + 1) * 64],
                       i=xsc[:, par * 64:(par + 1) * 64],
                       s3=deccol[:, par:par + 1]:
                       v.tensor_scalar(out=o, in0=i, scalar1=s3, scalar2=None,
                                       op0=ALU.mult),
                       reads=[("xsc", par), ("dec", par)], writes=[("xs3", par)])
                yb = 2 + h // 8
                pY = psb_f32(k, yb, 1)[:, (h % 8) * 64:(h % 8) * 64 + 64]
                k.emit("tensor", lambda te, o=pY,
                       l=mt[:, par * 128:(par + 1) * 128],
                       r=xsc[:, par * 64:(par + 1) * 64]:
                       te.matmul(out=o, lhsT=l, rhs=r, start=True, stop=False),
                       reads=[("mt", par), ("xsc", par)], writes=[("PS", yb, h % 8)],
                       inc=False)
                k.emit("tensor", lambda te, o=pY, l=idd[:, h * 128:(h + 1) * 128],
                       r=xtm[:, par * 64:(par + 1) * 64]:
                       te.matmul(out=o, lhsT=l, rhs=r, start=False, stop=True),
                       reads=[("idd",), ("xtm", par)], writes=[("PS", yb, h % 8)],
                       inc=False)
                pYr = psb_f32(k, 1 + (1 - par) * 5, 1)[:, 0:64]
                k.emit("tensor", lambda te, o=pYr,
                       l=cfmv, r=hbf[:, h * 64:(h + 1) * 64]:
                       te.matmul(out=o, lhsT=l, rhs=r, start=True, stop=True),
                       reads=[("ccp",), ("hbf",)],
                       writes=psk(1 + (1 - par) * 5, 1), inc=False)
                k.emit("vector", lambda v, o=yrt[:, par * 64:(par + 1) * 64],
                       i=pYr, s3=eacs[:, c32 + h:c32 + h + 1]:
                       v.tensor_scalar(out=o, in0=i, scalar1=s3, scalar2=None,
                                       op0=ALU.mult),
                       reads=psk(1 + (1 - par) * 5, 1) + [("eacs",)],
                       writes=[("yrt", par)])
                oy = ytm[:, cl * 2048 + h * 64: cl * 2048 + (h + 1) * 64]
                k.emit("vector", lambda v, o=oy, i0=yrt[:, par * 64:(par + 1) * 64],
                       i1=pY:
                       v.tensor_tensor(out=o, in0=i0, in1=i1, op=ALU.add),
                       reads=[("yrt", par), ("PS", yb, h % 8)], writes=[("ytm",)])
                pL = psb_f32(k, 7, 1)[0:64, 128 + (h % 4) * 64:128 + (h % 4) * 64 + 64]
                k.emit("tensor", lambda te, o=pL, l=btm[:],
                       r=xs3[:, par * 64:(par + 1) * 64]:
                       te.matmul(out=o, lhsT=l, rhs=r, start=True, stop=True),
                       reads=[("btm",), ("xs3", par)], writes=[("PS", 7, "L", h % 4)],
                       inc=False)
                k.emit("vector", lambda v, o=hsb[:, h * 64:(h + 1) * 64],
                       s3=lbc[0:64, c32 + h:c32 + h + 1], i1=pL:
                       v.scalar_tensor_tensor(out=o, in0=o, scalar=s3, in1=i1,
                                              op0=ALU.mult, op1=ALU.add),
                       reads=[("PS", 7, "L", h % 4), ("hsb",), ("lbc",)],
                       writes=[("hsb",)])
                k.emit("vector", lambda v, o=hbf[:, h * 64:(h + 1) * 64],
                       i=hsb[:, h * 64:(h + 1) * 64]:
                       v.tensor_copy(out=o, in_=i),
                       reads=[("hsb",)], writes=[("hbf",)])

        def mamba_tail(s, sb, first_wo, tb):
            for tl in range(4):
                zt = zsb[:, tl * 2048:(tl + 1) * 2048]
                yt = ytm[:, tl * 2048:(tl + 1) * 2048]
                k.emit("scalar", lambda s2, o=yg[:], i=zt:
                       s2.activation(out=o, in_=i, func=AF.Silu),
                       reads=[("zsb",)], writes=[("yg",)])
                k.emit("vector", lambda v, o=yt, i=yt:
                       v.tensor_tensor(out=o, in0=i, in1=yg[:], op=ALU.mult),
                       reads=[("ytm",), ("yg",)], writes=[("ytm",)])
                k.emit("vector", lambda v, i=yt, a=ysq[:, tl:tl + 1]:
                       v.scalar_tensor_tensor(out=yg[:], in0=i, scalar=1.0, in1=i,
                                              op0=ALU.mult, op1=ALU.mult,
                                              accum_out=a),
                       reads=[("ytm",)], writes=[("yg",), ("ysq",)])
            k.emit("vector", lambda v: v.tensor_scalar(
                out=yrs[:], in0=ysq[:], scalar1=1.0 / 2048, scalar2=None,
                op0=ALU.mult), reads=[("ysq",)], writes=[("yrs",)])
            k.emit("scalar", lambda s2: s2.activation(
                out=ysq[:], in_=yrs[:], func=AF.Sqrt, bias=epsc[:, 0:1]),
                reads=[("yrs",), ("epsc",)], writes=[("ysq",)])
            k.emit("vector", lambda v: v.reciprocal(out=yrs[:], in_=ysq[:]),
                   reads=[("ysq",)], writes=[("yrs",)])
            for tl in range(4):
                par = tl % 2
                yt = ytm[:, tl * 2048:(tl + 1) * 2048]
                k.emit("vector", lambda v, i=yt, s3=yrs[:, tl:tl + 1]:
                       v.tensor_scalar(out=yg[:], in0=i, scalar1=s3, scalar2=None,
                                       op0=ALU.mult),
                       reads=[("ytm",), ("yrs",)], writes=[("yg",)])
                ptr = psb_bf16(k, 6 + par, 1)
                for c in range(16):
                    k.emit("tensor", lambda te, o=ptr[:, (c % 8) * 128:(c % 8) * 128 + 128],
                           i=yg[:, c * 128:(c + 1) * 128]:
                           te.transpose(out=o, in_=i, identity=idn[:]),
                           reads=[("yg",), ("idn",)], writes=psk(6 + par, 1),
                           inc=False)
                    k.emit("vector", lambda v,
                           o=yfm[:, c * 512 + tl * 128: c * 512 + (tl + 1) * 128],
                           i=ptr[:, (c % 8) * 128:(c % 8) * 128 + 128],
                           s3=nwc[:, c:c + 1]:
                           v.tensor_scalar(out=o, in0=i, scalar1=s3, scalar2=None,
                                           op0=ALU.mult),
                           reads=psk(6 + par, 1) + [("nwc",)], writes=[("yfm",)])
            for nco in range(4):
                par = nco % 2
                wv3 = wsl[:, par * 4096:].rearrange("p (c n) -> p c n", n=256)[
                    :, 0:16, :]
                if first_wo:
                    sc3 = scr[:, par * 1024:].rearrange("p (c n) -> p c n", n=256)
                    for g4 in range(4):
                        k.dma("sync", sc3[:, 0:4, :],
                              woT_h[g4 * 512:(g4 + 1) * 512,
                                    nco * 256:(nco + 1) * 256].rearrange(
                                  "(c p) n -> p c n", p=128),
                              f"dw{par}", writes=[("scr", par)])
                        k.emit("vector", lambda v,
                               o=wsl[:, par * 4096 + g4 * 1024:
                                     par * 4096 + (g4 + 1) * 1024],
                               i=scr[:, par * 1024: par * 1024 + 1024]:
                               v.tensor_copy(out=o, in_=i),
                               reads=[("scr", par)], writes=[("wsl", par)])
                    k.dma("gpsimd", cwo2[:, nco * 256:(nco + 1) * 256].rearrange(
                        "(c p) n -> p c n", p=128), wv3, f"cw{par}",
                        reads=[("wsl", par)])
                else:
                    k.dma("sync", wv3, cwo2[:, nco * 256:(nco + 1) * 256].rearrange(
                        "(c p) n -> p c n", p=128), f"dw{par}",
                        writes=[("wsl", par)])
                for tl in range(4):
                    po = psb_f32(k, tl % 2, 1)[:, 0:256]
                    for ec in range(16):
                        k.emit("tensor", lambda te, o=po,
                               l=yfm[:, ec * 512 + tl * 128: ec * 512 + (tl + 1) * 128],
                               r=wv3[:, ec, :], st=(ec == 0), sp=(ec == 15):
                               te.matmul(out=o, lhsT=l, rhs=r, start=st, stop=sp),
                               reads=[("yfm",), ("wsl", par)],
                               writes=psk(tl % 2, 1), inc=False)
                    k.emit("vector", lambda v,
                           o=ostg[:, tl * 1024 + nco * 256:
                                  tl * 1024 + (nco + 1) * 256],
                           i=po: v.tensor_copy(out=o, in_=i),
                           reads=psk(tl % 2, 1), writes=[("ytm",)])
            for tl in range(4):
                t = sb * 4 + tl
                k.dma("sync", xres[:], x_h[(tb + t) * 128:(tb + t + 1) * 128, :],
                      "dxr", writes=[("xres",)])
                k.emit("vector", lambda v,
                       o=ostg[:, tl * 1024:(tl + 1) * 1024]:
                       v.tensor_tensor(out=o, in0=o, in1=xres[:], op=ALU.add),
                       reads=[("ytm",), ("xres",)], writes=[("ytm",)])
                k.dma("gpsimd", y_h[(tb + t) * 128:(tb + t + 1) * 128, :],
                      ostg[:, tl * 1024:(tl + 1) * 1024], "dy0",
                      reads=[("ytm",)], writes=[("y", 0)])

        first = [True]
        for si, s in enumerate(seqs):
            tb = 16 * s
            if si > 0:
                k.dma("sync", hsb[:], inis_h, "dc2", writes=[("hsb",)])
                k.emit("vector", lambda v: v.tensor_copy(out=hbf[:], in_=hsb[:]),
                       reads=[("hsb",)], writes=[("hbf",)])
            for sb in range(nsb):
                t0 = sb * 4
                tag = "mln"
                k.emit("vector", lambda v: v.memset(sumsq[:], 0.0),
                       writes=[(tag, "sumsq")])
                for tl in range(4):
                    t = t0 + tl
                    par = tl % 2
                    xt = scr[:, par * 1024:(par + 1) * 1024]
                    k.dma("sync", xt, x_h[(tb + t) * 128:(tb + t + 1) * 128, :],
                          f"dx{par}", writes=[(tag, "scr", par)])
                    k.emit("vector", lambda v, o=sums[:, tl:tl + 1], i=xt:
                           v.tensor_reduce(out=o, in_=i, axis=AX.X, op=ALU.add),
                           reads=[(tag, "scr", par)], writes=[(tag, "sums")])
                    k.emit("scalar", lambda s2, i=xt, a=sumsq[:, tl:tl + 1]:
                           s2.activation(out=sq2[:], in_=i, func=AF.Square,
                                         accum_out=a),
                           reads=[(tag, "scr", par)],
                           writes=[(tag, "sumsq"), ("sq2",)])
                k.emit("vector", lambda v: v.tensor_scalar(
                    out=negmean[:], in0=sums[:], scalar1=-1.0 / 1024, scalar2=None,
                    op0=ALU.mult), reads=[(tag, "sums")], writes=[(tag, "negmean")])
                k.emit("vector", lambda v: v.tensor_scalar(
                    out=rstd[:], in0=sumsq[:], scalar1=1.0 / 1024, scalar2=None,
                    op0=ALU.mult), reads=[(tag, "sumsq")], writes=[(tag, "rstd")])
                k.emit("vector", lambda v: v.scalar_tensor_tensor(
                    out=sums[:], in0=negmean[:], scalar=1.0, in1=negmean[:],
                    op0=ALU.mult, op1=ALU.mult),
                    reads=[(tag, "negmean")], writes=[(tag, "sums")])
                k.emit("vector", lambda v: v.tensor_tensor(
                    out=rstd[:], in0=rstd[:], in1=sums[:], op=ALU.subtract),
                    reads=[(tag, "sums"), (tag, "rstd")], writes=[(tag, "rstd")])
                k.emit("scalar", lambda s2: s2.activation(
                    out=sumsq[:], in_=rstd[:], func=AF.Sqrt, bias=epsc[:, 0:1]),
                    reads=[(tag, "rstd"), ("epsc",)], writes=[(tag, "sumsq")])
                k.emit("vector", lambda v: v.reciprocal(out=rstd[:], in_=sumsq[:]),
                       reads=[(tag, "sumsq")], writes=[(tag, "rstd")])
                for tl in range(4):
                    t = t0 + tl
                    par = tl % 2
                    xt = scr[:, par * 1024:(par + 1) * 1024]
                    k.dma("sync", xt, x_h[(tb + t) * 128:(tb + t + 1) * 128, :],
                          f"dx{par}", writes=[(tag, "scr", par)])
                    k.emit("vector", lambda v, i=xt, m=negmean[:, tl:tl + 1],
                           r=rstd[:, tl:tl + 1]:
                           v.tensor_scalar(out=xnscr[:], in0=i, scalar1=m,
                                           scalar2=r, op0=ALU.add, op1=ALU.mult),
                           reads=[(tag, "scr", par), (tag, "negmean"),
                                  (tag, "rstd")],
                           writes=[("xn",)])
                    ptr = psb_bf16(k, 6 + par, 1)
                    for c in range(8):
                        k.emit("tensor", lambda te, o=ptr[:, c * 128:(c + 1) * 128],
                               i=xnscr[:, c * 128:(c + 1) * 128]:
                               te.transpose(out=o, in_=i, identity=idn[:]),
                               reads=[("xn",), ("idn",)], writes=psk(6 + par, 1),
                               inc=False)
                    dst = ufm[:, :].rearrange("p (c n) -> p c n", c=8)[
                        :, :, tl * 128:(tl + 1) * 128]
                    k.emit("vector", lambda v, o=dst,
                           i=ptr.rearrange("p (c n) -> p c n", c=8):
                           v.tensor_copy(out=o, in_=i),
                           reads=psk(6 + par, 1), writes=[("ufm",)])
                frst = first[0]
                for zc in range(4):
                    par = zc % 2
                    wv = wsl[:, par * 4096: par * 4096 + 4096]
                    if frst:
                        sc = scr[:, par * 1024:(par + 1) * 1024]
                        for cc in range(8):
                            k.dma("sync", sc[:, 0:512],
                                  wiT_h[cc * 128:(cc + 1) * 128,
                                        zc * 512:(zc + 1) * 512],
                                  f"dw{par}", writes=[("scr", par)])
                            k.emit("gpsimd", lambda g,
                                   o=wv[:, cc * 512:(cc + 1) * 512],
                                   i=sc[:, 0:512]: g.tensor_copy(out=o, in_=i),
                                   reads=[("scr", par)], writes=[("wsl", par)])
                            k.dma("gpsimd", cwi[cc * 128:(cc + 1) * 128,
                                                zc * 512:(zc + 1) * 512],
                                  wv[:, cc * 512:(cc + 1) * 512], f"cw{par}",
                                  reads=[("wsl", par)])
                    else:
                        k.dma("sync", wv.rearrange("p (c n) -> p c n", n=512),
                              cwi[:, zc * 512:(zc + 1) * 512].rearrange(
                                  "(c p) n -> p c n", p=128),
                              f"dw{par}", writes=[("wsl", par)])
                    for tl in range(4):
                        pz = psb_f32(k, (tl % 2) * 2, 1)
                        for cc in range(8):
                            k.emit("tensor", lambda te, o=pz,
                                   l=ufm[:, cc * 512 + tl * 128:
                                         cc * 512 + (tl + 1) * 128],
                                   r=wv[:, cc * 512:(cc + 1) * 512], st=(cc == 0),
                                   sp=(cc == 7):
                                   te.matmul(out=o, lhsT=l, rhs=r, start=st,
                                             stop=sp),
                                   reads=[("ufm",), ("wsl", par)],
                                   writes=psk((tl % 2) * 2, 1), inc=False)
                        k.emit("scalar", lambda s2,
                               o=zsb[:, tl * 2048 + zc * 512:
                                     tl * 2048 + (zc + 1) * 512],
                               i=pz: s2.copy(out=o, in_=i),
                               reads=psk((tl % 2) * 2, 1), writes=[("zsb",)])
                for ec in range(NC):
                    par = ec % 2
                    e0 = 2048 + ec * 128
                    wv = wsl[:, par * 4096: par * 4096 + 1024]
                    if frst:
                        sc = scr[:, par * 1024:(par + 1) * 1024]
                        k.dma("sync", sc.rearrange("p (c n) -> p c n", n=128),
                              wiT_h[:, e0:e0 + 128].rearrange(
                                  "(c p) n -> p c n", p=128),
                              f"dw{par}", writes=[("scr", par)])
                        k.emit("gpsimd", lambda g, o=wv, i=sc:
                               g.tensor_copy(out=o, in_=i),
                               reads=[("scr", par)], writes=[("wsl", par)])
                        k.dma("gpsimd", cwi[:, e0:e0 + 128].rearrange(
                            "(c p) n -> p c n", p=128),
                            wv.rearrange("p (c n) -> p c n", n=128), f"cw{par}",
                            reads=[("wsl", par)])
                    else:
                        k.dma("sync", wv.rearrange("p (c n) -> p c n", n=128),
                              cwi[:, e0:e0 + 128].rearrange(
                                  "(c p) n -> p c n", p=128),
                              f"dw{par}", writes=[("wsl", par)])
                    pxb = psb_f32(k, 4 + (ec % 2), 1)
                    for cc in range(8):
                        k.emit("tensor", lambda te, o=pxb,
                               l=wv[:, cc * 128:(cc + 1) * 128],
                               r=ufm[:, cc * 512:(cc + 1) * 512], st=(cc == 0),
                               sp=(cc == 7):
                               te.matmul(out=o, lhsT=l, rhs=r, start=st, stop=sp),
                               reads=[("ufm",), ("wsl", par)],
                               writes=psk(4 + (ec % 2), 1), inc=False)
                    k.emit("scalar", lambda s2,
                           o=xbc[:, ec * 516 + 4: ec * 516 + 516], i=pxb:
                           s2.copy(out=o, in_=i),
                           reads=psk(4 + (ec % 2), 1), writes=[("xbc",)])
                if frst:
                    sc = scr[:, 0:256]
                    k.dma("sync", sc.rearrange("p (c n) -> p c n", n=32),
                          wiT_h[:, 4224:4256].rearrange("(c p) n -> p c n", p=128),
                          "dwd", writes=[("scr", 0)])
                    k.emit("gpsimd", lambda g: g.tensor_copy(
                        out=wsl[:, 8192:8448], in_=scr[:, 0:256]),
                        reads=[("scr", 0)], writes=[("wdt",)])
                    k.dma("gpsimd", cwi[:, 4224:4256].rearrange(
                        "(c p) n -> p c n", p=128),
                        wsl[:, 8192:8448].rearrange("p (c n) -> p c n", n=32),
                        "cwd", reads=[("wdt",)])
                    first[0] = False
                else:
                    k.dma("sync",
                          wsl[:, 8192:8448].rearrange("p (c n) -> p c n", n=32),
                          cwi[:, 4224:4256].rearrange("(c p) n -> p c n", p=128),
                          "dwd", writes=[("wdt",)])
                for tl in range(4):
                    pdt = psb_f32(k, 6, 1)[:, 0:32]
                    for cc in range(8):
                        k.emit("tensor", lambda te, o=pdt,
                               l=ufm[:, cc * 512 + tl * 128:
                                     cc * 512 + (tl + 1) * 128],
                               r=wsl[:, 8192 + cc * 32: 8192 + (cc + 1) * 32],
                               st=(cc == 0):
                               te.matmul(out=o, lhsT=l, rhs=r, start=st,
                                         stop=False),
                               reads=[("ufm",), ("wdt",)], writes=psk(6, 1),
                               inc=False)
                    k.emit("tensor", lambda te, o=pdt: te.matmul(
                        out=o, lhsT=r32[0:1, 0:32], rhs=onesf[0:1, 0:1],
                        start=False, stop=True) if False else te.matmul(
                        out=o, lhsT=onesf[0:1, 0:128], rhs=r32[0:1, 0:32],
                        start=False, stop=True),
                        reads=[("dtbr",), ("ones",)], writes=psk(6, 1), inc=False)
                    k.emit("scalar", lambda s2, i=pdt:
                           s2.activation(out=sq2[:, 0:32], in_=i, func=AF.Abs),
                           reads=psk(6, 1), writes=[("sp1",)])
                    k.emit("scalar", lambda s2: s2.activation(
                        out=sq2[:, 32:64], in_=sq2[:, 0:32], func=AF.Exp,
                        scale=-1.0), reads=[("sp1",)], writes=[("sp1",)])
                    k.emit("scalar", lambda s2: s2.activation(
                        out=sq2[:, 64:96], in_=sq2[:, 32:64], func=AF.Ln,
                        bias=onescc[:, 0:1]),
                        reads=[("sp1",), ("onescc",)], writes=[("sp1",)])
                    k.emit("vector", lambda v, i=pdt: v.tensor_scalar_max(
                        out=sq2[:, 96:128], in0=i, scalar1=0.0),
                        reads=psk(6, 1), writes=[("sp2",)])
                    k.emit("vector", lambda v, o=dtt[:, tl * 32:(tl + 1) * 32]:
                           v.tensor_tensor(out=o, in0=sq2[:, 96:128],
                                           in1=sq2[:, 64:96], op=ALU.add),
                           reads=[("sp1",), ("sp2",)], writes=[("dtt",)])
                k.emit("vector", lambda v: v.tensor_tensor(
                    out=adt[:, :].rearrange("p (t h) -> p t h", h=32),
                    in0=dtt[:, :].rearrange("p (t h) -> p t h", h=32),
                    in1=nab[:, :].unsqueeze(1).broadcast_to([128, 4, 32]),
                    op=ALU.mult), reads=[("dtt",), ("nab",)], writes=[("adt",)])
                if sb == 0:
                    k.emit("vector", lambda v: v.memset(
                        xbc.rearrange("p (c n) -> p c n", n=516)[:, :, 0:4], 0.0),
                        writes=[("xbc",)])
                for ec in range(NC):
                    par = ec % 2
                    pcv = psb_f32(k, 4 + par, 1)
                    for tp in range(4):
                        k.emit("tensor", lambda te, o=pcv,
                               l=dcw[:, (ec * 4 + tp) * 128:
                                     (ec * 4 + tp + 1) * 128],
                               r=xbc[:, ec * 516 + tp + 1: ec * 516 + tp + 513],
                               st=(tp == 0):
                               te.matmul(out=o, lhsT=l, rhs=r, start=st,
                                         stop=False),
                               reads=[("dcw",), ("xbc",)],
                               writes=psk(4 + par, 1), inc=False)
                    k.emit("tensor", lambda te, o=pcv,
                           l=cbr[0:1, ec * 128:(ec + 1) * 128]:
                           te.matmul(out=o, lhsT=l, rhs=onesf[0:1, 0:512],
                                     start=False, stop=True),
                           reads=[("cbr",), ("ones",)], writes=psk(4 + par, 1),
                           inc=False)
                    k.emit("scalar", lambda s2,
                           o=xcv[:, ec * 516 + 4: ec * 516 + 516], i=pcv:
                           s2.activation(out=o, in_=i, func=AF.Silu),
                           reads=psk(4 + par, 1), writes=[("xcv",)])
                k.emit("vector", lambda v: v.tensor_copy(
                    out=xbc.rearrange("p (c n) -> p c n", n=516)[:, :, 0:4],
                    in_=xbc.rearrange("p (c n) -> p c n", n=516)[:, :, 512:516]),
                    reads=[("xbc",)], writes=[("xbc",)])
                for cl in range(4):
                    ssd_chunk(s, sb, cl)
                mamba_tail(s, sb, si == 0 and sb == 0, tb)
                if debug and si == len(seqs) - 1 and sb == nsb - 1:
                    k.dma("gpsimd", dbg["xconv"], xcv[:], "dbg", reads=[("xcv",)])
                    k.dma("gpsimd", dbg["dt"], dtt[:], "dbg", reads=[("dtt",)])
                    k.dma("gpsimd", dbg["eacs"], eacs[:], "dbg", reads=[("eacs",)])
                    k.dma("gpsimd", dbg["hstate"], hsb[:], "dbg", reads=[("hsb",)])
                    k.dma("gpsimd", dbg["ytm"], ytm[:], "dbg", reads=[("ytm",)])
                    k.dma("gpsimd", dbg["yfm"], yfm[:], "dbg", reads=[("yfm",)])
        k.emit("sync", lambda e: e.nop(), reads=[("y", 0)])
        nops = k.finalize(same_engine_sync=ses) if fuse is None else {}
    return nc, nops, dbg


def build_full(races=False, blocks=("a0", "f0", "a1", "f1")):
    nc = bass.Bass("TRN2", target_bir_lowering=False,
                   detect_race_conditions=races)
    ctx = ExitStack()
    k = K(nc, ctx)
    with ctx:
        pool = ctx.enter_context(nc.sbuf_tensor("pool", [128, 53200], F32))
        k.pool = pool
        k.pool_cols = 53200
        k.PSB = ctx.enter_context(nc.psum_tensor("PSB", [128, 4096], F32))
        x_h = nc.dram_tensor("x", [4096, 1024], F32,
                             kind="ExternalInput").ap()
        y_h = nc.dram_tensor("y", [4096, 1024], F32,
                             kind="ExternalOutput").ap()
        hand = {}
        for i, b in enumerate(blocks):
            hand[b] = (x_h if i == 0 else hand[blocks[i - 1]][1],
                       y_h if i == len(blocks) - 1 else
                       nc.dram_tensor(f"h{i}b", [4096, 1024], F32,
                                      kind="Internal").ap())
        import os as _osf
        first = True
        for b in blocks:
            if not first:
                if "NOBAR" not in _osf.environ:
                    k.barrier()
            first = False
            if "NORST" in _osf.environ:
                k.pfx = b + "_"
            else:
                k.arena_reset(b + "_")
            fz = dict(nc=nc, ctx=ctx, k=k, x=hand[b][0], y=hand[b][1])
            if b == "a0":
                build_l0_attn(debug=False, races=races, fuse=fz)
            elif b == "f0":
                build_inner_txf(CFG_L0FFN, debug=False, races=races, fuse=fz)
            elif b == "a1":
                build_l1_attn(debug=False, races=races, fuse=fz)
            elif b == "f1":
                build_inner_txf(CFG_L1FFN, debug=False, races=races, fuse=fz)
            elif b == "cp":
                cbuf = k.sbuf("cbuf", [128, 2 * 1024], F32)
                for t in range(32):
                    par = t % 2
                    sl = cbuf[:, par * 1024:(par + 1) * 1024]
                    k.dma("sync", sl, fz["x"][t * 128:(t + 1) * 128, :],
                          f"ci{par}", writes=[("cp", par)])
                    k.dma("gpsimd", fz["y"][t * 128:(t + 1) * 128, :], sl,
                          f"co{par}", reads=[("cp", par)], writes=[("y", 0)])
                k.emit("sync", lambda e: e.nop(), reads=[("y", 0)])
        nops = k.finalize(same_engine_sync=True)
    return nc, nops



def kernel(**inputs):
    from concourse.bass_utils import run_bass_kernel_spmd
    ins = inputs
    x = np.ascontiguousarray(np.asarray(ins["L0_attn_x"], dtype=np.float32))
    idn_bf = np.eye(128, dtype=np.float32).astype(ml_dtypes.bfloat16)
    UTF, MLF, IDNF = host_consts_mamba()
    TBL0, MASK0 = host_consts_inner(CFG_L0FFN["W"])
    TBL1, MASK1 = host_consts_inner(CFG_L1FFN["W"])
    wi = ins["L1_attn_in_proj_w"]
    wo3 = ins["L1_attn_out_proj_w"]
    cw = ins["L1_attn_conv_w"]
    base = {
        "a0_Wq": ins["L0_attn_Wq"], "a0_bq": ins["L0_attn_bq"].reshape(1, -1),
        "a0_Wl": ins["L0_attn_Wl"], "a0_bl": ins["L0_attn_bl"].reshape(1, -1),
        "a0_Wk": ins["L0_attn_Wk"], "a0_bk": ins["L0_attn_bk"].reshape(1, -1),
        "a0_Wv": ins["L0_attn_Wv"], "a0_bv": ins["L0_attn_bv"].reshape(1, -1),
        "a0_Wo": ins["L0_attn_Wo"], "a0_bo": ins["L0_attn_bo"].reshape(1, -1),
        "a0_IDN": idn_bf,
        "f0_wqkv": ins["L0_ffn_Wqkv"], "f0_qs": ins["L0_ffn_q_scale"],
        "f0_ks": ins["L0_ffn_k_scale"], "f0_wo": ins["L0_ffn_Wo"],
        "f0_lng": ins["L0_ffn_ln_g"], "f0_lnb": ins["L0_ffn_ln_b"],
        "f0_w1": ins["L0_ffn_W1"], "f0_b1": ins["L0_ffn_b1"],
        "f0_w2": ins["L0_ffn_W2"], "f0_b2": ins["L0_ffn_b2"],
        "f0_IDN": idn_bf, "f0_TBL": TBL0, "f0_MASK": MASK0,
        "a1_wiT": np.ascontiguousarray(wi.T),
        "a1_woT": np.ascontiguousarray(wo3.T),
        "a1_cwv": np.ascontiguousarray(
            cw.reshape(17, 128, 4).transpose(1, 0, 2).reshape(128, 68)),
        "a1_cb": ins["L1_attn_conv_b"].reshape(1, -1),
        "a1_inis": np.ascontiguousarray(
            ins["L1_attn_init_states"].transpose(2, 0, 1).reshape(64, 2048)),
        "a1_dtb": ins["L1_attn_dt_bias"].reshape(1, -1),
        "a1_alog": ins["L1_attn_A_log"].reshape(1, -1),
        "a1_dd": ins["L1_attn_D"].reshape(1, -1),
        "a1_nw": ins["L1_attn_norm_w"].reshape(1, -1),
        "a1_IDN": idn_bf, "a1_IDNF": IDNF, "a1_UTF": UTF, "a1_MLF": MLF,
        "f1_wqkv": ins["L1_ffn_wqkv"], "f1_qs": ins["L1_ffn_q_scale"],
        "f1_ks": ins["L1_ffn_k_scale"], "f1_wo": ins["L1_ffn_wo"],
        "f1_lng": ins["L1_ffn_ln_g"], "f1_lnb": ins["L1_ffn_ln_b"],
        "f1_w1": ins["L1_ffn_w1"], "f1_b1": ins["L1_ffn_b1"],
        "f1_w2": ins["L1_ffn_w2"], "f1_b2": ins["L1_ffn_b2"],
        "f1_IDN": idn_bf, "f1_TBL": TBL1, "f1_MASK": MASK1,
    }
    cores = list(range(8))
    nc, _ = build_full()
    maps = [dict(base, x=np.ascontiguousarray(
        x[2 * c:2 * c + 2].reshape(4096, 1024))) for c in cores]
    res = run_bass_kernel_spmd(nc, maps, core_ids=cores)
    ys = [np.asarray(res.results[c]["y"]).reshape(2, 2048, 1024)
          for c in cores]
    return np.stack(ys).reshape(16, 2048, 1024).astype(np.float32)


# version 04
# speedup vs baseline: 1.0060x; 1.0060x over previous
import numpy as np
import ml_dtypes
import concourse.bass as bass
import concourse.mybir as mybir
from contextlib import ExitStack, nullcontext

F32 = mybir.dt.float32
BF16 = mybir.dt.bfloat16
AF = mybir.ActivationFunctionType
ALU = mybir.AluOpType
AX = mybir.AxisListType

ENGINES = ("sync", "scalar", "vector", "gpsimd", "tensor")


def ceil_div(a, b):
    return (a + b - 1) // b

bfc = lambda x: np.asarray(x, dtype=np.float32).astype(ml_dtypes.bfloat16)


class Op:
    __slots__ = ("eng", "fn", "reads", "writes", "inc", "idx", "dma_sem", "deps",
                 "_dbg_waits", "_dbg_incs", "dsnap")

    def __init__(self, eng, fn, reads, writes, inc, dma_sem=None):
        self.eng, self.fn = eng, fn
        self.reads, self.writes = list(reads), list(writes)
        self.inc, self.dma_sem = inc, dma_sem
        self.idx = None
        self.deps = []


class K:
    def __init__(self, nc, ctx):
        self.nc, self.ctx = nc, ctx
        self.ops = []
        self.last_w = {}
        self.readers = {}
        self.dma_cnt = {}
        self.full_streams = set()
        self.pfx = ""
        self.pool = None
        self.pool_cols = 0
        self.pool_off = 0

    def arena_reset(self, pfx):
        self.pfx = pfx
        self.pool_off = 0

    def barrier(self):
        snap = {}
        for op in reversed(self.ops):
            if op.eng not in snap:
                snap[op.eng] = op
            if len(snap) == len(ENGINES):
                break
        self._fence = {"snap": snap, "todo": set(ENGINES)}

    def emit(self, eng, fn, reads=(), writes=(), inc=True, dma_sem=None):
        op = Op(eng, fn, reads, writes, inc, dma_sem)
        op.dsnap = dict(self.dma_cnt)
        fence = getattr(self, "_fence", None)
        if fence and eng in fence["todo"]:
            fence["todo"].discard(eng)
            for e2, d in fence["snap"].items():
                if e2 != eng:
                    op.deps.append(d)
        for r in op.reads:
            w = self.last_w.get(r)
            if w is not None:
                op.deps.append(w)
            if isinstance(r, tuple) and len(r) == 2 and r[0] == "PS":
                for rd in self.readers.get(r, ()):
                    if rd is not op and rd.eng != eng and rd.dma_sem is None                             and op.dma_sem is None:
                        op.deps.append(rd)
            self.readers.setdefault(r, []).append(op)
        for wr in op.writes:
            w = self.last_w.get(wr)
            if w is not None and w is not op:
                op.deps.append(w)
            for rd in self.readers.get(wr, ()):
                if rd is not op:
                    op.deps.append(rd)
            self.readers[wr] = []
            self.last_w[wr] = op
        self.ops.append(op)
        return op

    def dma(self, queue, out, in_, stream, reads=(), writes=(), full=False):
        stream = self.pfx + stream
        self.dma_cnt[stream] = self.dma_cnt.get(stream, 0) + 1
        if full:
            self.full_streams.add(stream)
        return self.emit(queue, lambda e, o=out, i=in_: e.dma_start(out=o, in_=i),
                         reads=reads, writes=writes, inc=True, dma_sem=stream)

    def sbuf(self, name, shape, dtype):
        if self.pool is None:
            return self.ctx.enter_context(
                self.nc.sbuf_tensor(self.pfx + name, shape, dtype))
        itm = 4 if dtype == F32 else 2
        need = ceil_div(shape[1] * itm, 4)
        off = self.pool_off
        assert off + need <= self.pool_cols, \
            f"SBUF arena overflow at {self.pfx}{name} (+{need} @ {off})"
        self.pool_off = off + need
        v = self.pool[0:shape[0], off:off + need]
        if dtype == F32:
            return v
        return v.bitcast(dtype)[:, 0:shape[1]]

    def psum(self, name, shape, dtype=F32):
        return self.nc.psum_tensor(name, shape, dtype)

    def finalize(self, same_engine_sync=False):
        self.same_engine_sync = same_engine_sync
        return self._finalize()

    def _finalize(self):
        per_eng = {e: [] for e in ENGINES}
        for op in self.ops:
            op.idx = len(per_eng[op.eng])
            per_eng[op.eng].append(op)
        ses = getattr(self, "same_engine_sync", False)
        if ses is True:
            ses = {"sync", "scalar", "vector", "gpsimd"}
        elif ses is False:
            ses = set()
        self._ses_set = ses
        import os as _oss
        vlo = int(_oss.environ.get("VSES_LO", "0"))
        vhi = int(_oss.environ.get("VSES_HI", "1000000000"))
        self._vses = lambda op: (op.eng != "vector" or vlo <= op.idx < vhi)
        needed = set()
        for op in self.ops:
            for d in op.deps:
                if d.eng != op.eng and d.dma_sem is None:
                    needed.add(d)
                elif (d.eng in ses and d.eng == op.eng and d.dma_sem is None
                      and op.dma_sem is None and d.eng != "tensor"
                      and self._vses(op)):
                    needed.add(d)
                elif d.eng == op.eng and d.dma_sem is None and op.dma_sem is not None:
                    needed.add(d)
        for op in self.ops:
            if op.dma_sem is None:
                op.inc = op in needed
        sems = {}
        for e in ENGINES:
            sems[e] = self.ctx.enter_context(self.nc.semaphore(f"ms_{e}"))
        dsems = {s: self.ctx.enter_context(self.nc.semaphore(s))
                 for s in self.dma_cnt}
        inc_no = {}
        cover = {}
        for e in ENGINES:
            n = 0
            lst = per_eng[e]
            for op in lst:
                if op.dma_sem is not None:
                    continue
                if op.inc:
                    n += 1
                    inc_no[op] = n
            nxt = None
            for op in reversed(lst):
                if op.dma_sem is not None:
                    continue
                if op.inc:
                    nxt = inc_no[op]
                cover[op] = nxt
        dma_no = {s: 0 for s in self.dma_cnt}
        for op in self.ops:
            if op.dma_sem is not None:
                dma_no[op.dma_sem] += 16
                cover[op] = dma_no[op.dma_sem]
        hwm = {e: {} for e in ENGINES}
        plans = {e: [] for e in ENGINES}
        for op in self.ops:
            waits = {}
            for d in op.deps:
                if d.eng == op.eng and (d.dma_sem is None) == (op.dma_sem is None):
                    if d.dma_sem is None and not (d.eng in ses
                                                  and d.eng != "tensor"
                                                  and self._vses(op)):
                        continue
                if d.dma_sem is not None:
                    if (d.dma_sem == op.dma_sem
                            and d.dma_sem in self.full_streams):
                        continue
                    key = ("d", d.dma_sem)
                    val = (16 * op.dsnap.get(d.dma_sem, 0)
                           if d.dma_sem in self.full_streams else cover[d])
                    if val == 0:
                        continue
                else:
                    key = ("e", d.eng)
                    val = cover[d]
                    if val is None:
                        raise AssertionError(
                            "dependency on never-inc'd op: op eng=%s idx=%s "
                            "line=%s <- dep eng=%s idx=%s line=%s needed=%s "
                            "inc=%s" % (op.eng, op.idx,
                                        op.fn.__code__.co_firstlineno,
                                        d.eng, d.idx,
                                        d.fn.__code__.co_firstlineno,
                                        d in needed, d.inc))
                if waits.get(key, 0) < val:
                    waits[key] = val
            wl = []
            for key, val in waits.items():
                if hwm[op.eng].get(key, 0) >= val:
                    continue
                hwm[op.eng][key] = val
                sem = dsems[key[1]] if key[0] == "d" else sems[key[1]]
                wl.append((sem, val))
            if op.dma_sem is not None:
                incs = [(dsems[op.dma_sem], 16)]
            elif op.inc:
                incs = [(sems[op.eng], 1)]
            else:
                incs = []
            plans[op.eng].append((wl, op.fn, incs))
            op._dbg_waits = [(s.name, v) for s, v in wl]
            op._dbg_incs = [(s.name, n) for s, n in incs]
        blk = self.ctx.enter_context(self.nc.Block())
        nops = {e: len(plans[e]) for e in ENGINES}

        import os as _osn
        nop_eng = set(_osn.environ.get(
            "NOPW", "vector,scalar,gpsimd,sync").split(",")) - {""}

        wsep = "WSEP" in _osn.environ

        def section(e, plan, pad=False):
            for wl, fn, incs in plan:
                for sem, val in wl:
                    e.wait_ge(sem, val)
                    if pad and wsep:
                        e.nop()
                if pad and wl and not wsep:
                    e.nop()
                ins = fn(e)
                for sem, n in incs:
                    assert ins is not None, "inc on op returning no instruction"
                    ins = ins.then_inc(sem, n)

        for ename in ENGINES:
            getattr(blk, ename)(lambda e, p=plans[ename],
                                pd=(ename in nop_eng): section(e, p, pd))
        global LAST_STATS, LAST_OPS
        LAST_OPS = self.ops
        LAST_STATS = {e: {"ops": len(plans[e]),
                          "waits": sum(len(wl) for wl, _, _ in plans[e]),
                          "incs": sum(len(ic) for _, _, ic in plans[e])}
                      for e in ENGINES}
        return nops


def load_convert_bf16(k, src, nchunks, cols, dst, dst_col0, scr, scr_cols,
                      conv="vector", stream="dw", gamma_col=None, tag="w",
                      scr_key="scrA"):
    for c in range(nchunks):
        par = c % 2
        sc = scr[:, par * scr_cols: par * scr_cols + cols]
        k.dma("sync", sc, src[c * 128:(c + 1) * 128, :], f"{stream}{par}",
              writes=[(scr_key, par)])
        dv = dst[:, dst_col0 + c * cols: dst_col0 + (c + 1) * cols]
        if gamma_col is None:
            k.emit(conv, lambda e, o=dv, i=sc: e.tensor_copy(out=o, in_=i),
                   reads=[(scr_key, par)], writes=[(tag, "dst")])
        else:
            g = gamma_col[:, c:c + 1]
            k.emit(conv, lambda e, o=dv, i=sc, g=g: e.tensor_scalar(
                out=o, in0=i, scalar1=g, scalar2=None, op0=ALU.mult),
                reads=[(scr_key, par), (tag, "gamma")], writes=[(tag, "dst")])


def psb_f32(k, b0, nb, rows=128):
    return k.PSB[0:rows, b0 * 512:(b0 + nb) * 512]


def psb_bf16(k, b0, nb, rows=128):
    return k.PSB[0:rows, b0 * 512:(b0 + nb) * 512].bitcast(BF16)


def psk(b0, nb):
    return [("PS", i) for i in range(b0, b0 + nb)]


def ln_tokenmajor(k, x_src_fn, ntiles, scr, sums, sumsq, negmean, rstd, sqscr,
                  eps_col, tag, dstream="dx", post_sq_hook=None):
    k.emit("vector", lambda v: v.memset(sumsq[:, 0:ntiles], 0.0),
           writes=[(tag, "sumsq")])
    for t in range(ntiles):
        par = t % 2
        xt = scr[:, par * 1024:(par + 1) * 1024]
        k.dma("sync", xt, x_src_fn(t), f"{dstream}{par}",
              writes=[(tag, "scr", par)])
        k.emit("vector", lambda v, o=sums[:, t:t + 1], i=xt: v.tensor_reduce(
            out=o, in_=i, axis=AX.X, op=ALU.add),
            reads=[(tag, "scr", par)], writes=[(tag, "sums")])
        k.emit("scalar", lambda s, o=sqscr[:], i=xt, a=sumsq[:, t:t + 1]:
               s.activation(out=o, in_=i, func=AF.Square, accum_out=a),
               reads=[(tag, "scr", par)], writes=[(tag, "sumsq"), ("hsq",)])
    if post_sq_hook is not None:
        post_sq_hook()
    k.emit("vector", lambda v: v.tensor_scalar(
        out=negmean[:, 0:ntiles], in0=sums[:, 0:ntiles], scalar1=-1.0 / 1024,
        scalar2=None, op0=ALU.mult),
        reads=[(tag, "sums")], writes=[(tag, "negmean")])
    k.emit("vector", lambda v: v.tensor_scalar(
        out=rstd[:, 0:ntiles], in0=sumsq[:, 0:ntiles], scalar1=1.0 / 1024,
        scalar2=None, op0=ALU.mult),
        reads=[(tag, "sumsq")], writes=[(tag, "rstd")])
    k.emit("vector", lambda v: v.scalar_tensor_tensor(
        out=sums[:, 0:ntiles], in0=negmean[:, 0:ntiles], scalar=1.0,
        in1=negmean[:, 0:ntiles], op0=ALU.mult, op1=ALU.mult),
        reads=[(tag, "negmean")], writes=[(tag, "sums")])
    k.emit("vector", lambda v: v.tensor_tensor(
        out=rstd[:, 0:ntiles], in0=rstd[:, 0:ntiles], in1=sums[:, 0:ntiles],
        op=ALU.subtract), reads=[(tag, "sums"), (tag, "rstd")],
        writes=[(tag, "rstd")])
    k.emit("scalar", lambda s: s.activation(
        out=sumsq[:, 0:ntiles], in_=rstd[:, 0:ntiles], func=AF.Sqrt,
        bias=eps_col[:, 0:1]),
        reads=[(tag, "rstd"), ("epsc",)], writes=[(tag, "sumsq")])
    k.emit("vector", lambda v: v.reciprocal(
        out=rstd[:, 0:ntiles], in_=sumsq[:, 0:ntiles]),
        reads=[(tag, "sumsq")], writes=[(tag, "rstd")])


def build_l0_attn(debug=False, races=True, ses=True, fuse=None):
    if fuse is None:
        nc = bass.Bass("TRN2", target_bir_lowering=False,
                       detect_race_conditions=races)
        ctx = ExitStack()
        k = K(nc, ctx)
        outer = ctx
    else:
        nc, ctx, k = fuse["nc"], fuse["ctx"], fuse["k"]
        outer = nullcontext()

    def din(name, shape, dtype=F32):
        return nc.dram_tensor(k.pfx + name, shape, dtype,
                              kind="ExternalInput").ap()

    x_h = fuse["x"] if fuse else din("x", [4096, 1024])
    Wq_h = din("Wq", [1024, 1024]); bq_h = din("bq", [1, 1024])
    Wl_h = din("Wl", [1024, 256]);  bl_h = din("bl", [1, 256])
    Wk_h = din("Wk", [256, 1024]);  bk_h = din("bk", [1, 1024])
    Wv_h = din("Wv", [256, 1024]);  bv_h = din("bv", [1, 1024])
    Wo_h = din("Wo", [1024, 1024]); bo_h = din("bo", [1, 1024])
    idn_h = din("IDN", [128, 128], BF16)
    y_h = fuse["y"] if fuse else nc.dram_tensor(
        "y", [4096, 1024], F32, kind="ExternalOutput").ap()

    SCALE = 64 ** (-0.5)

    dbg = {}
    def dout(name, shape, dtype=F32):
        dbg[name] = nc.dram_tensor("dbg_" + name, shape, dtype,
                                   kind="ExternalOutput").ap()
        return dbg[name]
    if debug:
        dout("xnfm", [128, 16384], BF16)
        dout("stats", [128, 64], F32)
        dout("sumsq_raw", [128, 16], F32)
        dout("lat", [128, 4096], BF16)
        dout("vsb", [128, 16640], BF16)
        dout("q0", [64, 2048], BF16)
        dout("k0", [64, 2048], BF16)
        dout("nm0", [1, 16], F32)
        dout("negc0", [128, 1], F32)
        dout("pts00", [128, 1024], BF16)
        dout("linv00", [1, 512], F32)
        dout("ao0", [128, 16384], BF16)
    with outer:
        wq = k.sbuf("wq", [128, 8 * 1024], BF16)
        wl = k.sbuf("wl", [128, 8 * 256], BF16)
        wk = k.sbuf("wk", [128, 2 * 1024], BF16)
        wv = k.sbuf("wv", [128, 2 * 1024], BF16)
        wo = k.sbuf("wo", [128, 8 * 1024], BF16)
        bq = k.sbuf("bq_r", [1, 1024], F32); bl = k.sbuf("bl_r", [1, 256], F32)
        bk = k.sbuf("bk_r", [1, 1024], F32); bv = k.sbuf("bv_r", [1, 1024], F32)
        bo = k.sbuf("bo_r", [1, 1024], F32)
        onesf = k.sbuf("onesf", [1, 512], F32)
        ones64 = k.sbuf("ones64", [64, 1], BF16)
        idn = k.sbuf("idn", [128, 128], BF16)
        scrA = k.sbuf("scrA", [128, 2 * 1024], F32)
        scrB = k.sbuf("scrB", [128, 2 * 1024], F32)
        sums = k.sbuf("sums", [128, 16], F32)
        sumsq = k.sbuf("sumsq", [128, 16], F32)
        negmean = k.sbuf("negmean", [128, 16], F32)
        rstd = k.sbuf("rstd", [128, 16], F32)
        sqscr = k.sbuf("sqscr", [128, 1024], F32)
        xnscr = k.sbuf("xnscr", [128, 2 * 1024], BF16)
        xn_fm = k.sbuf("xn_fm", [128, 8 * 2048], BF16)
        lat = k.sbuf("lat", [128, 2 * 2048], BF16)
        v_sb = k.sbuf("v_sb", [128, 16 * 1040], BF16)
        q_sb = k.sbuf("q_sb", [64, 2048], BF16)
        k_sb = k.sbuf("k_sb", [64, 2048], BF16)
        hsq = sqscr[0:64, :].bitcast(BF16)
        pts = k.sbuf("pts", [128, 2 * 1024], BF16)
        attnout = k.sbuf("attnout", [128, 8 * 2048], BF16)
        nm = k.sbuf("nm", [1, 16], F32)
        epsc = k.sbuf("epsc", [128, 1], F32)
        negc = k.sbuf("negc", [128, 1], F32)
        linv = k.sbuf("linv", [1, 512], F32)
        if not hasattr(k, "PSB"):
            k.PSB = ctx.enter_context(
                nc.psum_tensor("PSB", [128, 4096], F32))

        load_convert_bf16(k, Wq_h, 8, 1024, wq, 0, scrA, 1024, tag="wq")
        load_convert_bf16(k, Wl_h, 8, 256, wl, 0, scrA, 1024, tag="wl")
        load_convert_bf16(k, Wk_h, 2, 1024, wk, 0, scrA, 1024, tag="wk")
        load_convert_bf16(k, Wv_h, 2, 1024, wv, 0, scrA, 1024, tag="wv")
        load_convert_bf16(k, Wo_h, 8, 1024, wo, 0, scrA, 1024, tag="wo")
        for dst, src in [(bq, bq_h), (bl, bl_h), (bk, bk_h), (bv, bv_h), (bo, bo_h)]:
            k.dma("sync", dst[:], src, "dwi", writes=[("bias",)], full=True)
        k.dma("sync", idn[:], idn_h, "dwi", writes=[("idn",)], full=True)
        k.emit("vector", lambda v: v.memset(onesf[:], 1.0), writes=[("ones",)])
        k.emit("vector", lambda v: v.memset(epsc[:], 1e-5), writes=[("epsc",)])
        k.emit("vector", lambda v: v.memset(nm[:], 0.0), writes=[("nm",)])
        k.emit("vector", lambda v: v.memset(attnout[:], 0.0), writes=[("attnout",)])
        k.emit("vector", lambda v: v.memset(ones64[:], 1.0), writes=[("ones64",)])

        WK = {"wq": [("wq", "dst")], "wl": [("wl", "dst")], "wk": [("wk", "dst")],
              "wv": [("wv", "dst")], "wo": [("wo", "dst")]}

        for s in range(2):
            tb = 16 * s
            tag = "ln"
            ln_tokenmajor(k, lambda t: x_h[(tb + t) * 128:(tb + t + 1) * 128, :],
                          16, scrB, sums, sumsq, negmean, rstd, sqscr, epsc, tag,
                          post_sq_hook=(lambda: k.dma("gpsimd", dbg["sumsq_raw"],
                                                      sumsq[:], "dbg",
                                                      reads=[(tag, "sumsq")]))
                          if debug and s == 0 else None)
            for t in range(16):
                par = t % 2
                xt = scrB[:, par * 1024:(par + 1) * 1024]
                k.dma("sync", xt, x_h[(tb + t) * 128:(tb + t + 1) * 128, :],
                      f"dx{par}", writes=[(tag, "scr", par)])
                xn = xnscr[:, par * 1024:(par + 1) * 1024]
                k.emit("vector", lambda v, o=xn, i=xt, m=negmean[:, t:t + 1],
                       r=rstd[:, t:t + 1]: v.tensor_scalar(
                           out=o, in0=i, scalar1=m, scalar2=r, op0=ALU.add,
                           op1=ALU.mult),
                       reads=[(tag, "scr", par), (tag, "negmean"), (tag, "rstd")],
                       writes=[("xn", par)])
                pbf = psb_bf16(k, par, 1)
                for c in range(8):
                    k.emit("tensor", lambda te, o=pbf[:, c * 128:(c + 1) * 128],
                           i=xn[:, c * 128:(c + 1) * 128]:
                           te.transpose(out=o, in_=i, identity=idn[:]),
                           reads=[("xn", par), ("idn",)], writes=psk(par, 1),
                           inc=False)
                o_ap = xn_fm[:, :].rearrange("p (c n) -> p c n", c=8)[
                    :, :, t * 128:(t + 1) * 128]
                i_ap = pbf.rearrange("p (c r) -> p c r", c=8)
                k.emit("vector", lambda v, o=o_ap, i=i_ap: v.tensor_copy(out=o, in_=i),
                       reads=psk(par, 1), writes=[("xnfm",)])
            if debug and s == 0:
                k.dma("gpsimd", dbg["xnfm"], xn_fm[:], "dbg", reads=[("xnfm",)])
                k.dma("gpsimd", dbg["stats"][:, 0:16], negmean[:], "dbg",
                      reads=[("ln", "negmean")])
                k.dma("gpsimd", dbg["stats"][:, 16:32], rstd[:], "dbg",
                      reads=[("ln", "rstd")])
                k.dma("gpsimd", dbg["stats"][:, 32:48], sums[:], "dbg",
                      reads=[("ln", "sums")])
                k.dma("gpsimd", dbg["stats"][:, 48:64], sumsq[:], "dbg",
                      reads=[("ln", "sumsq")])
            for c in range(2):
                for qb in range(4):
                    it = c * 4 + qb
                    par = it % 2
                    pl = psb_f32(k, par, 1)
                    for kc in range(8):
                        k.emit("tensor", lambda te, o=pl,
                               l=wl[:, kc * 256 + c * 128: kc * 256 + (c + 1) * 128],
                               r=xn_fm[:, kc * 2048 + qb * 512: kc * 2048 + (qb + 1) * 512],
                               st=(kc == 0):
                               te.matmul(out=o, lhsT=l, rhs=r, start=st,
                                         stop=False),
                               reads=[("wl", "dst"), ("xnfm",)], writes=psk(par, 1),
                               inc=False)
                    k.emit("tensor", lambda te, o=pl, l=bl[0:1, c * 128:(c + 1) * 128],
                           r=onesf[0:1, 0:512]:
                           te.matmul(out=o, lhsT=l, rhs=r, start=False, stop=True),
                           reads=[("bias",), ("ones",)], writes=psk(par, 1), inc=False)
                    k.emit("scalar", lambda sc, o=lat[:, c * 2048 + qb * 512:
                                                      c * 2048 + (qb + 1) * 512],
                           i=pl: sc.copy(out=o, in_=i),
                           reads=psk(par, 1), writes=[("lat",)])
            k.emit("vector", lambda v: v.memset(v_sb[:], 1.0), writes=[("vsb",)])
            for t in range(16):
                par = t % 2
                pv = psb_f32(k, 2 * par, 2)
                for kc in range(2):
                    for nc2 in range(2):
                        k.emit("tensor", lambda te,
                               o=pv[:, nc2 * 512:(nc2 + 1) * 512],
                               l=lat[:, kc * 2048 + t * 128: kc * 2048 + (t + 1) * 128],
                               r=wv[:, kc * 1024 + nc2 * 512: kc * 1024 + (nc2 + 1) * 512],
                               st=(kc == 0):
                               te.matmul(out=o, lhsT=l, rhs=r, start=st,
                                         stop=False),
                               reads=[("lat",), ("wv", "dst")], writes=psk(2 * par, 2),
                               inc=False)
                for nc2 in range(2):
                    k.emit("tensor", lambda te, o=pv[:, nc2 * 512:(nc2 + 1) * 512],
                           l=onesf[0:1, 0:128], r=bv[0:1, nc2 * 512:(nc2 + 1) * 512]:
                           te.matmul(out=o, lhsT=l, rhs=r, start=False, stop=True),
                           reads=[("bias",), ("ones",)], writes=psk(2 * par, 2),
                           inc=False)
                o_ap = v_sb[:, t * 1040:(t + 1) * 1040].rearrange(
                    "p (h c) -> p h c", h=16)[:, :, 0:64]
                i_ap = pv.rearrange("p (h c) -> p h c", h=16)
                k.emit("vector", lambda v, o=o_ap, i=i_ap: v.tensor_copy(out=o, in_=i),
                       reads=psk(2 * par, 2), writes=[("vsb",)])
            if debug and s == 0:
                k.dma("gpsimd", dbg["lat"], lat[:], "dbg", reads=[("lat",)])
                k.dma("gpsimd", dbg["vsb"], v_sb[:], "dbg", reads=[("vsb",)])
            for h in range(16):
                pq = psb_f32(k, 0, 4, rows=64)
                for qb in range(4):
                    pqs = pq[:, qb * 512:(qb + 1) * 512]
                    for kc in range(8):
                        k.emit("tensor", lambda te, o=pqs,
                               l=wq[:, kc * 1024 + h * 64: kc * 1024 + h * 64 + 64],
                               r=xn_fm[:, kc * 2048 + qb * 512: kc * 2048 + (qb + 1) * 512],
                               st=(kc == 0):
                               te.matmul(out=o, lhsT=l, rhs=r, start=st,
                                         stop=False),
                               reads=[("wq", "dst"), ("xnfm",)], writes=psk(0, 4),
                               inc=False)
                    k.emit("tensor", lambda te, o=pqs, l=bq[0:1, h * 64: h * 64 + 64],
                           r=onesf[0:1, 0:512]:
                           te.matmul(out=o, lhsT=l, rhs=r, start=False, stop=True),
                           reads=[("bias",), ("ones",)], writes=psk(0, 4), inc=False)
                k.emit("scalar", lambda sc: sc.copy(out=q_sb[:], in_=pq[:]),
                       reads=psk(0, 4), writes=[("q",)])
                k.emit("scalar", lambda sc: sc.activation(
                    out=hsq, in_=pq[:], func=AF.Square),
                    reads=psk(0, 4), writes=[("hsq",)])
                pn = psb_f32(k, 4, 1, rows=1)
                for qb in range(4):
                    k.emit("tensor", lambda te, r=hsq[:, qb * 512:(qb + 1) * 512]:
                           te.matmul(out=pn, lhsT=ones64[:], rhs=r, start=True,
                                     stop=True),
                           reads=[("ones64",), ("hsq",)], writes=psk(4, 1), inc=False)
                    k.emit("vector", lambda v, o=nm[0:1, qb:qb + 1]:
                           v.reduce_max(out=o, in_=pn, axis=AX.X),
                           reads=psk(4, 1), writes=[("nm",)])
                for qb in range(4):
                    pqs = pq[:, qb * 512:(qb + 1) * 512]
                    for kc in range(2):
                        k.emit("tensor", lambda te, o=pqs,
                               l=wk[:, kc * 1024 + h * 64: kc * 1024 + h * 64 + 64],
                               r=lat[:, kc * 2048 + qb * 512: kc * 2048 + (qb + 1) * 512],
                               st=(kc == 0):
                               te.matmul(out=o, lhsT=l, rhs=r, start=st,
                                         stop=False),
                               reads=[("wk", "dst"), ("lat",)], writes=psk(0, 4),
                               inc=False)
                    k.emit("tensor", lambda te, o=pqs, l=bk[0:1, h * 64: h * 64 + 64],
                           r=onesf[0:1, 0:512]:
                           te.matmul(out=o, lhsT=l, rhs=r, start=False, stop=True),
                           reads=[("bias",), ("ones",)], writes=psk(0, 4), inc=False)
                k.emit("scalar", lambda sc: sc.copy(out=k_sb[:], in_=pq[:]),
                       reads=psk(0, 4), writes=[("k",)])
                k.emit("scalar", lambda sc: sc.activation(
                    out=hsq, in_=pq[:], func=AF.Square),
                    reads=psk(0, 4), writes=[("hsq",)])
                for qb in range(4):
                    k.emit("tensor", lambda te, r=hsq[:, qb * 512:(qb + 1) * 512]:
                           te.matmul(out=pn, lhsT=ones64[:], rhs=r, start=True,
                                     stop=True),
                           reads=[("ones64",), ("hsq",)], writes=psk(4, 1), inc=False)
                    k.emit("vector", lambda v, o=nm[0:1, 4 + qb:5 + qb]:
                           v.reduce_max(out=o, in_=pn, axis=AX.X),
                           reads=psk(4, 1), writes=[("nm",)])
                k.emit("vector", lambda v: v.reduce_max(
                    out=nm[0:1, 8:9], in_=nm[0:1, 0:4], axis=AX.X),
                    reads=[("nm",)], writes=[("nm",)])
                k.emit("vector", lambda v: v.reduce_max(
                    out=nm[0:1, 9:10], in_=nm[0:1, 4:8], axis=AX.X),
                    reads=[("nm",)], writes=[("nm",)])
                k.emit("vector", lambda v: v.tensor_tensor(
                    out=nm[0:1, 10:11], in0=nm[0:1, 8:9], in1=nm[0:1, 9:10],
                    op=ALU.mult), reads=[("nm",)], writes=[("nm",)])
                k.emit("scalar", lambda sc: sc.activation(
                    out=nm[0:1, 11:12], in_=nm[0:1, 10:11], func=AF.Sqrt),
                    reads=[("nm",)], writes=[("nm",)])
                k.emit("vector", lambda v: v.tensor_scalar(
                    out=nm[0:1, 12:13], in0=nm[0:1, 11:12], scalar1=-SCALE,
                    scalar2=None, op0=ALU.mult), reads=[("nm",)], writes=[("nm",)])
                if debug and s == 0 and h == 0:
                    k.dma("gpsimd", dbg["q0"], q_sb[:], "dbg", reads=[("q",)])
                    k.dma("gpsimd", dbg["k0"], k_sb[:], "dbg", reads=[("k",)])
                    k.dma("gpsimd", dbg["nm0"], nm[:], "dbg", reads=[("nm",)])
                pnb = psb_f32(k, 5, 1)[:, 0:1]
                k.emit("tensor", lambda te: te.matmul(
                    out=pnb, lhsT=onesf[0:1, 0:128], rhs=nm[0:1, 12:13],
                    start=True, stop=True),
                    reads=[("nm",), ("ones",)], writes=psk(5, 1), inc=False)
                k.emit("vector", lambda v: v.tensor_copy(out=negc[:], in_=pnb),
                       reads=psk(5, 1), writes=[("negc",)])
                if debug and s == 0 and h == 0:
                    k.dma("gpsimd", dbg["negc0"], negc[:], "dbg", reads=[("negc",)])
                pO = psb_f32(k, 0, 4, rows=65)
                for kc in range(16):
                    for qh in range(2):
                        it = kc * 2 + qh
                        par = it % 2
                        pstv = psb_f32(k, 4 + 2 * par, 2)
                        for j in range(2):
                            qb = 2 * qh + j
                            k.emit("tensor", lambda te,
                                   o=pstv[:, j * 512:(j + 1) * 512],
                                   l=k_sb[0:64, kc * 128:(kc + 1) * 128],
                                   r=q_sb[0:64, qb * 512:(qb + 1) * 512]:
                                   te.matmul(out=o, lhsT=l, rhs=r, start=True,
                                             stop=True),
                                   reads=[("q",), ("k",)], writes=psk(4 + 2 * par, 2),
                                   inc=False)
                        ptv = pts[:, par * 1024:(par + 1) * 1024]
                        k.emit("scalar", lambda sc, o=ptv, i=pstv:
                               sc.activation(out=o, in_=i, func=AF.Exp, scale=SCALE,
                                             bias=negc[:, 0:1]),
                               reads=psk(4 + 2 * par, 2) + [("negc",)],
                               writes=[("pts", par)])
                        if debug and s == 0 and h == 0 and kc == 0 and qh == 0:
                            k.dma("gpsimd", dbg["pts00"], pts[:, 0:1024], "dbg",
                                  reads=[("pts", 0)])
                        for j in range(2):
                            qb = 2 * qh + j
                            k.emit("tensor", lambda te,
                                   o=pO[:, qb * 512:(qb + 1) * 512],
                                   l=v_sb[:, kc * 1040 + h * 65: kc * 1040 + h * 65 + 65],
                                   r=ptv[:, j * 512:(j + 1) * 512],
                                   st=(kc == 0), sp=(kc == 15):
                                   te.matmul(out=o, lhsT=l, rhs=r, start=st,
                                             stop=sp),
                                   reads=[("vsb",), ("pts", par)], writes=psk(0, 4),
                                   inc=False)
                for qb in range(4):
                    par = qb % 2
                    k.emit("vector", lambda v, o=linv[0:1, 0:512],
                           i=pO[64:65, qb * 512:(qb + 1) * 512]:
                           v.reciprocal(out=o, in_=i),
                           reads=[("PS", qb)], writes=[("linv",)])
                    pbv = psb_f32(k, 4 + par, 1, rows=64)
                    k.emit("tensor", lambda te, o=pbv,
                           r=linv[0:1, 0:512]:
                           te.matmul(out=o, lhsT=onesf[0:1, 0:64], rhs=r,
                                     start=True, stop=True),
                           reads=[("linv",), ("ones",)], writes=psk(4 + par, 1),
                           inc=False)
                    ao = attnout[64 * (h % 2):64 * (h % 2) + 64,
                                 (h // 2) * 2048 + qb * 512:
                                 (h // 2) * 2048 + (qb + 1) * 512]
                    k.emit("scalar", lambda sc, o=ao, i=pO[0:64, qb * 512:(qb + 1) * 512]:
                           sc.copy(out=o, in_=i),
                           reads=[("PS", qb)], writes=[("attnout",)])
                    k.emit("vector", lambda v, o=ao, b=pbv: v.tensor_tensor(
                        out=o, in0=o, in1=b, op=ALU.mult),
                        reads=[("attnout",), ("PS", 4 + par)], writes=[("attnout",)])
                    if debug and s == 0 and h == 0 and qb == 0:
                        k.dma("gpsimd", dbg["linv00"], linv[:], "dbg",
                              reads=[("linv",)])
                if debug and s == 0 and h == 0:
                    k.dma("gpsimd", dbg["ao0"], attnout[:], "dbg",
                          reads=[("attnout",)])
            for t in range(16):
                par = t % 2
                xres = scrA[:, par * 1024:(par + 1) * 1024]
                k.dma("gpsimd", xres, x_h[(tb + t) * 128:(tb + t + 1) * 128, :],
                      f"dr{par}", writes=[("scrA", par)])
                pw = psb_f32(k, 2 * par, 2)
                for nc2 in range(2):
                    for kc in range(8):
                        k.emit("tensor", lambda te, o=pw[:, nc2 * 512:(nc2 + 1) * 512],
                               l=attnout[:, kc * 2048 + t * 128: kc * 2048 + (t + 1) * 128],
                               r=wo[:, kc * 1024 + nc2 * 512: kc * 1024 + (nc2 + 1) * 512],
                               st=(kc == 0):
                               te.matmul(out=o, lhsT=l, rhs=r, start=st,
                                         stop=False),
                               reads=[("attnout",), ("wo", "dst")],
                               writes=psk(2 * par, 2), inc=False)
                    k.emit("tensor", lambda te, o=pw[:, nc2 * 512:(nc2 + 1) * 512],
                           l=onesf[0:1, 0:128], r=bo[0:1, nc2 * 512:(nc2 + 1) * 512]:
                           te.matmul(out=o, lhsT=l, rhs=r, start=False, stop=True),
                           reads=[("bias",), ("ones",)], writes=psk(2 * par, 2),
                           inc=False)
                yt = scrB[:, par * 1024:(par + 1) * 1024]
                k.emit("vector", lambda v, o=yt, i0=xres, i1=pw: v.tensor_tensor(
                    out=o, in0=i0, in1=i1, op=ALU.add),
                    reads=[("scrA", par)] + psk(2 * par, 2),
                    writes=[("ln", "scr", par)])
                k.dma("gpsimd", y_h[(tb + t) * 128:(tb + t + 1) * 128, :], yt,
                      f"dy{par}", reads=[("ln", "scr", par)], writes=[("y", par)])
        k.emit("sync", lambda e: e.nop(), reads=[("y", 0)])
        nops = k.finalize(same_engine_sync=ses) if fuse is None else {}
    return nc, nops


CFG_L0FFN = dict(name="l0ffn", depth=6, H=16, W=256, INNER=1024, HID=2730,
                 hgroups=2, tdenom=128)
CFG_L1FFN = dict(name="l1ffn", depth=2, H=8, W=512, INNER=512, HID=2730,
                 hgroups=1, tdenom=256)


def host_consts_inner(W):
    import numpy as np
    t = np.arange(2048, dtype=np.float64)
    di = np.arange(32, dtype=np.float64)
    inv_freq = 1.0 / 10000.0 ** (2 * di / 64)
    fr = t[:, None] * inv_freq[None, :]
    base = (2 * di + 0.4 * 64) / (1.4 * 64)
    power = (t - 1024.0) / (W // 2)
    scale = base[None, :] ** power[:, None]
    cs = np.cos(fr) * scale
    sn = np.sin(fr) * scale
    csk = np.cos(fr) / scale
    snk = np.sin(fr) / scale
    def dup(a):
        return np.concatenate([a, a], axis=1)
    CS = dup(cs); CSK = dup(csk)
    SSM = np.concatenate([-sn, sn], axis=1)
    SSMK = np.concatenate([-snk, snk], axis=1)
    def tm(a):
        return np.ascontiguousarray(
            a.reshape(16, 128, 64).transpose(1, 0, 2).reshape(128, 1024))
    TBL = np.concatenate([tm(CS), tm(SSM), tm(CSK), tm(SSMK)], axis=1)
    r = np.arange(128)[:, None]
    c = np.arange(W + 128)[None, :]
    MASK = np.where((c - r >= 0) & (c - r <= W), 0.0, -1e30)
    return bfc(TBL), bfc(MASK)


def build_inner_txf(cfg, debug=False, races=True, ses=True, depth_run=None,
                    seqs=(0, 1), fuse=None):
    name = cfg["name"]
    depth = cfg["depth"] if depth_run is None else depth_run
    full_depth = cfg["depth"]
    H, W, INNER, HID = cfg["H"], cfg["W"], cfg["INNER"], cfg["HID"]
    HG = cfg["hgroups"]
    HPG = H // HG
    NC3 = 3 * INNER
    NCH = INNER // 128
    LOOK = W // 128
    NHC = (HID + 127) // 128
    STB = LOOK + 2
    RK = 4 + LOOK

    if fuse is None:
        nc = bass.Bass("TRN2", target_bir_lowering=False,
                       detect_race_conditions=races)
        ctx = ExitStack()
        k = K(nc, ctx)
        outer = ctx
    else:
        nc, ctx, k = fuse["nc"], fuse["ctx"], fuse["k"]
        outer = nullcontext()

    def din(nm2, shape, dtype=F32):
        return nc.dram_tensor(k.pfx + nm2, shape, dtype,
                              kind="ExternalInput").ap()

    x_h = fuse["x"] if fuse else din("x", [4096, 1024])
    wqkv_h = din("wqkv", [full_depth, 1024, NC3])
    qs_h = din("qs", [full_depth, 64]); ks_h = din("ks", [full_depth, 64])
    wo_h = din("wo", [full_depth, INNER, 1024])
    lng_h = din("lng", [full_depth, 1024]); lnb_h = din("lnb", [full_depth, 1024])
    w1_h = din("w1", [full_depth, 1024, 5460]); b1_h = din("b1", [full_depth, 5460])
    w2_h = din("w2", [full_depth, 2730, 1024]); b2_h = din("b2", [full_depth, 1024])
    idn_h = din("IDN", [128, 128], BF16)
    tbl_h = din("TBL", [128, 4096], BF16)
    mask_h = din("MASK", [128, W + 128], BF16)
    y_h = fuse["y"] if fuse else nc.dram_tensor(
        "y", [4096, 1024], F32, kind="ExternalOutput").ap()
    cwqkv = nc.dram_tensor(k.pfx + "cwqkv", [full_depth, 1024, NC3], BF16,
                           kind="Internal").ap()
    cwo = nc.dram_tensor(k.pfx + "cwo", [full_depth, INNER, 1024], BF16,
                         kind="Internal").ap()
    cw1 = nc.dram_tensor(k.pfx + "cw1", [full_depth, 1024, 5460], BF16,
                         kind="Internal").ap()
    cw2 = nc.dram_tensor(k.pfx + "cw2", [full_depth, 2730, 1024], BF16,
                         kind="Internal").ap()

    dbg = {}
    def dout(nm2, shape, dtype=F32):
        dbg[nm2] = nc.dram_tensor("dbg_" + nm2, shape, dtype,
                                  kind="ExternalOutput").ap()
        return dbg[nm2]
    if debug:
        dout("xin", [128, 16 * 1024])
        dout("qkvt", [128, 1536], BF16)
        dout("qtm", [128, 512], BF16)
        dout("ktm", [128, 512], BF16)
        dout("ao", [128, NCH * 2048], BF16)
        dout("negc", [128, 1])
        dout("xn1", [128, 8192], BF16)
        dout("hh", [128, NHC * 1024], BF16)

    with outer:
        xin = k.sbuf("xin", [128, 16 * 1024], F32)
        wbuf = k.sbuf("wbuf", [128, 12288], BF16)
        bigbuf = k.sbuf("bigbuf", [128, 22528], BF16)
        kfm = k.sbuf("kfm", [128, RK * 512], BF16)
        vau = k.sbuf("vau", [128, RK * (HPG * 65)], BF16)
        qfm = k.sbuf("qfm", [128, 4 * 512], BF16)
        qktm = k.sbuf("qktm", [128, 1024], BF16)
        dtab = k.sbuf("dtab", [128, 4096], BF16)
        maskb = k.sbuf("maskb", [128, W + 128], BF16)
        pts = k.sbuf("pts", [128, max((LOOK + 1) * 512, 2048)], BF16)
        rtmp = pts[:, 0:1024]
        gelsb = pts[:, 1024:2048]
        pan = k.sbuf("pan", [128, 2 * 8 * 129], BF16)
        xs2 = k.sbuf("xs2", [128, 1024], F32)
        xnscr = k.sbuf("xnscr", [128, 1024], BF16)
        sqscr = k.sbuf("sqscr", [128, 1024], F32)
        scr = k.sbuf("scr", [128, 2 * 1024], F32)
        sums = k.sbuf("sums", [128, 16], F32)
        sumsq = k.sbuf("sumsq", [128, 16], F32)
        negmean = k.sbuf("negmean", [128, 16], F32)
        rstd = k.sbuf("rstd", [128, 16], F32)
        nrm = k.sbuf("nrm", [128, 32], F32)
        epsc = k.sbuf("epsc", [128, 1], F32)
        nm = k.sbuf("nm", [1, 32], F32)
        negc = k.sbuf("negc", [128, 1], F32)
        linv = k.sbuf("linv", [1, 512], F32)
        onesf = k.sbuf("onesf", [1, 512], F32)
        onesb = k.sbuf("onesb", [1, 512], BF16)
        zrowb = k.sbuf("zrowb", [1, 128], BF16)
        idn = k.sbuf("idn", [128, 128], BF16)
        qsr = k.sbuf("qsr", [1, 64], F32); ksr = k.sbuf("ksr", [1, 64], F32)
        qsb = k.sbuf("qsb", [128, 128], BF16)
        gbc = k.sbuf("gbc", [128, 1024], BF16)
        bbc = k.sbuf("bbc", [128, 1024], BF16)
        b2r = k.sbuf("b2r", [1, 1024], F32)
        b1sl = k.sbuf("b1sl", [1, 4 * 128], F32)
        if not hasattr(k, "PSB"):
            k.PSB = ctx.enter_context(
                nc.psum_tensor("PSB", [128, 4096], F32))

        def wpage(c0, c1, buf="wb"):
            return [(buf, p) for p in range(c0 // 1024, (c1 + 1023) // 1024)]

        k.dma("sync", maskb[:], mask_h, "dc", writes=[("mask",)], full=True)
        k.dma("sync", idn[:], idn_h, "dc", writes=[("idn",)], full=True)
        k.emit("vector", lambda v: v.memset(onesf[:], 1.0), writes=[("ones",)])
        k.emit("vector", lambda v: v.memset(onesb[:], 1.0), writes=[("onesb",)])
        k.emit("vector", lambda v: v.memset(zrowb[:], 0.0), writes=[("zrow",)])
        k.emit("vector", lambda v: v.memset(epsc[:], 1e-5), writes=[("epsc",)])
        k.emit("vector", lambda v: v.memset(nm[:], 0.0), writes=[("nm",)])
        k.emit("vector", lambda v: v.memset(vau[:], 1.0),
               writes=[("vau", sl) for sl in range(RK)])
        k.emit("vector", lambda v: v.memset(bigbuf[:], 0.0),
               writes=[("bb", p) for p in range(22)])

        def loadw(src_f32, cache_bf, dst, dcol, rows, cols, first, buf="wb",
                  conv="gpsimd"):
            nch2 = ceil_div(rows, 128)
            assert cols <= 1024
            for c in range(nch2):
                r0, r1 = c * 128, min((c + 1) * 128, rows)
                par = c % 2
                keys = wpage(dcol + c * cols, dcol + (c + 1) * cols, buf)
                dv = dst[0:r1 - r0, dcol + c * cols: dcol + (c + 1) * cols]
                if first:
                    sc = scr[0:r1 - r0, par * 1024: par * 1024 + cols]
                    k.dma("sync", sc, src_f32[r0:r1, :], f"dw{par}",
                          writes=[("scr", par)])
                    k.emit(conv, lambda e, o=dv, i=sc: e.tensor_copy(out=o, in_=i),
                           reads=[("scr", par)], writes=keys)
                    k.dma("gpsimd", cache_bf[r0:r1, :], dv, f"cw{par}", reads=keys)
                else:
                    k.dma("sync", dv, cache_bf[r0:r1, :], f"dw{par}", writes=keys)

        def qkv_tile(s, t, hg):
            par = t % 2
            xbf = xnscr
            k.emit("vector", lambda v, o=xbf[:], i=xin[:, t * 1024:(t + 1) * 1024]:
                   v.tensor_copy(out=o, in_=i),
                   reads=[("xin", t)], writes=[("xn",)])
            ptr = psb_bf16(k, 6 + par, 1)
            for c in range(8):
                k.emit("tensor", lambda te, o=ptr[:, c * 128:(c + 1) * 128],
                       i=xbf[:, c * 128:(c + 1) * 128]:
                       te.transpose(out=o, in_=i, identity=idn[:]),
                       reads=[("xn",), ("idn",)], writes=psk(6 + par, 1), inc=False)
            pview = pan[:, par * 1032:(par + 1) * 1032].rearrange(
                "p (c n) -> p c n", c=8)
            if t % 8 == 0 and hg is not None:
                pass
            if t == 0:
                k.emit("vector", lambda v, o=pview[:, :, 0:1]: v.memset(o, 0.0),
                       writes=[("pan", par)])
            k.emit("vector", lambda v, o=pview[:, :, 1:129],
                   i=ptr.rearrange("p (c n) -> p c n", c=8):
                   v.tensor_copy(out=o, in_=i),
                   reads=psk(6 + par, 1), writes=[("pan", par)])
            if t < 15:
                oview = pan[:, (1 - par) * 1032:(2 - par) * 1032].rearrange(
                    "p (c n) -> p c n", c=8)
                k.emit("vector", lambda v, o=oview[:, :, 0:1],
                       i=ptr.rearrange("p (c n) -> p c n", c=8)[:, :, 127:128]:
                       v.tensor_copy(out=o, in_=i),
                       reads=psk(6 + par, 1), writes=[("pan", 1 - par)])
            pp = psb_f32(k, 3 * par, 3)
            for kc in range(8):
                coff = kc * 129 + (1 if kc < 4 else 0)
                lv = pan[:, par * 1032 + coff: par * 1032 + coff + 128]
                for rj in range(3):
                    k.emit("tensor", lambda te, o=pp[:, rj * 512:(rj + 1) * 512],
                           l=lv, r=wbuf[:, rj * 4096 + kc * 512:
                                        rj * 4096 + (kc + 1) * 512],
                           st=(kc == 0), sp=(kc == 7):
                           te.matmul(out=o, lhsT=l, rhs=r, start=st, stop=sp),
                           reads=[("pan", par)] + wpage(rj * 4096, (rj + 1) * 4096),
                           writes=psk(3 * par, 3), inc=False)
            k.emit("scalar", lambda sc2: sc2.copy(out=qktm[:], in_=pp[:, 0:1024]),
                   reads=psk(3 * par, 3), writes=[("qktm",)])
            slot = t % RK
            vdst = vau[:, slot * (HPG * 65):(slot + 1) * (HPG * 65)].rearrange(
                "p (h c) -> p h c", h=HPG)[:, :, 0:64]
            k.emit("scalar", lambda sc2, o=vdst,
                   i=pp[:, 1024:1536].rearrange("p (h c) -> p h c", h=HPG):
                   sc2.copy(out=o, in_=i),
                   reads=psk(3 * par, 3), writes=[("vau", slot)])
            k.emit("scalar", lambda sc2: sc2.activation(
                out=sqscr[:], in_=qktm[:], func=AF.Square),
                reads=[("qktm",)], writes=[("sq",)])
            k.emit("vector", lambda v: v.tensor_reduce(
                out=nrm[:, 0:16], in_=sqscr.rearrange("p (s d) -> p s d", d=64),
                axis=AX.X, op=ALU.add),
                reads=[("sq",)], writes=[("nrm",)])
            k.emit("scalar", lambda sc2: sc2.activation(
                out=nrm[:, 16:32], in_=nrm[:, 0:16], func=AF.Sqrt),
                reads=[("nrm",)], writes=[("nrm",)])
            k.emit("vector", lambda v: v.tensor_scalar_max(
                out=nrm[:, 16:32], in0=nrm[:, 16:32], scalar1=1e-12),
                reads=[("nrm",)], writes=[("nrm",)])
            k.emit("vector", lambda v: v.reciprocal(
                out=nrm[:, 0:16], in_=nrm[:, 16:32]),
                reads=[("nrm",)], writes=[("nrm",)])
            for which, c0, tb0, rinv0 in [("q", 0, 0, 0), ("k", 512, 2048, 8)]:
                src = qktm[:, c0:c0 + 512]
                sview = src.rearrange("p (h d) -> p h d", d=64)
                rb = nrm[:, rinv0:rinv0 + 8].unsqueeze(2).broadcast_to([128, 8, 64])
                k.emit("vector", lambda v, o=sview, i=sview, f=rb:
                       v.tensor_tensor(out=o, in0=i, in1=f, op=ALU.mult),
                       reads=[("qktm",), ("nrm",)], writes=[("qktm",)])
                cs = dtab[:, tb0 + t % 16 * 64: tb0 + (t % 16) * 64 + 64]
                csb = cs.unsqueeze(1).broadcast_to([128, 8, 64])
                t1 = rtmp[:, 0:512].rearrange("p (h d) -> p h d", d=64)
                k.emit("vector", lambda v, o=t1, i=sview, f=csb:
                       v.tensor_tensor(out=o, in0=i, in1=f, op=ALU.mult),
                       reads=[("qktm",), ("dtab",)], writes=[("rt", 0), ("pts",)])
                ss = dtab[:, tb0 + 1024 + (t % 16) * 64: tb0 + 1024 + (t % 16) * 64 + 64]
                t2 = rtmp[:, 512:1024].rearrange("p (h d) -> p h d", d=64)
                ssl = ss[:, 0:32].unsqueeze(1).broadcast_to([128, 8, 32])
                ssh = ss[:, 32:64].unsqueeze(1).broadcast_to([128, 8, 32])
                k.emit("vector", lambda v, o=t2[:, :, 0:32], i=sview[:, :, 32:64],
                       f=ssl: v.tensor_tensor(out=o, in0=i, in1=f, op=ALU.mult),
                       reads=[("qktm",), ("dtab",)], writes=[("rt", 1), ("pts",)])
                k.emit("vector", lambda v, o=t2[:, :, 32:64], i=sview[:, :, 0:32],
                       f=ssh: v.tensor_tensor(out=o, in0=i, in1=f, op=ALU.mult),
                       reads=[("qktm",), ("dtab",)], writes=[("rt", 1), ("pts",)])
                k.emit("vector", lambda v, o=src, i0=rtmp[:, 0:512],
                       i1=rtmp[:, 512:1024]:
                       v.tensor_tensor(out=o, in0=i0, in1=i1, op=ALU.add),
                       reads=[("rt", 0), ("rt", 1)], writes=[("qktm",)])
            if debug and s == 0 and t == 0 and hg == 0:
                k.dma("gpsimd", dbg["qtm"], qktm[:, 0:512], "dbg", reads=[("qktm",)])
                k.dma("gpsimd", dbg["ktm"], qktm[:, 512:1024], "dbg",
                      reads=[("qktm",)])
            ptr2 = psb_bf16(k, 6 + par, 1)
            for c in range(8):
                k.emit("tensor", lambda te, o=ptr2[:, c * 128:(c + 1) * 128],
                       i=qktm[:, c * 128:(c + 1) * 128]:
                       te.transpose(out=o, in_=i, identity=idn[:]),
                       reads=[("qktm",), ("idn",)], writes=psk(6 + par, 1),
                       inc=False)
            qslot = t % 4
            k.emit("vector", lambda v, o=qfm[:, qslot * 512:(qslot + 1) * 512],
                   i=ptr2[:, 0:512]: v.tensor_copy(out=o, in_=i),
                   reads=psk(6 + par, 1), writes=[("qfm", qslot)])
            kslot = t % RK
            k.emit("vector", lambda v, o=kfm[:, kslot * 512:(kslot + 1) * 512],
                   i=ptr2[:, 512:1024]: v.tensor_copy(out=o, in_=i),
                   reads=psk(6 + par, 1), writes=[("kfm", kslot)])

        def attn_qbh(s, li, hg, qb, h):
            segs = []
            ptcol = 0
            for kc in range(max(0, 4 * qb - LOOK), min(16, 4 * qb + 4)):
                a = max(512 * qb, 128 * kc)
                b = min(512 * qb + 512, 128 * kc + W + 128, 2048)
                if b <= a:
                    continue
                segs.append((kc, a, b, ptcol))
                ptcol += b - a
            pst = psb_f32(k, 0, STB)
            stk = psk(0, STB)
            first = True
            for kc, a, b, pc in segs:
                kslot = kc % RK
                lts = kfm[64 * (h % 2):64 * (h % 2) + 64,
                          kslot * 512 + (h // 2) * 128: kslot * 512 + (h // 2) * 128 + 128]
                t0 = a // 128
                qv = qfm[:, :].rearrange("p (u n) -> p u n", n=512)
                off = 0
                while off < b - a:
                    ln2 = min(b - a - off, 512 - (pc + off) % 512)
                    us = (t0 + off // 128) % 4
                    nslots = ln2 // 128
                    us128 = 128 // 128
                    q3p = qv[64 * (h % 2):64 * (h % 2) + 64,
                             us:us + nslots,
                             (h // 2) * 128:(h // 2) * 128 + 128]
                    k.emit("tensor", lambda te, o=pst[:, pc + off: pc + off + ln2],
                           l=lts, r=q3p:
                           te.matmul(out=o, lhsT=l, rhs=r, start=True, stop=False),
                           reads=[("kfm", kslot)] +
                                 [("qfm", us + j3) for j3 in range(nslots)],
                           writes=stk, inc=False)
                    mo = a - 128 * kc + off
                    k.emit("tensor", lambda te, o=pst[:, pc + off: pc + off + ln2],
                           l=idn[:], r=maskb[:, mo: mo + ln2]:
                           te.matmul(out=o, lhsT=l, rhs=r, start=False, stop=True),
                           reads=[("idn",), ("mask",)], writes=stk, inc=False)
                    off += ln2
            k.emit("scalar", lambda sc2, o=pts[:, 0:ptcol], i=pst[:, 0:ptcol]:
                   sc2.activation(out=o, in_=i, func=AF.Exp, scale=8.0,
                                  bias=negc[:, 0:1]),
                   reads=stk + [("negc",)], writes=[("pts",)])
            pO = psb_f32(k, 6, 1, rows=65)
            k.emit("tensor", lambda te: te.matmul(
                out=pO, lhsT=zrowb[0:1, 0:65], rhs=onesb[0:1, 0:512],
                start=True, stop=False),
                reads=[("zrow",), ("onesb",)], writes=psk(6, 1), inc=False)
            for i2, (kc, a, b, pc) in enumerate(segs):
                kslot = kc % RK
                lv = vau[:, kslot * (HPG * 65) + h * 65: kslot * (HPG * 65) + h * 65 + 65]
                k.emit("tensor", lambda te, o=pO[:, a - 512 * qb: b - 512 * qb],
                       l=lv, r=pts[:, pc: pc + (b - a)],
                       sp=(i2 == len(segs) - 1):
                       te.matmul(out=o, lhsT=l, rhs=r, start=False, stop=sp),
                       reads=[("vau", kslot), ("pts",)], writes=psk(6, 1), inc=False)
            hh2 = hg * HPG + h
            k.emit("vector", lambda v: v.reciprocal(out=linv[:], in_=pO[64:65, :]),
                   reads=psk(6, 1), writes=[("linv",)])
            pB = psb_f32(k, 7, 1, rows=64)
            k.emit("tensor", lambda te: te.matmul(
                out=pB, lhsT=onesf[0:1, 0:64], rhs=linv[:], start=True, stop=True),
                reads=[("linv",), ("ones",)], writes=psk(7, 1), inc=False)
            ao = bigbuf[64 * (hh2 % 2):64 * (hh2 % 2) + 64,
                        (hh2 // 2) * 2048 + 512 * qb:(hh2 // 2) * 2048 + 512 * qb + 512]
            aokeys = wpage((hh2 // 2) * 2048 + 512 * qb,
                           (hh2 // 2) * 2048 + 512 * qb + 512, "bb")
            k.emit("scalar", lambda sc2, o=ao: sc2.copy(out=o, in_=pO[0:64, :]),
                   reads=psk(6, 1), writes=aokeys)
            k.emit("vector", lambda v, o=ao: v.tensor_tensor(
                out=o, in0=o, in1=pB, op=ALU.mult),
                reads=aokeys + psk(7, 1), writes=aokeys)

        def outproj_tile(t):
            par = t % 2
            pw = psb_f32(k, 2 * par, 2)
            for kc in range(NCH):
                for nc2 in range(2):
                    k.emit("tensor", lambda te, o=pw[:, nc2 * 512:(nc2 + 1) * 512],
                           l=bigbuf[:, kc * 2048 + t * 128: kc * 2048 + (t + 1) * 128],
                           r=wbuf[:, kc * 1024 + nc2 * 512: kc * 1024 + (nc2 + 1) * 512],
                           st=(kc == 0), sp=(kc == NCH - 1):
                           te.matmul(out=o, lhsT=l, rhs=r, start=st, stop=sp),
                           reads=wpage(kc * 2048 + t * 128, kc * 2048 + (t + 1) * 128,
                                       "bb") + wpage(kc * 1024, (kc + 1) * 1024),
                           writes=psk(2 * par, 2), inc=False)
            k.emit("vector", lambda v, o=xin[:, t * 1024:(t + 1) * 1024]:
                   v.tensor_tensor(out=o, in0=o, in1=pw, op=ALU.add),
                   reads=psk(2 * par, 2) + [("xin", t)], writes=[("xin", t)])

        def ffn_half(s, li, hs, frst, dump):
            bnd = linv
            if hs == 0:
                k.dma("gpsimd", bnd[0:1, :],
                      xin[127:128, 7 * 1024 + 512: 8 * 1024], "shb",
                      reads=[("xin", 7)], writes=[("linv",)])
            def assemble(tl):
                t = hs * 8 + tl
                par = 0
                xt = xs2[:, 0:1024]
                k.dma("gpsimd", xt[:, 0:512], xin[:, t * 1024: t * 1024 + 512],
                      f"sh{par}", reads=[("xin", t)], writes=[("xs2", par)])
                k.dma("gpsimd", xt[1:128, 512:1024],
                      xin[0:127, t * 1024 + 512: t * 1024 + 1024],
                      f"sh{par}", reads=[("xin", t)], writes=[("xs2", par)])
                if tl > 0:
                    k.dma("gpsimd", xt[0:1, 512:1024],
                          xin[127:128, (t - 1) * 1024 + 512: (t - 1) * 1024 + 1024],
                          f"sh{par}", reads=[("xin", t - 1)], writes=[("xs2", par)])
                elif hs == 1:
                    k.dma("gpsimd", xt[0:1, 512:1024], bnd[0:1, :],
                          f"sh{par}", reads=[("linv",)], writes=[("xs2", par)])
                else:
                    k.emit("vector", lambda v, o=xt[0:1, 512:1024]: v.memset(o, 0.0),
                           writes=[("xs2", par)])
                return xt, par
            k.emit("vector", lambda v: v.memset(sumsq[:, 0:8], 0.0),
                   writes=[("fsumsq",)])
            for tl in range(8):
                xt, par = assemble(tl)
                k.emit("vector", lambda v, o=sums[:, tl:tl + 1], i=xt:
                       v.tensor_reduce(out=o, in_=i, axis=AX.X, op=ALU.add),
                       reads=[("xs2", par)], writes=[("fsums",)])
                k.emit("scalar", lambda sc2, i=xt, a=sumsq[:, tl:tl + 1]:
                       sc2.activation(out=sqscr[:], in_=i, func=AF.Square,
                                      accum_out=a),
                       reads=[("xs2", par)], writes=[("fsumsq",), ("sq",)])
            k.emit("vector", lambda v: v.tensor_scalar(
                out=negmean[:, 0:8], in0=sums[:, 0:8], scalar1=-1.0 / 1024,
                scalar2=None, op0=ALU.mult), reads=[("fsums",)],
                writes=[("fnegmean",)])
            k.emit("vector", lambda v: v.tensor_scalar(
                out=rstd[:, 0:8], in0=sumsq[:, 0:8], scalar1=1.0 / 1024,
                scalar2=None, op0=ALU.mult), reads=[("fsumsq",)], writes=[("frstd",)])
            k.emit("vector", lambda v: v.scalar_tensor_tensor(
                out=sums[:, 0:8], in0=negmean[:, 0:8], scalar=1.0,
                in1=negmean[:, 0:8], op0=ALU.mult, op1=ALU.mult),
                reads=[("fnegmean",)], writes=[("fsums",)])
            k.emit("vector", lambda v: v.tensor_tensor(
                out=rstd[:, 0:8], in0=rstd[:, 0:8], in1=sums[:, 0:8],
                op=ALU.subtract), reads=[("fsums",), ("frstd",)],
                writes=[("frstd",)])
            k.emit("scalar", lambda sc2: sc2.activation(
                out=sumsq[:, 0:8], in_=rstd[:, 0:8], func=AF.Sqrt,
                bias=epsc[:, 0:1]),
                reads=[("frstd",), ("epsc",)], writes=[("fsumsq",)])
            k.emit("vector", lambda v: v.reciprocal(
                out=rstd[:, 0:8], in_=sumsq[:, 0:8]),
                reads=[("fsumsq",)], writes=[("frstd",)])
            for tl in range(8):
                xt, par = assemble(tl)
                k.emit("vector", lambda v, i=xt, m=negmean[:, tl:tl + 1],
                       r=rstd[:, tl:tl + 1]:
                       v.tensor_scalar(out=sqscr[:], in0=i, scalar1=m, scalar2=r,
                                       op0=ALU.add, op1=ALU.mult),
                       reads=[("xs2", par), ("fnegmean",), ("frstd",)],
                       writes=[("sq",)])
                k.emit("vector", lambda v: v.tensor_tensor(
                    out=sqscr[:], in0=sqscr[:], in1=gbc[:], op=ALU.mult),
                    reads=[("sq",), ("gbc",)], writes=[("sq",)])
                k.emit("vector", lambda v: v.tensor_tensor(
                    out=xnscr[:], in0=sqscr[:], in1=bbc[:], op=ALU.add),
                    reads=[("sq",), ("bbc",)], writes=[("xn",)])
                ptr = psb_bf16(k, 6 + par, 1)
                for c in range(8):
                    k.emit("tensor", lambda te, o=ptr[:, c * 128:(c + 1) * 128],
                           i=xnscr[:, c * 128:(c + 1) * 128]:
                           te.transpose(out=o, in_=i, identity=idn[:]),
                           reads=[("xn",), ("idn",)], writes=psk(6 + par, 1),
                           inc=False)
                dst = wbuf[:, 0:8192].rearrange("p (c n) -> p c n", c=8)[
                    :, :, tl * 128:(tl + 1) * 128]
                k.emit("vector", lambda v, o=dst,
                       i=ptr.rearrange("p (c n) -> p c n", c=8):
                       v.tensor_copy(out=o, in_=i),
                       reads=psk(6 + par, 1), writes=wpage(0, 8192))
            if dump:
                k.dma("gpsimd", dbg["xn1"], wbuf[:, 0:8192], "dbg",
                      reads=wpage(0, 8192))
            for hc in range(NHC):
                acols = min(2730, (hc + 1) * 128) - hc * 128
                par = hc % 2
                for gi, base in enumerate([0, 2730]):
                    c0 = base + hc * 128
                    woff = 8192 + (2 * par + gi) * 1024
                    wsl3 = wbuf[:, woff:].rearrange(
                        "p (c n) -> p c n", n=128)[:, 0:8, 0:acols]
                    wk2 = wpage(woff, woff + 1024)
                    if frst:
                        sc3 = scr[:, par * 1024:].rearrange(
                            "p (c n) -> p c n", n=128)[:, 0:8, 0:acols]
                        k.dma("sync", sc3,
                              w1_h[li, :, c0:c0 + acols].rearrange(
                                  "(c p) n -> p c n", p=128),
                              f"dw1{par}", writes=[("scr", par)])
                        k.emit("gpsimd", lambda e, o=wsl3, i=sc3:
                               e.tensor_copy(out=o, in_=i),
                               reads=[("scr", par)], writes=wk2)
                        k.dma("gpsimd", cw1[li, :, c0:c0 + acols].rearrange(
                            "(c p) n -> p c n", p=128), wsl3, f"cw{par}",
                            reads=wk2)
                    else:
                        k.dma("sync", wsl3,
                              cw1[li, :, c0:c0 + acols].rearrange(
                                  "(c p) n -> p c n", p=128),
                              f"dw1{par}", writes=wk2)
                    bsl = b1sl[0:1, (2 * par + gi) * 128:(2 * par + gi) * 128 + acols]
                    k.dma("sync", bsl, b1_h[li:li + 1, c0:c0 + acols],
                          f"db1{par}{gi}", writes=[("b1sl", par, gi)])
                pa = psb_f32(k, 4 * par, 2, rows=acols)
                pg = psb_f32(k, 4 * par + 2, 2, rows=acols)
                for gi, pt2 in [(0, pa), (1, pg)]:
                    woff = 8192 + (2 * par + gi) * 1024
                    wk2 = wpage(woff, woff + 1024)
                    for qb2 in range(2):
                        for kc in range(8):
                            lw = wbuf[:, woff:].rearrange(
                                "p (c n) -> p c n", n=128)[:, kc, 0:acols]
                            k.emit("tensor", lambda te,
                                   o=pt2[:, qb2 * 512:(qb2 + 1) * 512], l=lw,
                                   r=wbuf[:, kc * 1024 + qb2 * 512:
                                          kc * 1024 + (qb2 + 1) * 512],
                                   st=(kc == 0):
                                   te.matmul(out=o, lhsT=l, rhs=r, start=st,
                                             stop=False),
                                   reads=wk2 + wpage(0, 8192),
                                   writes=psk(4 * par + 2 * gi, 2), inc=False)
                        bsl = b1sl[0:1, (2 * par + gi) * 128:
                                   (2 * par + gi) * 128 + acols]
                        k.emit("tensor", lambda te,
                               o=pt2[:, qb2 * 512:(qb2 + 1) * 512], l=bsl:
                               te.matmul(out=o, lhsT=l, rhs=onesf[0:1, 0:512],
                                         start=False, stop=True),
                               reads=[("b1sl", par, gi), ("ones",)],
                               writes=psk(4 * par + 2 * gi, 2), inc=False)
                k.emit("scalar", lambda sc2, o=gelsb[0:acols, 0:1024], i=pg:
                       sc2.activation(out=o, in_=i, func=AF.Gelu),
                       reads=psk(4 * par + 2, 2), writes=[("gel",), ("pts",)])
                hdst = bigbuf[0:acols, hc * 1024:(hc + 1) * 1024]
                k.emit("vector", lambda v, o=hdst, i0=pa, i1=gelsb[0:acols, 0:1024]:
                       v.tensor_tensor(out=o, in0=i0, in1=i1, op=ALU.mult),
                       reads=psk(4 * par, 2) + [("gel",)],
                       writes=wpage(hc * 1024, (hc + 1) * 1024, "bb"))
            if dump:
                k.dma("gpsimd", dbg["hh"], bigbuf[:, 0:NHC * 1024], "dbg",
                      reads=[("bb", p) for p in range(NHC)])
            for nc2 in range(4):
                par = nc2 % 2
                wsl3 = wbuf[:, par * 5632:].rearrange(
                    "p (c n) -> p c n", n=256)[:, 0:NHC, :]
                if frst:
                    cv = w2_h[li, 0:2688, nc2 * 256:(nc2 + 1) * 256].rearrange(
                        "(c p) n -> p c n", p=128)
                    for g4 in range(6):
                        h0, h1 = g4 * 4, min((g4 + 1) * 4, 21)
                        if h1 > h0:
                            sc3 = scr[:, par * 1024:].rearrange(
                                "p (c n) -> p c n", n=256)[:, 0:h1 - h0, :]
                            k.dma("sync", sc3, cv[:, h0:h1, :], f"dw2{par}",
                                  writes=[("scr", par)])
                            k.emit("gpsimd", lambda e, o=wsl3[:, h0:h1, :], i=sc3:
                                   e.tensor_copy(out=o, in_=i),
                                   reads=[("scr", par)], writes=wpage(par * 5632, (par + 1) * 5632))
                    sc2r = scr[0:42, par * 1024: par * 1024 + 256]
                    k.dma("sync", sc2r,
                          w2_h[li, 2688:2730, nc2 * 256:(nc2 + 1) * 256],
                          f"dw2{par}", writes=[("scr", par)])
                    k.emit("gpsimd", lambda e, o=wsl3[0:42, 21, :], i=sc2r:
                           e.tensor_copy(out=o, in_=i),
                           reads=[("scr", par)], writes=wpage(par * 5632, (par + 1) * 5632))
                    k.dma("gpsimd", cw2[li, 0:2688, nc2 * 256:(nc2 + 1) * 256]
                          .rearrange("(c p) n -> p c n", p=128),
                          wsl3[:, 0:21, :],
                          f"cw{par}", reads=wpage(par * 5632, (par + 1) * 5632))
                    k.dma("gpsimd", cw2[li, 2688:2730, nc2 * 256:(nc2 + 1) * 256],
                          wsl3[0:42, 21, :], f"cw{par}", reads=wpage(par * 5632, (par + 1) * 5632))
                else:
                    srcv = cw2[li, 0:2688, nc2 * 256:(nc2 + 1) * 256].rearrange(
                        "(c p) n -> p c n", p=128)
                    k.dma("sync", wsl3[:, 0:21, :], srcv[:, 0:21, :], f"dw2{par}",
                          writes=wpage(par * 5632, (par + 1) * 5632))
                    k.dma("sync", wsl3[0:42, 21, :],
                          cw2[li, 2688:2730, nc2 * 256:(nc2 + 1) * 256],
                          f"dw2{par}", writes=wpage(par * 5632, (par + 1) * 5632))
                for tl in range(8):
                    t = hs * 8 + tl
                    pw = psb_f32(k, tl % 2, 1)[:, 0:256]
                    for hc in range(NHC):
                        acols = min(2730, (hc + 1) * 128) - hc * 128
                        k.emit("tensor", lambda te, o=pw,
                               l=bigbuf[0:acols, hc * 1024 + tl * 128:
                                        hc * 1024 + (tl + 1) * 128],
                               r=wsl3[0:acols, hc, :], st=(hc == 0):
                               te.matmul(out=o, lhsT=l, rhs=r, start=st,
                                         stop=False),
                               reads=wpage(par * 5632, (par + 1) * 5632) + [("bb", hc)],
                               writes=psk(tl % 2, 1), inc=False)
                    k.emit("tensor", lambda te, o=pw,
                           r=b2r[0:1, nc2 * 256:(nc2 + 1) * 256]:
                           te.matmul(out=o, lhsT=onesf[0:1, 0:128], rhs=r,
                                     start=False, stop=True),
                           reads=[("b2r",), ("ones",)], writes=psk(tl % 2, 1),
                           inc=False)
                    k.emit("vector", lambda v, p2=pw,
                           o=xin[:, t * 1024 + nc2 * 256: t * 1024 + (nc2 + 1) * 256]:
                           v.tensor_tensor(out=o, in0=o, in1=p2, op=ALU.add),
                           reads=psk(tl % 2, 1) + [("xin", t)],
                           writes=[("xin", t)])

        seen_layer = set()
        for si, s in enumerate(seqs):
            tb = 16 * s
            tag = "oln"
            ln_tokenmajor(k, lambda t: x_h[(tb + t) * 128:(tb + t + 1) * 128, :],
                          16, scr, sums, sumsq, negmean, rstd, sqscr, epsc, tag,
                          dstream="dx")
            for t in range(16):
                par = t % 2
                xt = scr[:, par * 1024:(par + 1) * 1024]
                k.dma("sync", xt, x_h[(tb + t) * 128:(tb + t + 1) * 128, :],
                      f"dx{par}", writes=[(tag, "scr", par)])
                k.emit("vector", lambda v, o=xin[:, t * 1024:(t + 1) * 1024], i=xt,
                       m=negmean[:, t:t + 1], r=rstd[:, t:t + 1]:
                       v.tensor_scalar(out=o, in0=i, scalar1=m, scalar2=r,
                                       op0=ALU.add, op1=ALU.mult),
                       reads=[(tag, "scr", par), (tag, "negmean"), (tag, "rstd")],
                       writes=[("xin", t)])

            for li in range(depth):
                frst = li not in seen_layer
                seen_layer.add(li)
                k.dma("sync", qsr[:], qs_h[li:li + 1, :], "dli", writes=[("qsr",)],
                      full=True)
                k.dma("sync", ksr[:], ks_h[li:li + 1, :], "dli", writes=[("ksr",)],
                      full=True)
                k.dma("sync", b2r[:], b2_h[li:li + 1, :], "dli", writes=[("b2r",)],
                      full=True)
                k.emit("vector", lambda v: v.reduce_max(
                    out=nm[0:1, 0:1], in_=qsr[:], axis=AX.X,
                    apply_absolute_value=True), reads=[("qsr",)], writes=[("nm",)])
                k.emit("vector", lambda v: v.reduce_max(
                    out=nm[0:1, 1:2], in_=ksr[:], axis=AX.X,
                    apply_absolute_value=True), reads=[("ksr",)], writes=[("nm",)])
                k.emit("vector", lambda v: v.tensor_tensor(
                    out=nm[0:1, 2:3], in0=nm[0:1, 0:1], in1=nm[0:1, 1:2],
                    op=ALU.mult), reads=[("nm",)], writes=[("nm",)])
                k.emit("vector", lambda v: v.tensor_scalar(
                    out=nm[0:1, 3:4], in0=nm[0:1, 2:3], scalar1=-8.0, scalar2=None,
                    op0=ALU.mult), reads=[("nm",)], writes=[("nm",)])
                pnb = psb_f32(k, 7, 1)[:, 0:1]
                k.emit("tensor", lambda te: te.matmul(
                    out=pnb, lhsT=onesf[0:1, 0:128], rhs=nm[0:1, 3:4],
                    start=True, stop=True),
                    reads=[("nm",), ("ones",)], writes=psk(7, 1), inc=False)
                k.emit("vector", lambda v: v.tensor_copy(out=negc[:], in_=pnb),
                       reads=psk(7, 1), writes=[("negc",)])
                pq1 = psb_f32(k, 7, 1)
                k.emit("tensor", lambda te: te.matmul(
                    out=pq1[:, 0:64], lhsT=onesf[0:1, 0:128], rhs=qsr[:],
                    start=True, stop=True), reads=[("qsr",), ("ones",)],
                    writes=psk(7, 1), inc=False)
                k.emit("tensor", lambda te: te.matmul(
                    out=pq1[:, 64:128], lhsT=onesf[0:1, 0:128], rhs=ksr[:],
                    start=True, stop=True), reads=[("ksr",), ("ones",)],
                    writes=psk(7, 1), inc=False)
                k.emit("vector", lambda v: v.tensor_copy(
                    out=qsb[:], in_=pq1[:, 0:128]),
                    reads=psk(7, 1), writes=[("qsb",)])
                k.dma("sync", dtab[:], tbl_h, "dtb", writes=[("dtab",)])
                for base_off, col, rot in [(0, 0, False), (1024, 0, True),
                                           (2048, 64, False), (3072, 64, True)]:
                    dst3 = dtab[:, base_off:base_off + 1024].rearrange(
                        "p (t d) -> p t d", d=64)
                    if not rot:
                        fac = qsb[:, col:col + 64].unsqueeze(1).broadcast_to(
                            [128, 16, 64])
                        k.emit("vector", lambda v, o=dst3, i=dst3, f=fac:
                               v.tensor_tensor(out=o, in0=i, in1=f, op=ALU.mult),
                               reads=[("dtab",), ("qsb",)], writes=[("dtab",)])
                    else:
                        f1 = qsb[:, col + 32:col + 64].unsqueeze(1).broadcast_to(
                            [128, 16, 32])
                        f2 = qsb[:, col:col + 32].unsqueeze(1).broadcast_to(
                            [128, 16, 32])
                        k.emit("vector", lambda v, o=dst3[:, :, 0:32],
                               i=dst3[:, :, 0:32], f=f1:
                               v.tensor_tensor(out=o, in0=i, in1=f, op=ALU.mult),
                               reads=[("dtab",), ("qsb",)], writes=[("dtab",)])
                        k.emit("vector", lambda v, o=dst3[:, :, 32:64],
                               i=dst3[:, :, 32:64], f=f2:
                               v.tensor_tensor(out=o, in0=i, in1=f, op=ALU.mult),
                               reads=[("dtab",), ("qsb",)], writes=[("dtab",)])
                for vech, dstb in [(lng_h, gbc), (lnb_h, bbc)]:
                    k.dma("sync", xs2[0:1, 0:1024], vech[li:li + 1, :], "dli",
                          writes=[("xs2", 0)], full=True)
                    pgb = psb_f32(k, 7, 1)
                    for half in range(2):
                        k.emit("tensor", lambda te, o=pgb,
                               r=xs2[0:1, half * 512:(half + 1) * 512],
                               st=True:
                               te.matmul(out=o, lhsT=onesf[0:1, 0:128], rhs=r,
                                         start=True, stop=True),
                               reads=[("xs2", 0), ("ones",)], writes=psk(7, 1),
                               inc=False)
                        k.emit("vector", lambda v,
                               o=dstb[:, half * 512:(half + 1) * 512], i=pgb:
                               v.tensor_copy(out=o, in_=i),
                               reads=psk(7, 1),
                               writes=[("gbc",) if dstb is gbc else ("bbc",)])
                for hg in range(HG):
                    for rj, roff in enumerate([0, INNER, 2 * INNER]):
                        c0 = roff + hg * HPG * 64
                        loadw(wqkv_h[li, :, c0:c0 + HPG * 64],
                              cwqkv[li, :, c0:c0 + HPG * 64],
                              wbuf, rj * 4096, 1024, HPG * 64, frst and si == 0)
                    for qb in range(4):
                        for tl in range(4):
                            qkv_tile(s, 4 * qb + tl, hg)
                        for h in range(HPG):
                            attn_qbh(s, li, hg, qb, h)
                if debug and si == 0 and li == depth - 1:
                    k.dma("gpsimd", dbg["ao"], bigbuf[:, 0:NCH * 2048], "dbg",
                          reads=[("bb", p) for p in range(ceil_div(NCH * 2048, 1024))])
                    k.dma("gpsimd", dbg["negc"], negc[:], "dbg", reads=[("negc",)])
                loadw(wo_h[li, :, :], cwo[li, :, :], wbuf, 0, INNER, 1024,
                      frst and si == 0)
                for t in range(16):
                    outproj_tile(t)
                import os as _os
                hsd = int(_os.environ.get("HSD", "0"))
                for hs in range(2):
                    ffn_half(s, li, hs, frst and si == 0,
                             debug and si == 0 and li == depth - 1 and hs == hsd)
            for t in range(16):
                par = t % 2
                xr = scr[:, par * 1024:(par + 1) * 1024]
                k.dma("sync", xr, x_h[(tb + t) * 128:(tb + t + 1) * 128, :],
                      f"dx{par}", writes=[("oln", "scr", par)])
                yt = xs2[:, 0:1024]
                k.emit("vector", lambda v, o=yt, i0=xr,
                       i1=xin[:, t * 1024:(t + 1) * 1024]:
                       v.tensor_tensor(out=o, in0=i0, in1=i1, op=ALU.add),
                       reads=[("oln", "scr", par), ("xin", t)],
                       writes=[("xs2", 0)])
                k.dma("gpsimd", y_h[(tb + t) * 128:(tb + t + 1) * 128, :], yt,
                      "dy0", reads=[("xs2", 0)], writes=[("y", 0)])
            if debug and si == 0:
                k.dma("gpsimd", dbg["xin"], xin[:], "dbg",
                      reads=[("xin", t) for t in range(16)])
        k.emit("sync", lambda e: e.nop(), reads=[("y", 0)])
        nops = k.finalize(same_engine_sync=ses) if fuse is None else {}
    return nc, nops, dbg


def host_consts_mamba():
    kk = np.arange(128)[:, None]
    ll = np.arange(128)[None, :]
    UTF = (ll >= kk).astype(np.float32)
    MLF = np.where(ll < kk, -1e30, 0.0).astype(np.float32)
    IDNF = np.eye(128, dtype=np.float32)
    return UTF, MLF, IDNF


def build_l1_attn(debug=False, races=True, ses=True, seqs=(0, 1), nsb=4,
                  fuse=None):
    NH, NC, CH = 32, 17, 128
    if fuse is None:
        nc = bass.Bass("TRN2", target_bir_lowering=False,
                       detect_race_conditions=races)
        ctx = ExitStack()
        k = K(nc, ctx)
        outer = ctx
    else:
        nc, ctx, k = fuse["nc"], fuse["ctx"], fuse["k"]
        outer = nullcontext()

    def din(nm2, shape, dtype=F32):
        return nc.dram_tensor(k.pfx + nm2, shape, dtype,
                              kind="ExternalInput").ap()

    x_h = fuse["x"] if fuse else din("x", [4096, 1024])
    wiT_h = din("wiT", [1024, 4256])
    woT_h = din("woT", [2048, 1024])
    cwv_h = din("cwv", [128, 68])
    cb_h = din("cb", [1, 2176])
    inis_h = din("inis", [64, 2048])
    dtb_h = din("dtb", [1, 32])
    alog_h = din("alog", [1, 32])
    dd_h = din("dd", [1, 32])
    nw_h = din("nw", [1, 2048])
    idn_h = din("IDN", [128, 128], BF16)
    idnf_h = din("IDNF", [128, 128], F32)
    utf_h = din("UTF", [128, 128], F32)
    mlf_h = din("MLF", [128, 128], F32)
    y_h = fuse["y"] if fuse else nc.dram_tensor(
        "y", [4096, 1024], F32, kind="ExternalOutput").ap()
    cwi = nc.dram_tensor(k.pfx + "cwi", [1024, 4256], BF16,
                         kind="Internal").ap()
    cwo2 = nc.dram_tensor(k.pfx + "cwo2", [2048, 1024], BF16,
                          kind="Internal").ap()

    dbg = {}
    def dout(nm2, shape, dtype=F32):
        dbg[nm2] = nc.dram_tensor("dbg_" + nm2, shape, dtype,
                                  kind="ExternalOutput").ap()
    if debug:
        dout("xconv", [128, 17 * 516], BF16)
        dout("dt", [128, 4 * 32])
        dout("eacs", [128, 4 * 32])
        dout("hstate", [64, 2048])
        dout("ytm", [128, 4 * 2048], BF16)
        dout("lmat0", [128, 128], BF16)
        dout("yfm", [128, 16 * 512], BF16)

    with outer:
        ufm = k.sbuf("ufm", [128, 8 * 512], BF16)
        wsl = k.sbuf("wsl", [128, 8448], BF16)
        zsb = k.sbuf("zsb", [128, 4 * 2048], BF16)
        xbc = k.sbuf("xbc", [128, 17 * 516], BF16)
        xcv = k.sbuf("xcv", [128, 17 * 516], BF16)
        dcw = k.sbuf("dcw", [128, 68 * 128], BF16)
        idd = k.sbuf("idd", [128, 32 * 128], BF16)
        cwvc = k.sbuf("cwvc", [128, 68], F32)
        cbr = k.sbuf("cbr", [1, 2176], F32)
        dtt = k.sbuf("dtt", [128, 4 * 32], F32)
        adt = k.sbuf("adt", [128, 4 * 32], F32)
        eacs = k.sbuf("eacs", [128, 4 * 32], F32)
        acst = k.sbuf("acst", [128, 4 * 32], F32)
        nacst = k.sbuf("nacst", [128, 4 * 32], F32)
        lam = k.sbuf("lam", [1, 4 * 32], F32)
        lbc = k.sbuf("lbc", [128, 4 * 32], F32)
        nab = k.sbuf("nab", [128, 32], F32)
        dbc = k.sbuf("dbc", [128, 32], F32)
        hsb = k.sbuf("hsb", [64, 2048], F32)
        hbf = k.sbuf("hbf", [64, 2048], BF16)
        lmat = k.sbuf("lmat", [128, 2 * 128], BF16)
        mt = k.sbuf("mt", [128, 2 * 128], BF16)
        btm = k.sbuf("btm", [128, 64], BF16)
        ccp = k.sbuf("ccp", [64, 512], BF16)
        xtm = k.sbuf("xtm", [128, 2 * 64], BF16)
        xsc = k.sbuf("xsc", [128, 2 * 64], BF16)
        xs3 = k.sbuf("xs3", [128, 2 * 64], BF16)
        yrt = k.sbuf("yrt", [128, 2 * 64], BF16)
        ytm = k.sbuf("ytm", [128, 4 * 2048], BF16)
        yg = k.sbuf("yg", [128, 2048], BF16)
        ysq = k.sbuf("ysq", [128, 4], F32)
        yrs = k.sbuf("yrs", [128, 4], F32)
        yfm = k.sbuf("yfm", [128, 16 * 512], BF16)
        ostg = ytm[:, 0:8192].bitcast(F32)
        xres = k.sbuf("xres", [128, 1024], F32)
        rowbuf = k.sbuf("rowbuf", [1, 128], F32)
        deccol = k.sbuf("deccol", [128, 2], F32)
        idn = k.sbuf("idn", [128, 128], BF16)
        idnf = k.sbuf("idnf", [128, 128], F32)
        utf = k.sbuf("utf", [128, 128], F32)
        mlf = k.sbuf("mlf", [128, 128], F32)
        nwc = k.sbuf("nwc", [128, 16], F32)
        rows = k.sbuf("rows", [1, 2048], F32)
        r32 = k.sbuf("r32", [1, 4 * 32], F32)
        onesf = k.sbuf("onesf", [1, 512], F32)
        onescc = k.sbuf("onescc", [128, 1], F32)
        epsc = k.sbuf("epsc", [128, 1], F32)
        scr = k.sbuf("scr", [128, 2 * 1024], F32)
        xs2 = k.sbuf("xs2", [128, 1024], F32)
        sq2 = xs2
        xnscr = k.sbuf("xnscr", [128, 1024], BF16)
        sums = k.sbuf("sums", [128, 4], F32)
        sumsq = k.sbuf("sumsq", [128, 4], F32)
        negmean = k.sbuf("negmean", [128, 4], F32)
        rstd = k.sbuf("rstd", [128, 4], F32)
        if not hasattr(k, "PSB"):
            k.PSB = ctx.enter_context(
                nc.psum_tensor("PSB", [128, 4096], F32))

        for dst, srch, key in [(idn, idn_h, "idn"), (idnf, idnf_h, "idnf"),
                               (utf, utf_h, "utf"), (mlf, mlf_h, "mlf"),
                               (cwvc, cwv_h, "cwvc"), (rows, nw_h, "nwrow"),
                               (cbr, cb_h, "cbr"), (hsb, inis_h, "hsb")]:
            k.dma("sync", dst[:], srch, "dc", writes=[(key,)], full=True)
        k.dma("sync", r32[0:1, 0:32], dtb_h, "dc", writes=[("dtbr",)], full=True)
        k.dma("sync", r32[0:1, 32:64], alog_h, "dc", writes=[("alogr",)], full=True)
        k.dma("sync", r32[0:1, 64:96], dd_h, "dc", writes=[("ddr",)], full=True)
        import os as _osp
        for _pi in range(int(_osp.environ.get("VPAD", "0"))):
            k.emit("vector", lambda v: v.memset(onesf[:], 1.0),
                   writes=[("ones",)])
        k.emit("vector", lambda v: v.memset(onesf[:], 1.0), writes=[("ones",)])
        k.emit("vector", lambda v: v.memset(onescc[:], 1.0), writes=[("onescc",)])
        k.emit("vector", lambda v: v.memset(xcv[:], 0.0), writes=[("xcv",)])
        k.emit("vector", lambda v: v.memset(epsc[:], 1e-5), writes=[("epsc",)])
        k.emit("vector", lambda v: v.tensor_copy(out=hbf[:], in_=hsb[:]),
               reads=[("hsb",)], writes=[("hbf",)])
        k.emit("scalar", lambda s2: s2.activation(
            out=r32[0:1, 96:128], in_=r32[0:1, 32:64], func=AF.Exp),
            reads=[("alogr",)], writes=[("negar",)])
        k.emit("vector", lambda v: v.tensor_scalar(
            out=r32[0:1, 96:128], in0=r32[0:1, 96:128], scalar1=-1.0, scalar2=None,
            op0=ALU.mult), reads=[("negar",)], writes=[("negar",)])
        pb1 = psb_f32(k, 7, 1)
        k.emit("tensor", lambda te: te.matmul(
            out=pb1[:, 0:32], lhsT=onesf[0:1, 0:128], rhs=r32[0:1, 96:128],
            start=True, stop=True), reads=[("negar",), ("ones",)],
            writes=psk(7, 1), inc=False)
        k.emit("tensor", lambda te: te.matmul(
            out=pb1[:, 32:64], lhsT=onesf[0:1, 0:128], rhs=r32[0:1, 64:96],
            start=True, stop=True), reads=[("ddr",), ("ones",)],
            writes=psk(7, 1), inc=False)
        k.emit("vector", lambda v: v.tensor_copy(out=nab[:], in_=pb1[:, 0:32]),
               reads=psk(7, 1), writes=[("nab",)])
        k.emit("vector", lambda v: v.tensor_copy(out=dbc[:], in_=pb1[:, 32:64]),
               reads=psk(7, 1), writes=[("dbc",)])
        for c in range(16):
            pt3 = psb_f32(k, 7, 1)
            k.emit("tensor", lambda te, o=pt3[:, 0:1],
                   i=rows[0:1, c * 128:(c + 1) * 128]:
                   te.transpose(out=o, in_=i, identity=idnf[0:1, 0:1]),
                   reads=[("nwrow",), ("idnf",)], writes=psk(7, 1), inc=False)
            k.emit("vector", lambda v, o=nwc[:, c:c + 1], i=pt3[:, 0:1]:
                   v.tensor_copy(out=o, in_=i), reads=psk(7, 1), writes=[("nwc",)])
        for c in range(NC):
            for tp in range(4):
                j = c * 4 + tp
                wb2 = cwvc[:, j:j + 1].broadcast_to([128, 128])
                k.emit("gpsimd", lambda g, o=dcw[:, j * 128:(j + 1) * 128], i=wb2:
                       g.affine_select(out=o, in_=i, pattern=[[-1, 128]],
                                       compare_op=ALU.is_equal, fill=0.0,
                                       base=0, channel_multiplier=1),
                       reads=[("cwvc",)], writes=[("dcw",)])
        for h in range(NH):
            k.emit("vector", lambda v, o=idd[:, h * 128:(h + 1) * 128],
                   s2=dbc[:, h:h + 1]:
                   v.tensor_scalar(out=o, in0=idn[:], scalar1=s2, scalar2=None,
                                   op0=ALU.mult),
                   reads=[("idn",), ("dbc",)], writes=[("idd",)])

        def ssd_chunk(s, sb, cl):
            c32 = cl * 32
            tok0 = 4 + cl * 128
            pacs = psb_f32(k, 7, 1)[:, 0:32]
            k.emit("tensor", lambda te: te.matmul(
                out=pacs, lhsT=utf[:], rhs=adt[:, c32:c32 + 32],
                start=True, stop=True),
                reads=[("utf",), ("adt",)], writes=psk(7, 1), inc=False)
            k.emit("scalar", lambda s2: s2.activation(
                out=eacs[:, c32:c32 + 32], in_=pacs, func=AF.Exp),
                reads=psk(7, 1), writes=[("eacs",)])
            k.emit("vector", lambda v: v.tensor_copy(
                out=acst[:, c32:c32 + 32], in_=pacs),
                reads=psk(7, 1), writes=[("acst",)])
            k.emit("vector", lambda v: v.tensor_scalar(
                out=nacst[:, c32:c32 + 32], in0=pacs, scalar1=-1.0, scalar2=None,
                op0=ALU.mult), reads=psk(7, 1), writes=[("nacst",)])
            plam = psb_f32(k, 7, 1)[0:1, 64:96]
            k.emit("tensor", lambda te: te.matmul(
                out=plam, lhsT=onescc[:], rhs=adt[:, c32:c32 + 32],
                start=True, stop=True),
                reads=[("onescc",), ("adt",)], writes=psk(7, 1), inc=False)
            k.emit("vector", lambda v: v.tensor_copy(
                out=lam[0:1, c32:c32 + 32], in_=plam),
                reads=psk(7, 1), writes=[("lam",)])
            plb = psb_f32(k, 7, 1)[:, 96:128]
            k.emit("tensor", lambda te: te.matmul(
                out=plb, lhsT=onesf[0:1, 0:128], rhs=lam[0:1, c32:c32 + 32],
                start=True, stop=True),
                reads=[("lam",), ("ones",)], writes=psk(7, 1), inc=False)
            k.emit("scalar", lambda s2: s2.activation(
                out=lbc[:, c32:c32 + 32], in_=plb, func=AF.Exp),
                reads=psk(7, 1), writes=[("lbc",)])
            pS = psb_f32(k, 0, 1)[:, 0:128]
            bfmv = xcv[0:64, 16 * 516 + tok0: 16 * 516 + tok0 + 128]
            if cl == 0:
                k.emit("vector", lambda v: v.tensor_copy(
                    out=ccp[:], in_=xcv[64:128, 16 * 516 + 4: 16 * 516 + 516]),
                    reads=[("xcv",)], writes=[("ccp",)])
            cfmv = ccp[:, cl * 128:(cl + 1) * 128]
            k.emit("tensor", lambda te: te.matmul(
                out=pS, lhsT=bfmv, rhs=cfmv, start=True, stop=True),
                reads=[("ccp",), ("xcv",)], writes=psk(0, 1), inc=False)
            pbt = psb_bf16(k, 7, 1)
            k.emit("tensor", lambda te: te.transpose(
                out=pbt[:, 0:64], in_=bfmv, identity=idn[0:64, 0:64]),
                reads=[("xcv",), ("idn",)], writes=psk(7, 1), inc=False)
            k.emit("vector", lambda v: v.tensor_copy(out=btm[:], in_=pbt[:, 0:64]),
                   reads=psk(7, 1), writes=[("btm",)])
            for h in range(NH):
                par = h % 2
                pr = psb_f32(k, 7, 1)[0:1, 0:128]
                k.emit("tensor", lambda te, o=pr, i=acst[:, c32 + h:c32 + h + 1]:
                       te.transpose(out=o, in_=i, identity=idnf[:]),
                       reads=[("acst",), ("idnf",)], writes=psk(7, 1), inc=False)
                k.emit("vector", lambda v, i=pr: v.tensor_copy(out=rowbuf[:], in_=i),
                       reads=psk(7, 1), writes=[("rowb",)])
                pD = psb_f32(k, 1 + par * 5, 1)[:, 0:128]
                k.emit("tensor", lambda te, o=pD: te.matmul(
                    out=o, lhsT=onesf[0:1, 0:128], rhs=rowbuf[:],
                    start=True, stop=True),
                    reads=[("rowb",), ("ones",)], writes=psk(1 + par * 5, 1),
                    inc=False)
                k.emit("vector", lambda v, o=pD: v.tensor_tensor(
                    out=o, in0=o, in1=mlf[:], op=ALU.add),
                    reads=psk(1 + par * 5, 1) + [("mlf",)],
                    writes=psk(1 + par * 5, 1))
                k.emit("scalar", lambda s2, o=lmat[:, par * 128:(par + 1) * 128],
                       i=pD, b=nacst[:, c32 + h:c32 + h + 1]:
                       s2.activation(out=o, in_=i, func=AF.Exp, bias=b),
                       reads=psk(1 + par * 5, 1) + [("nacst",)],
                       writes=[("lmat", par)])
                if debug and h == 0 and cl == 3 and dbg:
                    k.dma("gpsimd", dbg["lmat0"], lmat[:, par * 128:(par + 1) * 128],
                          "dbg", reads=[("lmat", par)])
                k.emit("vector", lambda v, o=mt[:, par * 128:(par + 1) * 128],
                       i=pS, l2=lmat[:, par * 128:(par + 1) * 128]:
                       v.tensor_tensor(out=o, in0=i, in1=l2, op=ALU.mult),
                       reads=psk(0, 1) + [("lmat", par)], writes=[("mt", par)])
                ch2, ro = h // 2, 64 * (h % 2)
                xfmv = xcv[ro:ro + 64, ch2 * 516 + tok0: ch2 * 516 + tok0 + 128]
                pxt = psb_bf16(k, 7, 1)
                idv = idn[:, 0:64] if ro == 0 else idn[64:128, 64:128]
                idv = idn[0:64, 0:64] if ro == 0 else idn[64:128, 64:128]
                k.emit("tensor", lambda te, o=pxt[:, 0:64], i=xfmv, iv=idv:
                       te.transpose(out=o, in_=i, identity=iv),
                    reads=[("xcv",), ("idn",)], writes=psk(7, 1), inc=False)
                k.emit("vector", lambda v, o=xtm[:, par * 64:(par + 1) * 64],
                       i=pxt[:, 0:64]:
                       v.tensor_copy(out=o, in_=i),
                       reads=psk(7, 1), writes=[("xtm", par)])
                k.emit("vector", lambda v, o=xsc[:, par * 64:(par + 1) * 64],
                       i=xtm[:, par * 64:(par + 1) * 64],
                       s3=dtt[:, c32 + h:c32 + h + 1]:
                       v.tensor_scalar(out=o, in0=i, scalar1=s3, scalar2=None,
                                       op0=ALU.mult),
                       reads=[("xtm", par), ("dtt",)], writes=[("xsc", par)])
                k.emit("vector", lambda v, o=deccol[:, par:par + 1],
                       i=lmat[:, par * 128 + 127: par * 128 + 128]:
                       v.tensor_copy(out=o, in_=i),
                       reads=[("lmat", par)], writes=[("dec", par)])
                k.emit("vector", lambda v, o=xs3[:, par * 64:(par + 1) * 64],
                       i=xsc[:, par * 64:(par + 1) * 64],
                       s3=deccol[:, par:par + 1]:
                       v.tensor_scalar(out=o, in0=i, scalar1=s3, scalar2=None,
                                       op0=ALU.mult),
                       reads=[("xsc", par), ("dec", par)], writes=[("xs3", par)])
                yb = 2 + h // 8
                pY = psb_f32(k, yb, 1)[:, (h % 8) * 64:(h % 8) * 64 + 64]
                k.emit("tensor", lambda te, o=pY,
                       l=mt[:, par * 128:(par + 1) * 128],
                       r=xsc[:, par * 64:(par + 1) * 64]:
                       te.matmul(out=o, lhsT=l, rhs=r, start=True, stop=False),
                       reads=[("mt", par), ("xsc", par)], writes=[("PS", yb, h % 8)],
                       inc=False)
                k.emit("tensor", lambda te, o=pY, l=idd[:, h * 128:(h + 1) * 128],
                       r=xtm[:, par * 64:(par + 1) * 64]:
                       te.matmul(out=o, lhsT=l, rhs=r, start=False, stop=True),
                       reads=[("idd",), ("xtm", par)], writes=[("PS", yb, h % 8)],
                       inc=False)
                pYr = psb_f32(k, 1 + (1 - par) * 5, 1)[:, 0:64]
                k.emit("tensor", lambda te, o=pYr,
                       l=cfmv, r=hbf[:, h * 64:(h + 1) * 64]:
                       te.matmul(out=o, lhsT=l, rhs=r, start=True, stop=True),
                       reads=[("ccp",), ("hbf",)],
                       writes=psk(1 + (1 - par) * 5, 1), inc=False)
                k.emit("vector", lambda v, o=yrt[:, par * 64:(par + 1) * 64],
                       i=pYr, s3=eacs[:, c32 + h:c32 + h + 1]:
                       v.tensor_scalar(out=o, in0=i, scalar1=s3, scalar2=None,
                                       op0=ALU.mult),
                       reads=psk(1 + (1 - par) * 5, 1) + [("eacs",)],
                       writes=[("yrt", par)])
                oy = ytm[:, cl * 2048 + h * 64: cl * 2048 + (h + 1) * 64]
                k.emit("vector", lambda v, o=oy, i0=yrt[:, par * 64:(par + 1) * 64],
                       i1=pY:
                       v.tensor_tensor(out=o, in0=i0, in1=i1, op=ALU.add),
                       reads=[("yrt", par), ("PS", yb, h % 8)], writes=[("ytm",)])
                pL = psb_f32(k, 7, 1)[0:64, 128 + (h % 4) * 64:128 + (h % 4) * 64 + 64]
                k.emit("tensor", lambda te, o=pL, l=btm[:],
                       r=xs3[:, par * 64:(par + 1) * 64]:
                       te.matmul(out=o, lhsT=l, rhs=r, start=True, stop=True),
                       reads=[("btm",), ("xs3", par)], writes=[("PS", 7, "L", h % 4)],
                       inc=False)
                k.emit("vector", lambda v, o=hsb[:, h * 64:(h + 1) * 64],
                       s3=lbc[0:64, c32 + h:c32 + h + 1], i1=pL:
                       v.scalar_tensor_tensor(out=o, in0=o, scalar=s3, in1=i1,
                                              op0=ALU.mult, op1=ALU.add),
                       reads=[("PS", 7, "L", h % 4), ("hsb",), ("lbc",)],
                       writes=[("hsb",)])
                k.emit("vector", lambda v, o=hbf[:, h * 64:(h + 1) * 64],
                       i=hsb[:, h * 64:(h + 1) * 64]:
                       v.tensor_copy(out=o, in_=i),
                       reads=[("hsb",)], writes=[("hbf",)])

        def mamba_tail(s, sb, first_wo, tb):
            for tl in range(4):
                zt = zsb[:, tl * 2048:(tl + 1) * 2048]
                yt = ytm[:, tl * 2048:(tl + 1) * 2048]
                k.emit("scalar", lambda s2, o=yg[:], i=zt:
                       s2.activation(out=o, in_=i, func=AF.Silu),
                       reads=[("zsb",)], writes=[("yg",)])
                k.emit("vector", lambda v, o=yt, i=yt:
                       v.tensor_tensor(out=o, in0=i, in1=yg[:], op=ALU.mult),
                       reads=[("ytm",), ("yg",)], writes=[("ytm",)])
                k.emit("vector", lambda v, i=yt, a=ysq[:, tl:tl + 1]:
                       v.scalar_tensor_tensor(out=yg[:], in0=i, scalar=1.0, in1=i,
                                              op0=ALU.mult, op1=ALU.mult,
                                              accum_out=a),
                       reads=[("ytm",)], writes=[("yg",), ("ysq",)])
            k.emit("vector", lambda v: v.tensor_scalar(
                out=yrs[:], in0=ysq[:], scalar1=1.0 / 2048, scalar2=None,
                op0=ALU.mult), reads=[("ysq",)], writes=[("yrs",)])
            k.emit("scalar", lambda s2: s2.activation(
                out=ysq[:], in_=yrs[:], func=AF.Sqrt, bias=epsc[:, 0:1]),
                reads=[("yrs",), ("epsc",)], writes=[("ysq",)])
            k.emit("vector", lambda v: v.reciprocal(out=yrs[:], in_=ysq[:]),
                   reads=[("ysq",)], writes=[("yrs",)])
            for tl in range(4):
                par = tl % 2
                yt = ytm[:, tl * 2048:(tl + 1) * 2048]
                k.emit("vector", lambda v, i=yt, s3=yrs[:, tl:tl + 1]:
                       v.tensor_scalar(out=yg[:], in0=i, scalar1=s3, scalar2=None,
                                       op0=ALU.mult),
                       reads=[("ytm",), ("yrs",)], writes=[("yg",)])
                ptr = psb_bf16(k, 6 + par, 1)
                for c in range(16):
                    k.emit("tensor", lambda te, o=ptr[:, (c % 8) * 128:(c % 8) * 128 + 128],
                           i=yg[:, c * 128:(c + 1) * 128]:
                           te.transpose(out=o, in_=i, identity=idn[:]),
                           reads=[("yg",), ("idn",)], writes=psk(6 + par, 1),
                           inc=False)
                    k.emit("vector", lambda v,
                           o=yfm[:, c * 512 + tl * 128: c * 512 + (tl + 1) * 128],
                           i=ptr[:, (c % 8) * 128:(c % 8) * 128 + 128],
                           s3=nwc[:, c:c + 1]:
                           v.tensor_scalar(out=o, in0=i, scalar1=s3, scalar2=None,
                                           op0=ALU.mult),
                           reads=psk(6 + par, 1) + [("nwc",)], writes=[("yfm",)])
            for nco in range(4):
                par = nco % 2
                wv3 = wsl[:, par * 4096:].rearrange("p (c n) -> p c n", n=256)[
                    :, 0:16, :]
                if first_wo:
                    sc3 = scr[:, par * 1024:].rearrange("p (c n) -> p c n", n=256)
                    for g4 in range(4):
                        k.dma("sync", sc3[:, 0:4, :],
                              woT_h[g4 * 512:(g4 + 1) * 512,
                                    nco * 256:(nco + 1) * 256].rearrange(
                                  "(c p) n -> p c n", p=128),
                              f"dw{par}", writes=[("scr", par)])
                        k.emit("vector", lambda v,
                               o=wsl[:, par * 4096 + g4 * 1024:
                                     par * 4096 + (g4 + 1) * 1024],
                               i=scr[:, par * 1024: par * 1024 + 1024]:
                               v.tensor_copy(out=o, in_=i),
                               reads=[("scr", par)], writes=[("wsl", par)])
                    k.dma("gpsimd", cwo2[:, nco * 256:(nco + 1) * 256].rearrange(
                        "(c p) n -> p c n", p=128), wv3, f"cw{par}",
                        reads=[("wsl", par)])
                else:
                    k.dma("sync", wv3, cwo2[:, nco * 256:(nco + 1) * 256].rearrange(
                        "(c p) n -> p c n", p=128), f"dw{par}",
                        writes=[("wsl", par)])
                for tl in range(4):
                    po = psb_f32(k, tl % 2, 1)[:, 0:256]
                    for ec in range(16):
                        k.emit("tensor", lambda te, o=po,
                               l=yfm[:, ec * 512 + tl * 128: ec * 512 + (tl + 1) * 128],
                               r=wv3[:, ec, :], st=(ec == 0), sp=(ec == 15):
                               te.matmul(out=o, lhsT=l, rhs=r, start=st, stop=sp),
                               reads=[("yfm",), ("wsl", par)],
                               writes=psk(tl % 2, 1), inc=False)
                    k.emit("vector", lambda v,
                           o=ostg[:, tl * 1024 + nco * 256:
                                  tl * 1024 + (nco + 1) * 256],
                           i=po: v.tensor_copy(out=o, in_=i),
                           reads=psk(tl % 2, 1), writes=[("ytm",)])
            for tl in range(4):
                t = sb * 4 + tl
                k.dma("sync", xres[:], x_h[(tb + t) * 128:(tb + t + 1) * 128, :],
                      "dxr", writes=[("xres",)])
                k.emit("vector", lambda v,
                       o=ostg[:, tl * 1024:(tl + 1) * 1024]:
                       v.tensor_tensor(out=o, in0=o, in1=xres[:], op=ALU.add),
                       reads=[("ytm",), ("xres",)], writes=[("ytm",)])
                k.dma("gpsimd", y_h[(tb + t) * 128:(tb + t + 1) * 128, :],
                      ostg[:, tl * 1024:(tl + 1) * 1024], "dy0",
                      reads=[("ytm",)], writes=[("y", 0)])

        first = [True]
        for si, s in enumerate(seqs):
            tb = 16 * s
            if si > 0:
                k.dma("sync", hsb[:], inis_h, "dc2", writes=[("hsb",)])
                k.emit("vector", lambda v: v.tensor_copy(out=hbf[:], in_=hsb[:]),
                       reads=[("hsb",)], writes=[("hbf",)])
            for sb in range(nsb):
                t0 = sb * 4
                tag = "mln"
                k.emit("vector", lambda v: v.memset(sumsq[:], 0.0),
                       writes=[(tag, "sumsq")])
                for tl in range(4):
                    t = t0 + tl
                    par = tl % 2
                    xt = scr[:, par * 1024:(par + 1) * 1024]
                    k.dma("sync", xt, x_h[(tb + t) * 128:(tb + t + 1) * 128, :],
                          f"dx{par}", writes=[(tag, "scr", par)])
                    k.emit("vector", lambda v, o=sums[:, tl:tl + 1], i=xt:
                           v.tensor_reduce(out=o, in_=i, axis=AX.X, op=ALU.add),
                           reads=[(tag, "scr", par)], writes=[(tag, "sums")])
                    k.emit("scalar", lambda s2, i=xt, a=sumsq[:, tl:tl + 1]:
                           s2.activation(out=sq2[:], in_=i, func=AF.Square,
                                         accum_out=a),
                           reads=[(tag, "scr", par)],
                           writes=[(tag, "sumsq"), ("sq2",)])
                k.emit("vector", lambda v: v.tensor_scalar(
                    out=negmean[:], in0=sums[:], scalar1=-1.0 / 1024, scalar2=None,
                    op0=ALU.mult), reads=[(tag, "sums")], writes=[(tag, "negmean")])
                k.emit("vector", lambda v: v.tensor_scalar(
                    out=rstd[:], in0=sumsq[:], scalar1=1.0 / 1024, scalar2=None,
                    op0=ALU.mult), reads=[(tag, "sumsq")], writes=[(tag, "rstd")])
                k.emit("vector", lambda v: v.scalar_tensor_tensor(
                    out=sums[:], in0=negmean[:], scalar=1.0, in1=negmean[:],
                    op0=ALU.mult, op1=ALU.mult),
                    reads=[(tag, "negmean")], writes=[(tag, "sums")])
                k.emit("vector", lambda v: v.tensor_tensor(
                    out=rstd[:], in0=rstd[:], in1=sums[:], op=ALU.subtract),
                    reads=[(tag, "sums"), (tag, "rstd")], writes=[(tag, "rstd")])
                k.emit("scalar", lambda s2: s2.activation(
                    out=sumsq[:], in_=rstd[:], func=AF.Sqrt, bias=epsc[:, 0:1]),
                    reads=[(tag, "rstd"), ("epsc",)], writes=[(tag, "sumsq")])
                k.emit("vector", lambda v: v.reciprocal(out=rstd[:], in_=sumsq[:]),
                       reads=[(tag, "sumsq")], writes=[(tag, "rstd")])
                for tl in range(4):
                    t = t0 + tl
                    par = tl % 2
                    xt = scr[:, par * 1024:(par + 1) * 1024]
                    k.dma("sync", xt, x_h[(tb + t) * 128:(tb + t + 1) * 128, :],
                          f"dx{par}", writes=[(tag, "scr", par)])
                    k.emit("vector", lambda v, i=xt, m=negmean[:, tl:tl + 1],
                           r=rstd[:, tl:tl + 1]:
                           v.tensor_scalar(out=xnscr[:], in0=i, scalar1=m,
                                           scalar2=r, op0=ALU.add, op1=ALU.mult),
                           reads=[(tag, "scr", par), (tag, "negmean"),
                                  (tag, "rstd")],
                           writes=[("xn",)])
                    ptr = psb_bf16(k, 6 + par, 1)
                    for c in range(8):
                        k.emit("tensor", lambda te, o=ptr[:, c * 128:(c + 1) * 128],
                               i=xnscr[:, c * 128:(c + 1) * 128]:
                               te.transpose(out=o, in_=i, identity=idn[:]),
                               reads=[("xn",), ("idn",)], writes=psk(6 + par, 1),
                               inc=False)
                    dst = ufm[:, :].rearrange("p (c n) -> p c n", c=8)[
                        :, :, tl * 128:(tl + 1) * 128]
                    k.emit("vector", lambda v, o=dst,
                           i=ptr.rearrange("p (c n) -> p c n", c=8):
                           v.tensor_copy(out=o, in_=i),
                           reads=psk(6 + par, 1), writes=[("ufm",)])
                frst = first[0]
                for zc in range(4):
                    par = zc % 2
                    wv = wsl[:, par * 4096: par * 4096 + 4096]
                    if frst:
                        sc = scr[:, par * 1024:(par + 1) * 1024]
                        for cc in range(8):
                            k.dma("sync", sc[:, 0:512],
                                  wiT_h[cc * 128:(cc + 1) * 128,
                                        zc * 512:(zc + 1) * 512],
                                  f"dw{par}", writes=[("scr", par)])
                            k.emit("gpsimd", lambda g,
                                   o=wv[:, cc * 512:(cc + 1) * 512],
                                   i=sc[:, 0:512]: g.tensor_copy(out=o, in_=i),
                                   reads=[("scr", par)], writes=[("wsl", par)])
                            k.dma("gpsimd", cwi[cc * 128:(cc + 1) * 128,
                                                zc * 512:(zc + 1) * 512],
                                  wv[:, cc * 512:(cc + 1) * 512], f"cw{par}",
                                  reads=[("wsl", par)])
                    else:
                        k.dma("sync", wv.rearrange("p (c n) -> p c n", n=512),
                              cwi[:, zc * 512:(zc + 1) * 512].rearrange(
                                  "(c p) n -> p c n", p=128),
                              f"dw{par}", writes=[("wsl", par)])
                    for tl in range(4):
                        pz = psb_f32(k, (tl % 2) * 2, 1)
                        for cc in range(8):
                            k.emit("tensor", lambda te, o=pz,
                                   l=ufm[:, cc * 512 + tl * 128:
                                         cc * 512 + (tl + 1) * 128],
                                   r=wv[:, cc * 512:(cc + 1) * 512], st=(cc == 0),
                                   sp=(cc == 7):
                                   te.matmul(out=o, lhsT=l, rhs=r, start=st,
                                             stop=sp),
                                   reads=[("ufm",), ("wsl", par)],
                                   writes=psk((tl % 2) * 2, 1), inc=False)
                        k.emit("scalar", lambda s2,
                               o=zsb[:, tl * 2048 + zc * 512:
                                     tl * 2048 + (zc + 1) * 512],
                               i=pz: s2.copy(out=o, in_=i),
                               reads=psk((tl % 2) * 2, 1), writes=[("zsb",)])
                for ec in range(NC):
                    par = ec % 2
                    e0 = 2048 + ec * 128
                    wv = wsl[:, par * 4096: par * 4096 + 1024]
                    if frst:
                        sc = scr[:, par * 1024:(par + 1) * 1024]
                        k.dma("sync", sc.rearrange("p (c n) -> p c n", n=128),
                              wiT_h[:, e0:e0 + 128].rearrange(
                                  "(c p) n -> p c n", p=128),
                              f"dw{par}", writes=[("scr", par)])
                        k.emit("gpsimd", lambda g, o=wv, i=sc:
                               g.tensor_copy(out=o, in_=i),
                               reads=[("scr", par)], writes=[("wsl", par)])
                        k.dma("gpsimd", cwi[:, e0:e0 + 128].rearrange(
                            "(c p) n -> p c n", p=128),
                            wv.rearrange("p (c n) -> p c n", n=128), f"cw{par}",
                            reads=[("wsl", par)])
                    else:
                        k.dma("sync", wv.rearrange("p (c n) -> p c n", n=128),
                              cwi[:, e0:e0 + 128].rearrange(
                                  "(c p) n -> p c n", p=128),
                              f"dw{par}", writes=[("wsl", par)])
                    pxb = psb_f32(k, 4 + (ec % 2), 1)
                    for cc in range(8):
                        k.emit("tensor", lambda te, o=pxb,
                               l=wv[:, cc * 128:(cc + 1) * 128],
                               r=ufm[:, cc * 512:(cc + 1) * 512], st=(cc == 0),
                               sp=(cc == 7):
                               te.matmul(out=o, lhsT=l, rhs=r, start=st, stop=sp),
                               reads=[("ufm",), ("wsl", par)],
                               writes=psk(4 + (ec % 2), 1), inc=False)
                    k.emit("scalar", lambda s2,
                           o=xbc[:, ec * 516 + 4: ec * 516 + 516], i=pxb:
                           s2.copy(out=o, in_=i),
                           reads=psk(4 + (ec % 2), 1), writes=[("xbc",)])
                if frst:
                    sc = scr[:, 0:256]
                    k.dma("sync", sc.rearrange("p (c n) -> p c n", n=32),
                          wiT_h[:, 4224:4256].rearrange("(c p) n -> p c n", p=128),
                          "dwd", writes=[("scr", 0)])
                    k.emit("gpsimd", lambda g: g.tensor_copy(
                        out=wsl[:, 8192:8448], in_=scr[:, 0:256]),
                        reads=[("scr", 0)], writes=[("wdt",)])
                    k.dma("gpsimd", cwi[:, 4224:4256].rearrange(
                        "(c p) n -> p c n", p=128),
                        wsl[:, 8192:8448].rearrange("p (c n) -> p c n", n=32),
                        "cwd", reads=[("wdt",)])
                    first[0] = False
                else:
                    k.dma("sync",
                          wsl[:, 8192:8448].rearrange("p (c n) -> p c n", n=32),
                          cwi[:, 4224:4256].rearrange("(c p) n -> p c n", p=128),
                          "dwd", writes=[("wdt",)])
                for tl in range(4):
                    pdt = psb_f32(k, 6, 1)[:, 0:32]
                    for cc in range(8):
                        k.emit("tensor", lambda te, o=pdt,
                               l=ufm[:, cc * 512 + tl * 128:
                                     cc * 512 + (tl + 1) * 128],
                               r=wsl[:, 8192 + cc * 32: 8192 + (cc + 1) * 32],
                               st=(cc == 0):
                               te.matmul(out=o, lhsT=l, rhs=r, start=st,
                                         stop=False),
                               reads=[("ufm",), ("wdt",)], writes=psk(6, 1),
                               inc=False)
                    k.emit("tensor", lambda te, o=pdt: te.matmul(
                        out=o, lhsT=r32[0:1, 0:32], rhs=onesf[0:1, 0:1],
                        start=False, stop=True) if False else te.matmul(
                        out=o, lhsT=onesf[0:1, 0:128], rhs=r32[0:1, 0:32],
                        start=False, stop=True),
                        reads=[("dtbr",), ("ones",)], writes=psk(6, 1), inc=False)
                    k.emit("scalar", lambda s2, i=pdt:
                           s2.activation(out=sq2[:, 0:32], in_=i, func=AF.Abs),
                           reads=psk(6, 1), writes=[("sp1",)])
                    k.emit("scalar", lambda s2: s2.activation(
                        out=sq2[:, 32:64], in_=sq2[:, 0:32], func=AF.Exp,
                        scale=-1.0), reads=[("sp1",)], writes=[("sp1",)])
                    k.emit("scalar", lambda s2: s2.activation(
                        out=sq2[:, 64:96], in_=sq2[:, 32:64], func=AF.Ln,
                        bias=onescc[:, 0:1]),
                        reads=[("sp1",), ("onescc",)], writes=[("sp1",)])
                    k.emit("vector", lambda v, i=pdt: v.tensor_scalar_max(
                        out=sq2[:, 96:128], in0=i, scalar1=0.0),
                        reads=psk(6, 1), writes=[("sp2",)])
                    k.emit("vector", lambda v, o=dtt[:, tl * 32:(tl + 1) * 32]:
                           v.tensor_tensor(out=o, in0=sq2[:, 96:128],
                                           in1=sq2[:, 64:96], op=ALU.add),
                           reads=[("sp1",), ("sp2",)], writes=[("dtt",)])
                k.emit("vector", lambda v: v.tensor_tensor(
                    out=adt[:, :].rearrange("p (t h) -> p t h", h=32),
                    in0=dtt[:, :].rearrange("p (t h) -> p t h", h=32),
                    in1=nab[:, :].unsqueeze(1).broadcast_to([128, 4, 32]),
                    op=ALU.mult), reads=[("dtt",), ("nab",)], writes=[("adt",)])
                if sb == 0:
                    k.emit("vector", lambda v: v.memset(
                        xbc.rearrange("p (c n) -> p c n", n=516)[:, :, 0:4], 0.0),
                        writes=[("xbc",)])
                for ec in range(NC):
                    par = ec % 2
                    pcv = psb_f32(k, 4 + par, 1)
                    for tp in range(4):
                        k.emit("tensor", lambda te, o=pcv,
                               l=dcw[:, (ec * 4 + tp) * 128:
                                     (ec * 4 + tp + 1) * 128],
                               r=xbc[:, ec * 516 + tp + 1: ec * 516 + tp + 513],
                               st=(tp == 0):
                               te.matmul(out=o, lhsT=l, rhs=r, start=st,
                                         stop=False),
                               reads=[("dcw",), ("xbc",)],
                               writes=psk(4 + par, 1), inc=False)
                    k.emit("tensor", lambda te, o=pcv,
                           l=cbr[0:1, ec * 128:(ec + 1) * 128]:
                           te.matmul(out=o, lhsT=l, rhs=onesf[0:1, 0:512],
                                     start=False, stop=True),
                           reads=[("cbr",), ("ones",)], writes=psk(4 + par, 1),
                           inc=False)
                    k.emit("scalar", lambda s2,
                           o=xcv[:, ec * 516 + 4: ec * 516 + 516], i=pcv:
                           s2.activation(out=o, in_=i, func=AF.Silu),
                           reads=psk(4 + par, 1), writes=[("xcv",)])
                k.emit("vector", lambda v: v.tensor_copy(
                    out=xbc.rearrange("p (c n) -> p c n", n=516)[:, :, 0:4],
                    in_=xbc.rearrange("p (c n) -> p c n", n=516)[:, :, 512:516]),
                    reads=[("xbc",)], writes=[("xbc",)])
                for cl in range(4):
                    ssd_chunk(s, sb, cl)
                mamba_tail(s, sb, si == 0 and sb == 0, tb)
                if debug and si == len(seqs) - 1 and sb == nsb - 1:
                    k.dma("gpsimd", dbg["xconv"], xcv[:], "dbg", reads=[("xcv",)])
                    k.dma("gpsimd", dbg["dt"], dtt[:], "dbg", reads=[("dtt",)])
                    k.dma("gpsimd", dbg["eacs"], eacs[:], "dbg", reads=[("eacs",)])
                    k.dma("gpsimd", dbg["hstate"], hsb[:], "dbg", reads=[("hsb",)])
                    k.dma("gpsimd", dbg["ytm"], ytm[:], "dbg", reads=[("ytm",)])
                    k.dma("gpsimd", dbg["yfm"], yfm[:], "dbg", reads=[("yfm",)])
        k.emit("sync", lambda e: e.nop(), reads=[("y", 0)])
        nops = k.finalize(same_engine_sync=ses) if fuse is None else {}
    return nc, nops, dbg


def build_full(races=False, blocks=("a0", "f0", "a1", "f1")):
    nc = bass.Bass("TRN2", target_bir_lowering=False,
                   detect_race_conditions=races)
    ctx = ExitStack()
    k = K(nc, ctx)
    with ctx:
        pool = ctx.enter_context(nc.sbuf_tensor("pool", [128, 53200], F32))
        k.pool = pool
        k.pool_cols = 53200
        k.PSB = ctx.enter_context(nc.psum_tensor("PSB", [128, 4096], F32))
        x_h = nc.dram_tensor("x", [4096, 1024], F32,
                             kind="ExternalInput").ap()
        y_h = nc.dram_tensor("y", [4096, 1024], F32,
                             kind="ExternalOutput").ap()
        hand = {}
        for i, b in enumerate(blocks):
            hand[b] = (x_h if i == 0 else hand[blocks[i - 1]][1],
                       y_h if i == len(blocks) - 1 else
                       nc.dram_tensor(f"h{i}b", [4096, 1024], F32,
                                      kind="Internal").ap())
        import os as _osf
        first = True
        for b in blocks:
            if not first:
                if "NOBAR" not in _osf.environ:
                    k.barrier()
            first = False
            if "NORST" in _osf.environ:
                k.pfx = b + "_"
            else:
                k.arena_reset(b + "_")
            fz = dict(nc=nc, ctx=ctx, k=k, x=hand[b][0], y=hand[b][1])
            if b == "a0":
                build_l0_attn(debug=False, races=races, fuse=fz)
            elif b == "f0":
                build_inner_txf(CFG_L0FFN, debug=False, races=races, fuse=fz)
            elif b == "a1":
                build_l1_attn(debug=False, races=races, fuse=fz)
            elif b == "f1":
                build_inner_txf(CFG_L1FFN, debug=False, races=races, fuse=fz)
            elif b == "cp":
                cbuf = k.sbuf("cbuf", [128, 2 * 1024], F32)
                for t in range(32):
                    par = t % 2
                    sl = cbuf[:, par * 1024:(par + 1) * 1024]
                    k.dma("sync", sl, fz["x"][t * 128:(t + 1) * 128, :],
                          f"ci{par}", writes=[("cp", par)])
                    k.dma("gpsimd", fz["y"][t * 128:(t + 1) * 128, :], sl,
                          f"co{par}", reads=[("cp", par)], writes=[("y", 0)])
                k.emit("sync", lambda e: e.nop(), reads=[("y", 0)])
        nops = k.finalize(same_engine_sync={"vector"})
    return nc, nops



def kernel(**inputs):
    from concourse.bass_utils import run_bass_kernel_spmd
    ins = inputs
    x = np.ascontiguousarray(np.asarray(ins["L0_attn_x"], dtype=np.float32))
    idn_bf = np.eye(128, dtype=np.float32).astype(ml_dtypes.bfloat16)
    UTF, MLF, IDNF = host_consts_mamba()
    TBL0, MASK0 = host_consts_inner(CFG_L0FFN["W"])
    TBL1, MASK1 = host_consts_inner(CFG_L1FFN["W"])
    wi = ins["L1_attn_in_proj_w"]
    wo3 = ins["L1_attn_out_proj_w"]
    cw = ins["L1_attn_conv_w"]
    base = {
        "a0_Wq": ins["L0_attn_Wq"], "a0_bq": ins["L0_attn_bq"].reshape(1, -1),
        "a0_Wl": ins["L0_attn_Wl"], "a0_bl": ins["L0_attn_bl"].reshape(1, -1),
        "a0_Wk": ins["L0_attn_Wk"], "a0_bk": ins["L0_attn_bk"].reshape(1, -1),
        "a0_Wv": ins["L0_attn_Wv"], "a0_bv": ins["L0_attn_bv"].reshape(1, -1),
        "a0_Wo": ins["L0_attn_Wo"], "a0_bo": ins["L0_attn_bo"].reshape(1, -1),
        "a0_IDN": idn_bf,
        "f0_wqkv": ins["L0_ffn_Wqkv"], "f0_qs": ins["L0_ffn_q_scale"],
        "f0_ks": ins["L0_ffn_k_scale"], "f0_wo": ins["L0_ffn_Wo"],
        "f0_lng": ins["L0_ffn_ln_g"], "f0_lnb": ins["L0_ffn_ln_b"],
        "f0_w1": ins["L0_ffn_W1"], "f0_b1": ins["L0_ffn_b1"],
        "f0_w2": ins["L0_ffn_W2"], "f0_b2": ins["L0_ffn_b2"],
        "f0_IDN": idn_bf, "f0_TBL": TBL0, "f0_MASK": MASK0,
        "a1_wiT": np.ascontiguousarray(wi.T),
        "a1_woT": np.ascontiguousarray(wo3.T),
        "a1_cwv": np.ascontiguousarray(
            cw.reshape(17, 128, 4).transpose(1, 0, 2).reshape(128, 68)),
        "a1_cb": ins["L1_attn_conv_b"].reshape(1, -1),
        "a1_inis": np.ascontiguousarray(
            ins["L1_attn_init_states"].transpose(2, 0, 1).reshape(64, 2048)),
        "a1_dtb": ins["L1_attn_dt_bias"].reshape(1, -1),
        "a1_alog": ins["L1_attn_A_log"].reshape(1, -1),
        "a1_dd": ins["L1_attn_D"].reshape(1, -1),
        "a1_nw": ins["L1_attn_norm_w"].reshape(1, -1),
        "a1_IDN": idn_bf, "a1_IDNF": IDNF, "a1_UTF": UTF, "a1_MLF": MLF,
        "f1_wqkv": ins["L1_ffn_wqkv"], "f1_qs": ins["L1_ffn_q_scale"],
        "f1_ks": ins["L1_ffn_k_scale"], "f1_wo": ins["L1_ffn_wo"],
        "f1_lng": ins["L1_ffn_ln_g"], "f1_lnb": ins["L1_ffn_ln_b"],
        "f1_w1": ins["L1_ffn_w1"], "f1_b1": ins["L1_ffn_b1"],
        "f1_w2": ins["L1_ffn_w2"], "f1_b2": ins["L1_ffn_b2"],
        "f1_IDN": idn_bf, "f1_TBL": TBL1, "f1_MASK": MASK1,
    }
    cores = list(range(8))
    nc, _ = build_full()
    maps = [dict(base, x=np.ascontiguousarray(
        x[2 * c:2 * c + 2].reshape(4096, 1024))) for c in cores]
    res = run_bass_kernel_spmd(nc, maps, core_ids=cores)
    ys = [np.asarray(res.results[c]["y"]).reshape(2, 2048, 1024)
          for c in cores]
    return np.stack(ys).reshape(16, 2048, 1024).astype(np.float32)
